# version 06
# baseline (speedup 1.0000x reference)
import numpy as np
import ml_dtypes
from contextlib import ExitStack

import concourse.bass as bass
import concourse.mybir as mybir
from concourse.bass_utils import run_bass_kernel_spmd

F32 = mybir.dt.float32
BF16 = mybir.dt.bfloat16
I32 = mybir.dt.int32
AF = mybir.ActivationFunctionType
ALU = mybir.AluOpType
AX = mybir.AxisListType

H, KVH, D, CHUNK = 32, 8, 128, 128
HID = 4096
L = 4096
ROPE_THETA = 10000.0
RMS_EPS = 1e-5
LN_EPS = 1e-5
NEG = -1e30

GU_CT = 22
NPLANE = 8
DN_CT = 4
HB = 88

BFnp = ml_dtypes.bfloat16



def _rope_tables_fm():
    inv_freq = (1.0 / ROPE_THETA ** (np.arange(0, D, 2, dtype=np.float32) / D)).astype(np.float32)
    ang = np.arange(L, dtype=np.float32)[:, None] * inv_freq[None, :]
    cos = np.concatenate([np.cos(ang), np.cos(ang)], axis=-1)
    sin = np.concatenate([np.sin(ang), np.sin(ang)], axis=-1)
    sinm = sin.copy()
    sinm[:, :64] *= -1.0
    return (np.ascontiguousarray(cos.T).astype(BFnp),
            np.ascontiguousarray(sinm.T).astype(BFnp))


def _wblocks(W):
    FO, K = W.shape
    return np.ascontiguousarray(W.reshape(FO // 128, 128, K // 128, 128).transpose(0, 3, 2, 1)).astype(np.float32)


def _gu_perm_c():
    perm = np.full(GU_CT * 128, -1, np.int64)
    perm[0:1376] = np.arange(0, 1376)
    perm[1408:1408 + 1376] = np.arange(1376, 2752)
    return perm


def _dn_perm_i():
    pos = np.arange(HB * 128)
    ct, sp, c = pos // 1024, (pos // 128) % 8, pos % 128
    corig = 128 * ct + c
    return np.where(corig < 1376, 8 * corig + sp, -1)


def host_prep(inputs, n_main=16, n_prefix=16):
    x = np.asarray(inputs["L0_attn_hidden_states"])
    Wq = np.asarray(inputs["L0_attn_Wq"]); Wk = np.asarray(inputs["L0_attn_Wk"])
    Wv = np.asarray(inputs["L0_attn_Wv"]); Wo = np.asarray(inputs["L0_attn_Wo"])
    Wg = np.asarray(inputs["L0_attn_Wg"]); Win = np.asarray(inputs["L0_attn_Win"])

    sh = {}
    sh["wq_r"] = _wblocks(Wq)
    sh["wk_r"] = _wblocks(Wk)
    sh["wv_r"] = _wblocks(Wv)
    sh["wg_r"] = _wblocks(Wg)
    sh["wo_r"] = _wblocks(Wo.T)
    sh["win_r"] = np.ascontiguousarray(Win.reshape(32, 32, 128).transpose(2, 1, 0)).astype(np.float32)[None]
    sh["dt_bias_col"] = np.asarray(inputs["L0_attn_dt_bias"]).reshape(32, 1).astype(np.float32)
    sh["albias_col"] = np.asarray(inputs["L0_attn_A_log_bias"]).reshape(32, 1).astype(np.float32)
    sh["gnw_col"] = np.asarray(inputs["L0_attn_gnorm_w"]).reshape(128, 1).astype(np.float32)
    sh["c_ident"] = np.eye(128, dtype=np.float32)
    sh["c_ones1"] = np.ones((128, 1), np.float32)
    sh["c_onesr"] = np.ones((1, 128), np.float32)
    ij = np.arange(128)
    sh["c_u"] = (ij[:, None] <= ij[None, :]).astype(np.float32)
    sh["c_mneg"] = np.where(ij[:, None] <= ij[None, :], 0.0, NEG).astype(np.float32)
    sh["c_eps"] = np.full((128, 2), 1.0, np.float32)
    sh["c_eps"][:, 0] = LN_EPS
    rsel = np.zeros((32, 32 * 128), np.float32)
    for k in range(32):
        rsel[k, 128 * k:128 * (k + 1)] = 1.0
    sh["c_rsel"] = rsel

    qgu = np.asarray(inputs["L0_ffn_qweight_gu"])
    sgu = np.asarray(inputs["L0_ffn_scales_gu"], np.float32)
    zgu = np.asarray(inputs["L0_ffn_qzeros_gu"])
    permc = _gu_perm_c()
    valid = permc >= 0
    qguT = np.zeros((GU_CT * 128, HID), np.int32)
    qguT[valid] = np.ascontiguousarray(qgu.T)[permc[valid]]
    sh["qguT"] = qguT.reshape(GU_CT, 128, HID)
    scol = np.zeros((128, GU_CT, NPLANE, 32), np.float32)
    pm = permc.reshape(GU_CT, 128)
    for ct in range(GU_CT):
        v = pm[ct] >= 0
        n = 8 * pm[ct][v][:, None] + np.arange(8)[None, :]
        scol[v, ct, :, :] = sgu.T[n]
    sh["sgu_cols"] = np.ascontiguousarray(scol.reshape(128, GU_CT * NPLANE * 32))
    zguT = np.zeros((GU_CT * 128, 32), np.int32)
    zguT[valid] = np.ascontiguousarray(zgu.T)[permc[valid]]
    sh["zguT"] = zguT.reshape(GU_CT, 128, 32)

    qdn = np.asarray(inputs["L0_ffn_qweight_dn"])
    sdn = np.asarray(inputs["L0_ffn_scales_dn"], np.float32)
    zdn = np.asarray(inputs["L0_ffn_qzeros_dn"])
    permi = _dn_perm_i()
    vi = permi >= 0
    qdnp = np.zeros((HB * 128, 512), np.int32)
    qdnp[vi] = qdn[permi[vi]]
    sh["qdnT"] = np.ascontiguousarray(qdnp.T).reshape(DN_CT, 128, HB * 128)
    sdnp = np.zeros((HB, HID), np.float32)
    sdnp[:86] = sdn
    dcol = np.zeros((128, DN_CT, NPLANE, HB), np.float32)
    for ct in range(DN_CT):
        n = 8 * (128 * ct + np.arange(128))[:, None] + np.arange(8)[None, :]
        dcol[:, ct, :, :] = sdnp.T[n]
    sh["sdn_cols"] = np.ascontiguousarray(dcol.reshape(128, DN_CT * NPLANE * HB))
    zdnp = np.zeros((HB, 512), np.int32)
    zdnp[:86] = zdn
    sh["zdnT"] = np.ascontiguousarray(zdnp.T)

    cos, sinm = _rope_tables_fm()
    TM, TP = n_main * CHUNK, n_prefix * CHUNK
    in_maps = []
    for core in range(8):
        b, hf = core // 2, core % 2
        off = 2048 * hf
        m = dict(sh)
        m["x_main"] = np.ascontiguousarray(x[b, off:off + TM]).astype(np.float32)
        m["ropec_m"] = np.ascontiguousarray(cos[:, off:off + TM])
        m["ropes_m"] = np.ascontiguousarray(sinm[:, off:off + TM])
        if TP > 0:
            poff = off - TP if hf == 1 else 0
            if hf == 1:
                m["x_prefix"] = np.ascontiguousarray(x[b, poff:off]).astype(np.float32)
            else:
                m["x_prefix"] = np.zeros((TP, HID), np.float32)
            m["ropec_p"] = np.ascontiguousarray(cos[:, poff:poff + TP])
            m["ropes_p"] = np.ascontiguousarray(sinm[:, poff:poff + TP])
        in_maps.append(m)
    return in_maps



class P:
    def __init__(self, n_main, n_prefix, debug):
        self.nc = bass.Bass("TRN2", target_bir_lowering=False, debug=False)
        self.ctx = ExitStack()
        self.n_main, self.n_prefix, self.debug = n_main, n_prefix, debug
        self.TM, self.TP = n_main * CHUNK, n_prefix * CHUNK
        self.phases = []
        self._frees = []

    def sb(self, name, shape, dtype):
        cm = self.nc.sbuf_tensor(name, shape, dtype)
        t = cm.__enter__()
        self._frees.append(cm)
        return t, cm

    @staticmethod
    def free(cm):
        cm.__exit__(None, None, None)

    def sem(self, name):
        return self.ctx.enter_context(self.nc.semaphore(name))

    def din(self, name, shape, dtype):
        return self.nc.dram_tensor(name, shape, dtype, kind="ExternalInput").ap()

    def dout(self, name, shape, dtype):
        return self.nc.dram_tensor(name, shape, dtype, kind="ExternalOutput").ap()

    def dtmp(self, name, shape, dtype):
        kind = "ExternalOutput" if self.debug else "Internal"
        return self.nc.dram_tensor(name, shape, dtype, kind=kind).ap()

    def phase(self, name, **emits):
        self.phases.append((name, emits))

    def barrier(self, name):
        if not hasattr(self, "_barsems"):
            self._barsems = [self.sem(f"bar{i}") for i in range(5)]
            self._barcnt = 0
        self._barcnt += 1
        k = self._barcnt
        sems = self._barsems

        def mk(i):
            def f(eng):
                eng.sem_inc(sems[i], 1)
                for j in range(5):
                    eng.wait_ge(sems[j], k)
            return f
        self.phase("bar_" + name, sync=mk(0), vector=mk(1), scalar=mk(2),
                   tensor=mk(3), gpsimd=mk(4))

    def emit(self):
        with self.nc.Block() as block:
            @block.sync
            def _(q):
                for _n, em in self.phases:
                    if "sync" in em:
                        em["sync"](q)

            @block.vector
            def _(v):
                for _n, em in self.phases:
                    if "vector" in em:
                        em["vector"](v)

            @block.scalar
            def _(sc):
                for _n, em in self.phases:
                    if "scalar" in em:
                        em["scalar"](sc)

            @block.tensor
            def _(t):
                for _n, em in self.phases:
                    if "tensor" in em:
                        em["tensor"](t)

            @block.gpsimd
            def _(gp):
                for _n, em in self.phases:
                    if "gpsimd" in em:
                        em["gpsimd"](gp)



def ln_phase(p, tag, nchunks, x_dram, xn_buf, Ttok, eps_sb=None):
    s_x1 = [p.sem(f"{tag}_x1a"), p.sem(f"{tag}_x1b")]
    s_x2 = s_x1
    npar = [(nchunks + 1) // 2, nchunks // 2]
    s_sum = p.sem(f"{tag}_sum")
    s_ssq = p.sem(f"{tag}_ssq")
    s_ch = p.sem(f"{tag}_ch")
    s_ln = p.sem(f"{tag}_ln")
    s_tr = [p.sem(f"{tag}_tra"), p.sem(f"{tag}_trb")]

    xbuf, cm_xb = p.sb(f"{tag}_xb", [128, 2 * HID], F32)
    sums, cm1 = p.sb(f"{tag}_sums", [128, nchunks], F32)
    ssqs, cm2 = p.sb(f"{tag}_ssqs", [128, nchunks], F32)
    mean, cm3 = p.sb(f"{tag}_mean", [128, nchunks], F32)
    var, cm4 = p.sb(f"{tag}_var", [128, nchunks], F32)
    rstd, cm5 = p.sb(f"{tag}_rstd", [128, nchunks], F32)
    nbias, cm6 = p.sb(f"{tag}_nbias", [128, nchunks], F32)
    xn_tm, cm_xn = p.sb(f"{tag}_xntm", [128, 2 * HID], BF16)
    sqscr = xn_tm
    frees = [cm_xb, cm1, cm2, cm3, cm4, cm5, cm6, cm_xn]

    def xsl(i):
        return xbuf[:, (i % 2) * HID:(i % 2 + 1) * HID]

    def xnsl(i):
        return xn_tm[:, (i % 2) * HID:(i % 2 + 1) * HID]

    def sync_fn(q):
        for tc in range(nchunks):
            if tc >= 2:
                q.wait_ge(s_ssq, tc - 1)
                q.wait_ge(s_sum, tc - 1)
            q.dma_start(out=xsl(tc), in_=x_dram[tc * 128:(tc + 1) * 128, :]).then_inc(s_x1[tc % 2], 16)
        def transposes(tc):
            q.wait_ge(s_ln, tc + 1)
            for hc in range(32):
                inst = q.dma_start(out=xn_buf[:, hc * Ttok + tc * 128:hc * Ttok + (tc + 1) * 128],
                                   in_=xnsl(tc)[:, hc * 128:(hc + 1) * 128],
                                   transpose=True)
                inst.then_inc(s_tr[tc % 2], 16)

        for tc in range(nchunks):
            if tc == 0:
                q.wait_ge(s_sum, nchunks)
                q.wait_ge(s_ssq, nchunks)
            if tc >= 2:
                q.wait_ge(s_ln, tc - 1)
            q.dma_start(out=xsl(tc), in_=x_dram[tc * 128:(tc + 1) * 128, :]).then_inc(s_x2[tc % 2], 16)
            if tc >= 1:
                transposes(tc - 1)
        transposes(nchunks - 1)
        for par in range(2):
            n = len([t for t in range(nchunks) if t % 2 == par])
            if n:
                q.wait_ge(s_tr[par], 16 * 32 * n)

    def v_fn(v):
        for tc in range(nchunks):
            v.wait_ge(s_x1[tc % 2], 16 * (tc // 2 + 1))
            v.reduce_sum(sums[:, tc:tc + 1], xsl(tc), axis=AX.X).then_inc(s_sum, 1)
        v.wait_ge(s_ssq, nchunks)
        v.wait_ge(s_sum, nchunks)
        v.tensor_scalar_mul(mean[:], sums[:], 1.0 / HID).then_inc(s_ch, 1)
        v.wait_ge(s_ch, 1)
        v.tensor_mul(var[:], mean[:], mean[:]).then_inc(s_ch, 1)
        v.wait_ge(s_ch, 2)
        v.scalar_tensor_tensor(var[:], ssqs[:], 1.0 / HID, var[:], ALU.mult, ALU.subtract).then_inc(s_ch, 1)
        v.wait_ge(s_ch, 5)
        v.scalar_tensor_tensor(nbias[:], mean[:], -1.0, rstd[:], ALU.mult, ALU.mult).then_inc(s_ch, 1)

    def sc_fn(sc):
        for tc in range(nchunks):
            sc.wait_ge(s_x1[tc % 2], 16 * (tc // 2 + 1))
            if tc >= 1:
                sc.wait_ge(s_ssq, tc)
            sc.activation(sqscr[:, 0:HID], xsl(tc), AF.Square, bias=0.0, scale=1.0,
                          accum_out=ssqs[:, tc:tc + 1]).then_inc(s_ssq, 1)
        sc.wait_ge(s_ch, 3)
        sc.activation(nbias[:], var[:], AF.Ln, bias=eps_sb[:, 0:1], scale=1.0).then_inc(s_ch, 1)
        sc.wait_ge(s_ch, 4)
        sc.activation(rstd[:], nbias[:], AF.Exp, bias=0.0, scale=-0.5).then_inc(s_ch, 1)
        sc.wait_ge(s_ch, 6)
        sc.wait_ge(s_ssq, nchunks)
        for tc in range(nchunks):
            sc.wait_ge(s_x2[tc % 2], 16 * (npar[tc % 2] + tc // 2 + 1))
            if tc >= 2:
                sc.wait_ge(s_tr[tc % 2], 16 * 32 * (tc // 2))
            sc.activation(xnsl(tc), xsl(tc), AF.Identity,
                          bias=nbias[:, tc:tc + 1], scale=rstd[:, tc:tc + 1]).then_inc(s_ln, 1)

    p.phase(tag, sync=sync_fn, vector=v_fn, scalar=sc_fn)
    return dict(sem=s_tr[(nchunks - 1) % 2], n=16 * 32 * ((nchunks - 1) // 2 + 1),
                sem2=s_tr[nchunks % 2], n2=16 * 32 * (nchunks // 2) if nchunks >= 2 else 0,
                frees=frees)



def proj_phase(p, tag, nchunks, xn_buf, Ttok, groups, ln_dep, ropec_d, ropes_d, R):
    STn = Ttok // 512
    assert Ttok % 512 == 0 and STn <= 4
    s_w = [p.sem(f"{tag}_wa"), p.sem(f"{tag}_wb2")]
    s_pe = p.sem(f"{tag}_pe")
    s_ev = p.sem(f"{tag}_ev")
    s_rp1 = p.sem(f"{tag}_rp1")
    s_rp = p.sem(f"{tag}_rp")
    s_sp = p.sem(f"{tag}_sp")
    s_ktr = p.sem(f"{tag}_ktr")
    s_rop = p.sem(f"{tag}_rop")

    wbuf, cw = p.sb(f"{tag}_wb", [128, 2 * 32 * 128], BF16)
    ev_scr, ce = p.sb(f"{tag}_ev", [128, 512], F32)
    t1, c1 = p.sb(f"{tag}_t1", [128, 512], F32)
    t2, c2 = p.sb(f"{tag}_t2", [128, 512], F32)
    sw, c2b = p.sb(f"{tag}_sw", [128, 512], F32)
    stage, c3 = p.sb(f"{tag}_stg", [128, Ttok], BF16)
    ktm_stage, c4 = p.sb(f"{tag}_ktms", [128, Ttok], BF16)
    ropec, c5 = p.sb(f"{tag}_ropec", [128, Ttok], BF16)
    ropes, c6 = p.sb(f"{tag}_ropes", [128, Ttok], BF16)
    frees = [cw, ce, c1, c2, c2b, c3, c4, c5, c6]

    pb = R["pb"]

    seq = []
    for g in groups:
        for fo in range(g["n_fo"]):
            seq.append((g, fo))

    ev_cnt, rp_cnt, sp_cnt, ktr_cnt = {}, {}, {}, {}
    c_ev = c_rp = c_sp = c_ktr = 0
    for si, (g, fo) in enumerate(seq):
        for st in range(STn):
            c_ev += 1
            ev_cnt[(si, st)] = c_ev
            if g["kind"] in ("q", "k"):
                c_rp += 1
                rp_cnt[(si, st)] = c_rp
        k = g["kind"]
        if k in ("q", "g"):
            c_sp += 16
        elif k == "k":
            c_sp += 32 if g.get("dst") is not None else 16
            c_ktr += 16 * nchunks
        elif k == "v":
            c_sp += 16
            c_ktr += 16 * nchunks
        sp_cnt[si] = c_sp
        ktr_cnt[si] = c_ktr

    def prev_sp(si):
        return sp_cnt[si - 1] if si >= 1 else 0

    def gp_fn(gp):
        for si, (g, fo) in enumerate(seq):
            if si >= 2:
                gp.wait_ge(s_pe, (si - 1) * STn)
            fw = g["fo_cols"]
            gp.dma_start(out=wbuf[:, (si % 2) * 4096:(si % 2) * 4096 + 32 * fw],
                         in_=g["w"][fo].rearrange("p hc c -> p (hc c)")).then_inc(s_w[si % 2], 16)

    def pe_fn(t):
        t.wait_ge(ln_dep["sem"], ln_dep["n"])
        if ln_dep.get("n2"):
            t.wait_ge(ln_dep["sem2"], ln_dep["n2"])
        for si, (g, fo) in enumerate(seq):
            t.wait_ge(s_w[si % 2], 16 * (si // 2 + 1))
            if si >= 2:
                t.wait_ge(s_ev, ev_cnt[(si - 2, STn - 1)])
            fw = g["fo_cols"]
            for st in range(STn):
                bank = pb[(si % 2) * 4 + st]
                for hc in range(32):
                    inst = t.matmul(bank[0:fw, :],
                                    wbuf[:, (si % 2) * 4096 + hc * fw:(si % 2) * 4096 + (hc + 1) * fw],
                                    xn_buf[:, hc * Ttok + st * 512:hc * Ttok + (st + 1) * 512],
                                    start=(hc == 0), stop=(hc == 31))
                inst.then_inc(s_pe, 1)

    def sc_fn(sc):
        for si, (g, fo) in enumerate(seq):
            kind = g["kind"]
            for st in range(STn):
                sc.wait_ge(s_pe, si * STn + st + 1)
                bank = pb[(si % 2) * 4 + st]
                if kind in ("q", "k"):
                    i = rp_cnt[(si, st)]
                    if i >= 2:
                        sc.wait_ge(s_rp1, 2 * i - 3)
                    sc.activation(ev_scr[:], bank[:, :], AF.Identity,
                                  bias=0.0, scale=1.0).then_inc(s_ev, 1)
                elif kind in ("v", "g"):
                    if st == 0:
                        sc.wait_ge(s_sp, prev_sp(si))
                    sc.activation(stage[:, st * 512:(st + 1) * 512], bank[:, :],
                                  AF.Identity, bias=0.0, scale=1.0).then_inc(s_ev, 1)
                else:
                    sc.activation(R["dt_raw"][0:32, g["dt_off"] + st * 512:g["dt_off"] + (st + 1) * 512],
                                  bank[0:32, :], AF.Identity,
                                  bias=R["dtb_sb"][:, 0:1], scale=1.0).then_inc(s_ev, 1)

    def v_fn(v):
        v.wait_ge(s_rop, 32)
        for si, (g, fo) in enumerate(seq):
            if g["kind"] not in ("q", "k"):
                continue
            for st in range(STn):
                i = rp_cnt[(si, st)]
                v.wait_ge(s_ev, ev_cnt[(si, st)])
                if i >= 2:
                    v.wait_ge(s_rp, i - 1)
                    v.wait_ge(s_rp1, 2 * (i - 1))
                if st == 0:
                    v.wait_ge(s_sp, prev_sp(si))
                cs = ropec[:, st * 512:(st + 1) * 512]
                ss = ropes[:, st * 512:(st + 1) * 512]
                v.tensor_copy(sw[0:64, :], ev_scr[64:128, :])
                v.tensor_copy(sw[64:128, :], ev_scr[0:64, :])
                v.tensor_mul(t1[:], ev_scr[:], cs).then_inc(s_rp1, 1)
                v.wait_ge(s_rp1, 2 * i - 1)
                v.tensor_mul(t2[:], sw[:], ss).then_inc(s_rp1, 1)
                v.wait_ge(s_rp1, 2 * i)
                v.tensor_add(stage[:, st * 512:(st + 1) * 512], t1[:], t2[:]).then_inc(s_rp, 1)

    def sync_fn(q):
        q.dma_start(out=ropec[:], in_=ropec_d[:, 0:Ttok]).then_inc(s_rop, 16)
        q.dma_start(out=ropes[:], in_=ropes_d[:, 0:Ttok]).then_inc(s_rop, 16)
        for si, (g, fo) in enumerate(seq):
            kind = g["kind"]
            if kind == "win":
                continue
            if kind in ("q", "k"):
                q.wait_ge(s_rp, rp_cnt[(si, STn - 1)])
            else:
                q.wait_ge(s_ev, ev_cnt[(si, STn - 1)])
            src = stage.rearrange("p (t c) -> p t c", c=128)
            if kind in ("q", "g", "k") and g.get("dst") is not None:
                dst = (g["dst"].rearrange("t p (f c) -> t p f c", c=128)
                       [g["tc0"]:g["tc0"] + nchunks, :, fo, :].transpose([1, 0, 2]))
                q.dma_start(out=dst, in_=src).then_inc(s_sp, 16)
            if kind in ("k", "v"):
                if si >= 1:
                    q.wait_ge(s_sp, prev_sp(si))
                for tc in range(nchunks):
                    q.dma_start(out=ktm_stage[:, tc * 128:(tc + 1) * 128],
                                in_=stage[:, tc * 128:(tc + 1) * 128],
                                transpose=True).then_inc(s_ktr, 16)
                q.wait_ge(s_ktr, ktr_cnt[si])
                dst2 = (g["dst2"].rearrange("t p (f c) -> t p f c", c=128)
                        [g["ktc0"]:g["ktc0"] + nchunks, :, fo, :].transpose([1, 0, 2]))
                q.dma_start(out=dst2, in_=ktm_stage.rearrange("p (t c) -> p t c", c=128)).then_inc(s_sp, 16)
        q.wait_ge(s_sp, c_sp)
        if c_ktr:
            q.wait_ge(s_ktr, c_ktr)

    p.phase(tag, sync=sync_fn, gpsimd=gp_fn, scalar=sc_fn, tensor=pe_fn, vector=v_fn)
    return dict(frees=frees, s_sp=s_sp, n_sp=c_sp, s_ev=s_ev, n_ev=c_ev)




def fdn_phase(p, TM, qdnT, sdn_cols, zdnT, h_s, ffn_s, R):
    TH = TM // 2 if TM >= 1024 else TM
    NHALF = TM // TH
    ST2 = TH // 512
    assert TH % 512 == 0 and ST2 <= 2
    pb = R["pb"]
    s_hld = p.sem("fdn_hld")
    s_qd = [p.sem("fdn_qda"), p.sem("fdn_qdb")]
    s_zc = p.sem("fdn_zc")
    s_zs = p.sem("fdn_zs")
    s_nx = p.sem("fdn_nx")
    s_dq = p.sem("fdn_dq")
    s_tr = [p.sem(f"fdn_tr{i}") for i in range(4)]
    s_pe = p.sem("fdn_pe")
    s_ev = p.sem("fdn_ev")
    s_sp = [p.sem("fdn_spa"), p.sem("fdn_spb")]

    hbuf, ch = p.sb("fdn_h", [128, HB * TH], BF16)
    qsub, cq = p.sb("fdn_qsub", [128, 2 * 1024], I32)
    nib, cn = p.sb("fdn_nib", [128, 2 * 1024], I32)
    wtT, cw = p.sb("fdn_wtT", [128, 2 * 1024], BF16)
    wkd, ck = p.sb("fdn_wkd", [128, 2 * 8 * 128], BF16)
    scc, cs1 = p.sb("fdn_scc", [128, 4 * HB], F32)
    zc, cs2 = p.sb("fdn_zc", [128, HB], I32)
    nibz = nib
    zsc, cs4 = p.sb("fdn_zsc", [128, 4 * HB], F32)
    stage, cst = p.sb("fdn_stage", [128, 2 * 512], BF16)
    frees = [ch, cq, cn, cw, ck, cs1, cs2, cs4, cst]

    NIS = 11
    npass = NHALF * DN_CT * 2

    def it_idx(half, ct, sg, isub):
        return ((half * DN_CT + ct) * 2 + sg) * NIS + isub

    def pass_idx(half, ct, sg):
        return (half * DN_CT + ct) * 2 + sg

    def sync_fn(q):
        for half in range(NHALF):
            if half >= 1:
                q.wait_ge(s_pe, 4 * (it_idx(half - 1, DN_CT - 1, 1, NIS - 1) + 1))
            for hb in range(HB):
                q.dma_start(out=hbuf[:, hb * TH:(hb + 1) * TH],
                            in_=h_s[hb][:, half * TH:half * TH + TH]).then_inc(s_hld, 16)
            for ct in range(DN_CT):
                for sg in range(2):
                    pidx = pass_idx(half, ct, sg)
                    if pidx >= 1:
                        q.wait_ge(s_dq, 4 * NIS * pidx)
                    q.dma_start(out=scc[:], in_=sdn_cols[:, ct * NPLANE * HB + sg * 4 * HB:
                                                         ct * NPLANE * HB + (sg * 4 + 4) * HB]).then_inc(s_zc, 16)
                    if sg == 0:
                        q.dma_start(out=zc[:], in_=zdnT[ct * 128:(ct + 1) * 128, :]).then_inc(s_zc, 16)
                    else:
                        q.sem_inc(s_zc, 16)
                    for isub in range(NIS):
                        gi = it_idx(half, ct, sg, isub)
                        if gi >= 2:
                            q.wait_ge(s_nx, 4 * (gi - 1))
                        q.dma_start(out=qsub[:, (gi % 2) * 1024:(gi % 2 + 1) * 1024],
                                    in_=qdnT[ct][:, isub * 1024:(isub + 1) * 1024]).then_inc(s_qd[gi % 2], 16)
                        for spl in range(4):
                            ei = gi * 4 + spl
                            q.wait_ge(s_dq, ei + 1)
                            if ei >= 2:
                                q.wait_ge(s_pe, ei - 1)
                            for j in range(0, 8):
                                cls = (ei % 2) * 2 + (j % 2)
                                q.dma_start(out=wkd[:, ((ei % 2) * 8 + j) * 128:((ei % 2) * 8 + j + 1) * 128],
                                            in_=wtT[:, (ei % 2) * 1024 + j * 128:(ei % 2) * 1024 + (j + 1) * 128],
                                            transpose=True).then_inc(s_tr[cls], 16)
                    nb = 4 * ST2
                    for spl in range(4):
                        sp_ = sg * 4 + spl
                        for st in range(ST2):
                            t0 = half * TH + st * 512
                            dst2 = ffn_s[8 * ct:8 * ct + 8, :, t0:t0 + 512] \
                                .rearrange("a (pl e) t -> a pl e t", e=8)[:, :, sp_, :]
                            ev_i = pidx * nb + spl * ST2 + st
                            src = stage[:, (ev_i % 2) * 512:(ev_i % 2 + 1) * 512]
                            q.wait_ge(s_ev, ev_i + 1)
                            q.dma_start(out=dst2, in_=src).then_inc(s_sp[ev_i % 2], 16)
        ntot = npass * 4 * ST2
        q.wait_ge(s_sp[0], 16 * ((ntot + 1) // 2))
        q.wait_ge(s_sp[1], 16 * (ntot // 2))
        n_items = npass * NIS
        for c in range(4):
            q.wait_ge(s_tr[c], 16 * 4 * n_items)

    def v_fn(v):
        for half in range(NHALF):
            for ct in range(DN_CT):
                for sg in range(2):
                    pidx = pass_idx(half, ct, sg)
                    v.wait_ge(s_zc, 32 * (pidx + 1))
                    if pidx >= 1:
                        v.wait_ge(s_dq, 4 * NIS * pidx)
                    for spl in range(4):
                        sp_ = sg * 4 + spl
                        inst = v.tensor_scalar(nibz[:, spl * HB:(spl + 1) * HB], zc[:], 4 * sp_, 15,
                                               ALU.logical_shift_right, ALU.bitwise_and)
                    inst.then_inc(s_zs, 1)
                    v.wait_ge(s_zs, 2 * pidx + 1)
                    for spl in range(4):
                        inst = v.scalar_tensor_tensor(zsc[:, spl * HB:(spl + 1) * HB],
                                                      nibz[:, spl * HB:(spl + 1) * HB], -1.0,
                                                      scc[:, spl * HB:(spl + 1) * HB],
                                                      ALU.mult, ALU.mult)
                    inst.then_inc(s_zs, 1)
                    for isub in range(NIS):
                        gi = it_idx(half, ct, sg, isub)
                        v.wait_ge(s_qd[gi % 2], 16 * (gi // 2 + 1))
                        if isub == 0:
                            v.wait_ge(s_zs, 2 * pidx + 2)
                        for spl in range(4):
                            sp_ = sg * 4 + spl
                            ei = gi * 4 + spl
                            if ei >= 2:
                                v.wait_ge(s_dq, ei - 1)
                            v.tensor_scalar(nib[:, (ei % 2) * 1024:(ei % 2 + 1) * 1024],
                                            qsub[:, (gi % 2) * 1024:(gi % 2 + 1) * 1024],
                                            4 * sp_, 15, ALU.logical_shift_right,
                                            ALU.bitwise_and).then_inc(s_nx, 1)

    def sc_fn(sc):
        for half in range(NHALF):
            for ct in range(DN_CT):
                for sg in range(2):
                    pidx = pass_idx(half, ct, sg)
                    for isub in range(NIS):
                        gi = it_idx(half, ct, sg, isub)
                        for spl in range(4):
                            ei = gi * 4 + spl
                            sc.wait_ge(s_nx, ei + 1)
                            sc.wait_ge(s_zs, 2 * pidx + 2)
                            if ei >= 2:
                                zz = ei - 2
                                sc.wait_ge(s_tr[(zz % 2) * 2], 16 * 4 * (zz // 2 + 1))
                                sc.wait_ge(s_tr[(zz % 2) * 2 + 1], 16 * 4 * (zz // 2 + 1))
                            for q8 in range(8):
                                g = 8 * isub + q8
                                src = nib[:, (ei % 2) * 1024:(ei % 2 + 1) * 1024] \
                                    .rearrange("p (s c) -> p s c", c=128)[:, :, q8 * 16:(q8 + 1) * 16]
                                dst = wtT[:, (ei % 2) * 1024:(ei % 2 + 1) * 1024] \
                                    .rearrange("p (s c) -> p s c", c=128)[:, :, q8 * 16:(q8 + 1) * 16]
                                inst = sc.activation(dst, src, AF.Identity,
                                                     bias=zsc[:, spl * HB + g:spl * HB + g + 1],
                                                     scale=scc[:, spl * HB + g:spl * HB + g + 1])
                            inst.then_inc(s_dq, 1)
                    nb = 4 * ST2
                    sc.wait_ge(s_pe, 4 * (pidx + 1) * NIS)
                    for spl in range(4):
                        for st in range(ST2):
                            ev_i = pidx * nb + spl * ST2 + st
                            if ev_i >= 2:
                                zz = ev_i - 2
                                sc.wait_ge(s_sp[zz % 2], 16 * (zz // 2 + 1))
                            sc.activation(stage[:, (ev_i % 2) * 512:(ev_i % 2 + 1) * 512],
                                          pb[spl * ST2 + st][:, :], AF.Identity,
                                          bias=0.0, scale=1.0).then_inc(s_ev, 1)


    def pe_fn(t):
        for half in range(NHALF):
            t.wait_ge(s_hld, 16 * HB * (half + 1))
            for ct in range(DN_CT):
                for sg in range(2):
                    pidx = pass_idx(half, ct, sg)
                    if pidx >= 1:
                        t.wait_ge(s_ev, pidx * 4 * ST2)
                    for isub in range(NIS):
                        gi = it_idx(half, ct, sg, isub)
                        for spl in range(4):
                            ei = gi * 4 + spl
                            t.wait_ge(s_tr[(ei % 2) * 2], 16 * 4 * (ei // 2 + 1))
                            t.wait_ge(s_tr[(ei % 2) * 2 + 1], 16 * 4 * (ei // 2 + 1))
                            for j in range(8):
                                hb = isub * 8 + j
                                for st in range(ST2):
                                    inst = t.matmul(
                                        pb[spl * ST2 + st][:, :],
                                        wkd[:, ((ei % 2) * 8 + j) * 128:((ei % 2) * 8 + j + 1) * 128],
                                        hbuf[:, hb * TH + st * 512:hb * TH + st * 512 + 512],
                                        start=(isub == 0 and j == 0), stop=(isub == NIS - 1 and j == 7))
                            inst.then_inc(s_pe, 1)

    p.phase("fdn", sync=sync_fn, vector=v_fn, scalar=sc_fn, tensor=pe_fn)
    return dict(frees=frees)


def att_phase(p, NM, NP, q_s, g_s, kfm_s, ktm_s, vtm_s, dtraw_s, y_s, R):
    NT = NM + NP
    pb = R["pb"]
    ident_sb, ones1_sb, onesr_sb = R["ident_sb"], R["ones1_sb"], R["onesr_sb"]
    u_sb, mneg_sb, acol_sb, gnw_sb = R["u_sb"], R["mneg_sb"], R["acol_sb"], R["gnw_sb"]
    eps_sb, c_rsel = R["eps_sb"], R["c_rsel"]

    s_dt = p.sem("att_dt")
    s_dta = p.sem("att_dta")
    s_dtv = p.sem("att_dtv")
    s_ld = [p.sem("att_lda"), p.sem("att_ldb")]
    s_clp = p.sem("att_clp")
    s_clv = p.sem("att_clv")
    s_cla = p.sem("att_cla")
    s_xk = p.sem("att_xk")
    s_hp = p.sem("att_hp")
    s_hv = p.sem("att_hv")
    s_ha = p.sem("att_ha")
    s_hg = p.sem("att_hg")
    s_ytr = p.sem("att_ytr")
    s_ysp = p.sem("att_ysp")

    TPc = NP * 128
    NTT = NT * 128
    rsel_sb, c0 = p.sb("att_rsel", [32, 32 * 128], F32)
    dtb, c1 = p.sb("att_dtraw", [32, NTT], F32)
    spb, c2 = p.sb("att_spb", [32, NTT], F32)
    gall, c3 = p.sb("att_gall", [32, NTT], F32)
    qc, c4 = p.sb("att_qc", [128, 2 * 4096], BF16)
    gc, c5 = p.sb("att_gc", [128, 2 * 4096], BF16)
    kf, c6 = p.sb("att_kf", [128, 2 * 1024], BF16)
    kt, c7 = p.sb("att_kt", [128, 2 * 1024], BF16)
    vt, c8 = p.sb("att_vt", [128, 2 * 1024], BF16)
    Sf, c9 = p.sb("att_Sf", [128, 32 * 128], F32)
    Sb, c10 = p.sb("att_Sb", [128, 32 * 128], BF16)
    xc_all, c11 = p.sb("att_xc", [128, 32 * 128], BF16)
    kd_all, c12 = p.sb("att_kd", [128, 32 * 128], BF16)
    dtT, c13 = p.sb("att_dtT", [128, 32], F32)
    gt, c14 = p.sb("att_gt", [128, 32], F32)
    GT, c15 = p.sb("att_GT", [128, 32], F32)
    glrow, c16 = p.sb("att_glrow", [1, 32], F32)
    egrow, c17 = p.sb("att_egrow", [1, 32], F32)
    EGB, c18 = p.sb("att_EGB", [128, 32], F32)
    d1, c19 = p.sb("att_d1", [128, 32], F32)
    EJL, c20 = p.sb("att_EJL", [128, 32], F32)
    EI, c21 = p.sb("att_EI", [128, 32], F32)
    gth, c22 = p.sb("att_gth", [32, 128], F32)
    d2, c23 = p.sb("att_d2", [128, 2 * 128], F32)
    dec, c24 = p.sb("att_dec", [128, 2 * 128], BF16)
    pbf, c25 = p.sb("att_pbf", [128, 2 * 128], BF16)
    yscr, c26 = p.sb("att_yscr", [128, 2 * 128], F32)
    ytm, c27 = p.sb("att_ytm", [128, 2 * 128], F32)
    ynb, c28 = p.sb("att_ynb", [128, 2 * 128], BF16)
    ynf, c29 = p.sb("att_ynf", [128, 2 * 128], BF16)
    sgt, c30 = p.sb("att_sgt", [128, 2 * 128], F32)
    tmul, c31 = p.sb("att_tmul", [128, 2 * 128], BF16)
    ystg, c32 = p.sb("att_ystg", [128, 32 * 128], BF16)
    sqt, c33 = p.sb("att_sqt", [128, 2 * 128], F32)
    ssq, c34 = p.sb("att_ssq", [128, 2], F32)
    rms, c35 = p.sb("att_rms", [128, 2], F32)
    frees = [c0, c1, c2, c3, c4, c5, c6, c7, c8, c9, c10, c11, c12, c13, c14, c15,
             c16, c17, c18, c19, c20, c21, c22, c23, c24, c25, c26, c27, c28,
             c29, c30, c31, c32, c33, c34, c35]

    NCLP, NCLV, NCLA = 7, 6, 4
    def is_main(n):
        return n >= NP
    HP_M, HV_M, HA_M, HG_M = 5, 7, 6, 1
    HP_P, HV_P = 1, 2

    hp_base = [0]; hv_base = [0]; ha_base = [0]; hg_base = [0]; xk_base = [0]
    ytr_base = [0]; ysp_base = [0]; ld_base = [[0], [0]]
    for n in range(NT):
        m = is_main(n)
        hp_base.append(hp_base[-1] + 32 * (HP_M if m else HP_P))
        hv_base.append(hv_base[-1] + 32 * (HV_M if m else HV_P))
        ha_base.append(ha_base[-1] + 32 * (HA_M if m else 0))
        hg_base.append(hg_base[-1] + 32 * (HG_M if m else 0))
        xk_base.append(xk_base[-1] + 32)
        ytr_base.append(ytr_base[-1] + (16 * 32 if m else 0))
        ysp_base.append(ysp_base[-1] + (16 * 32 if m else 0))
        nl = 5 if m else 2
        for par in range(2):
            ld_base[par].append(ld_base[par][-1] + (16 * nl if n % 2 == par else 0))

    def qsl(n):
        return qc[:, (n % 2) * 4096:(n % 2) * 4096 + 4096]

    def gsl(n):
        return gc[:, (n % 2) * 4096:(n % 2) * 4096 + 4096]

    def kfsl(n):
        return kf[:, (n % 2) * 1024:(n % 2) * 1024 + 1024]

    def ktsl(n):
        return kt[:, (n % 2) * 1024:(n % 2) * 1024 + 1024]

    def vtsl(n):
        return vt[:, (n % 2) * 1024:(n % 2) * 1024 + 1024]

    def chunk_consumed_waits(eng, n):
        eng.wait_ge(s_hp, hp_base[n + 1])
        eng.wait_ge(s_hv, hv_base[n + 1])
        if is_main(n):
            eng.wait_ge(s_ha, ha_base[n + 1])
            eng.wait_ge(s_hg, hg_base[n + 1])

    def sync_fn(q):
        q.dma_start(out=rsel_sb[:], in_=c_rsel).then_inc(s_dt, 16)
        q.dma_start(out=dtb[:], in_=dtraw_s).then_inc(s_dt, 16)

        def loads(n):
            par = n % 2
            if n >= 2:
                chunk_consumed_waits(q, n - 2)
            if is_main(n):
                nm = n - NP
                q.dma_start(out=qsl(n), in_=q_s[nm]).then_inc(s_ld[par], 16)
                q.dma_start(out=gsl(n), in_=g_s[nm]).then_inc(s_ld[par], 16)
                q.dma_start(out=kfsl(n), in_=kfm_s[nm]).then_inc(s_ld[par], 16)
            q.dma_start(out=ktsl(n), in_=ktm_s[n]).then_inc(s_ld[par], 16)
            q.dma_start(out=vtsl(n), in_=vtm_s[n]).then_inc(s_ld[par], 16)

        loads(0)
        for n in range(NT):
            if n + 1 < NT:
                loads(n + 1)
            if is_main(n):
                nm = n - NP
                for h in range(32):
                    q.wait_ge(s_hv, hv_base[n] + h * HV_M + 7)
                    if h >= 2:
                        q.wait_ge(s_hg, hg_base[n] + h - 1)
                    q.dma_start(out=ynf[:, (h % 2) * 128:(h % 2) * 128 + 128],
                                in_=ynb[:, (h % 2) * 128:(h % 2) * 128 + 128],
                                transpose=True).then_inc(s_ytr, 16)
                    q.wait_ge(s_ha, ha_base[n] + h * HA_M + 6)
                    q.dma_start(out=y_s[h][:, nm * 128:(nm + 1) * 128],
                                in_=ystg[:, h * 128:(h + 1) * 128]).then_inc(s_ysp, 16)
        q.wait_ge(s_ysp, ysp_base[NT])
        q.wait_ge(s_ytr, ytr_base[NT])

    def v_fn(v):
        v.memzero(Sf[:])
        v.memzero(Sb[:]).then_inc(s_dtv, 1)
        v.wait_ge(s_dt, 32)
        v.tensor_scalar_max(gall[:], dtb[:], 0.0).then_inc(s_dtv, 1)
        v.wait_ge(s_dta, 3)
        v.wait_ge(s_dtv, 2)
        v.tensor_add(spb[:], spb[:], gall[:]).then_inc(s_dtv, 1)
        v.wait_ge(s_dtv, 3)
        v.tensor_scalar(gall[:], spb[:], acol_sb[0:32, 0:1], None, ALU.mult).then_inc(s_dtv, 1)
        for n in range(NT):
            m = is_main(n)
            if n >= 1:
                v.wait_ge(s_hp, hp_base[n])
                v.wait_ge(s_xk, xk_base[n])
                v.wait_ge(s_cla, NCLA * n)
            v.wait_ge(s_clp, NCLP * n + 1)
            v.tensor_copy(dtT[:], pb[0][:, 0:32]).then_inc(s_clv, 1)
            v.wait_ge(s_clp, NCLP * n + 2)
            v.tensor_copy(gt[:], pb[1][:, 0:32]).then_inc(s_clv, 1)
            v.wait_ge(s_clp, NCLP * n + 3)
            v.tensor_copy(GT[:], pb[2][:, 0:32]).then_inc(s_clv, 1)
            v.wait_ge(s_clp, NCLP * n + 5)
            v.tensor_copy(EGB[:], pb[1][:, 0:32]).then_inc(s_clv, 1)
            v.wait_ge(s_clp, NCLP * n + 6)
            v.tensor_sub(d1[:], pb[2][:, 0:32], GT[:]).then_inc(s_clv, 1)
            v.wait_ge(s_clp, NCLP * n + 7)
            v.tensor_copy(gth[:], pb[0][0:32, 0:128]).then_inc(s_clv, 1)
            v.wait_ge(s_ld[n % 2], ld_base[n % 2][n + 1])
            for h in range(32):
                kv = h // 4
                v.wait_ge(s_cla, NCLA * n + 3)
                v.tensor_scalar(xc_all[:, h * 128:(h + 1) * 128], vtsl(n)[:, kv * 128:(kv + 1) * 128],
                                dtT[:, h:h + 1], None, ALU.mult)
                v.tensor_scalar(kd_all[:, h * 128:(h + 1) * 128], ktsl(n)[:, kv * 128:(kv + 1) * 128],
                                EJL[:, h:h + 1], None, ALU.mult).then_inc(s_xk, 1)
            if m:
                for h in range(32):
                    hb = hv_base[n] + h * HV_M
                    v.wait_ge(s_hp, hp_base[n] + h * HP_M + 1)
                    v.scalar_tensor_tensor(d2[:, (h % 2) * 128:(h % 2) * 128 + 128],
                                           pb[3][:, 0:128], GT[:, h:h + 1], mneg_sb[:],
                                           ALU.subtract, ALU.add).then_inc(s_hv, 1)
                    v.wait_ge(s_hp, hp_base[n] + h * HP_M + 2)
                    v.wait_ge(s_ha, ha_base[n] + h * HA_M + 1)
                    v.tensor_mul(pbf[:, (h % 2) * 128:(h % 2) * 128 + 128],
                                 pb[h % 2][:, 0:128],
                                 dec[:, (h % 2) * 128:(h % 2) * 128 + 128]).then_inc(s_hv, 1)
                    v.wait_ge(s_hp, hp_base[n] + h * HP_M + 3)
                    v.tensor_scalar(yscr[:, (h % 2) * 128:(h % 2) * 128 + 128],
                                    pb[4 + (h % 2) * 2][:, 0:128], EI[:, h:h + 1], None,
                                    ALU.mult).then_inc(s_hv, 1)
                    v.wait_ge(s_hp, hp_base[n] + h * HP_M + 4)
                    v.wait_ge(s_hv, hb + 3)
                    v.tensor_add(ytm[:, (h % 2) * 128:(h % 2) * 128 + 128],
                                 pb[5 + (h % 2) * 2][:, 0:128],
                                 yscr[:, (h % 2) * 128:(h % 2) * 128 + 128]).then_inc(s_hv, 1)
                    v.wait_ge(s_hp, hp_base[n] + h * HP_M + 5)
                    v.scalar_tensor_tensor(Sf[:, h * 128:(h + 1) * 128], Sf[:, h * 128:(h + 1) * 128],
                                           EGB[:, h:h + 1], pb[2][:, 0:128],
                                           ALU.mult, ALU.add).then_inc(s_hv, 1)
                    v.wait_ge(s_hv, hb + 5)
                    v.tensor_copy(Sb[:, h * 128:(h + 1) * 128], Sf[:, h * 128:(h + 1) * 128]).then_inc(s_hv, 1)
                    v.wait_ge(s_ha, ha_base[n] + h * HA_M + 4)
                    v.tensor_scalar(ynb[:, (h % 2) * 128:(h % 2) * 128 + 128],
                                    ytm[:, (h % 2) * 128:(h % 2) * 128 + 128],
                                    rms[:, h % 2:h % 2 + 1], None, ALU.mult).then_inc(s_hv, 1)
            else:
                for h in range(32):
                    hb = hv_base[n] + h * HV_P
                    v.wait_ge(s_hp, hp_base[n] + h * HP_P + 1)
                    v.scalar_tensor_tensor(Sf[:, h * 128:(h + 1) * 128], Sf[:, h * 128:(h + 1) * 128],
                                           EGB[:, h:h + 1], pb[2 + h % 2][:, 0:128],
                                           ALU.mult, ALU.add).then_inc(s_hv, 1)
                    v.wait_ge(s_hv, hb + 1)
                    if n == NP - 1:
                        v.tensor_copy(Sb[:, h * 128:(h + 1) * 128], Sf[:, h * 128:(h + 1) * 128]).then_inc(s_hv, 1)
                    else:
                        v.sem_inc(s_hv, 1)

    def sc_fn(sc):
        sc.wait_ge(s_dt, 32)
        sc.activation(spb[:], dtb[:], AF.Abs, bias=0.0, scale=1.0).then_inc(s_dta, 1)
        sc.wait_ge(s_dta, 1)
        sc.activation(spb[:], spb[:], AF.Exp, bias=0.0, scale=-1.0).then_inc(s_dta, 1)
        sc.wait_ge(s_dta, 2)
        sc.activation(spb[:], spb[:], AF.Ln, bias=eps_sb[0:32, 1:2], scale=1.0).then_inc(s_dta, 1)
        for n in range(NT):
            m = is_main(n)
            if n >= 1:
                sc.wait_ge(s_hv, hv_base[n])
                sc.wait_ge(s_xk, xk_base[n])
            sc.wait_ge(s_clp, NCLP * n + 4)
            sc.activation(glrow[0:1, :], pb[0][0:1, 0:32], AF.Identity, bias=0.0, scale=1.0).then_inc(s_cla, 1)
            sc.wait_ge(s_cla, NCLA * n + 1)
            sc.activation(egrow[0:1, :], glrow[0:1, :], AF.Exp, bias=0.0, scale=1.0).then_inc(s_cla, 1)
            sc.wait_ge(s_clv, NCLV * n + 5)
            sc.activation(EJL[:], d1[:], AF.Exp, bias=0.0, scale=1.0).then_inc(s_cla, 1)
            sc.wait_ge(s_clv, NCLV * n + 3)
            sc.activation(EI[:], GT[:], AF.Exp, bias=0.0, scale=1.0).then_inc(s_cla, 1)
            if not m:
                continue
            sc.wait_ge(s_ld[n % 2], ld_base[n % 2][n + 1])
            sc.wait_ge(s_ysp, ysp_base[n])
            for h in range(32):
                ab = ha_base[n] + h * HA_M
                sc.wait_ge(s_hv, hv_base[n] + h * HV_M + 1)
                if h >= 2:
                    sc.wait_ge(s_hv, hv_base[n] + (h - 2) * HV_M + 2)
                sc.activation(dec[:, (h % 2) * 128:(h % 2) * 128 + 128],
                              d2[:, (h % 2) * 128:(h % 2) * 128 + 128], AF.Exp,
                              bias=0.0, scale=1.0).then_inc(s_ha, 1)
                sc.wait_ge(s_hv, hv_base[n] + h * HV_M + 4)
                if h >= 2:
                    sc.wait_ge(s_ha, ha_base[n] + (h - 2) * HA_M + 2)
                sc.activation(sqt[:, (h % 2) * 128:(h % 2) * 128 + 128],
                              ytm[:, (h % 2) * 128:(h % 2) * 128 + 128], AF.Square,
                              bias=0.0, scale=1.0, accum_out=ssq[:, h % 2:h % 2 + 1]).then_inc(s_ha, 1)
                sc.wait_ge(s_ha, ab + 2)
                sc.activation(rms[:, h % 2:h % 2 + 1], ssq[:, h % 2:h % 2 + 1], AF.Ln,
                              bias=eps_sb[:, 0:1], scale=1.0 / 128).then_inc(s_ha, 1)
                sc.wait_ge(s_ha, ab + 3)
                sc.activation(rms[:, h % 2:h % 2 + 1], rms[:, h % 2:h % 2 + 1], AF.Exp,
                              bias=0.0, scale=-0.5).then_inc(s_ha, 1)
                if h >= 2:
                    sc.wait_ge(s_hg, hg_base[n] + h - 1)
                sc.activation(sgt[:, (h % 2) * 128:(h % 2) * 128 + 128],
                              gsl(n)[:, h * 128:(h + 1) * 128], AF.Silu,
                              bias=0.0, scale=1.0).then_inc(s_ha, 1)
                sc.wait_ge(s_hg, hg_base[n] + h + 1)
                sc.activation(ystg[:, h * 128:(h + 1) * 128],
                              tmul[:, (h % 2) * 128:(h % 2) * 128 + 128], AF.Identity,
                              bias=0.0, scale=gnw_sb[:, 0:1]).then_inc(s_ha, 1)

    def gp_fn(gp):
        for n in range(NT):
            if not is_main(n):
                continue
            for h in range(32):
                gp.wait_ge(s_ytr, ytr_base[n] + 16 * (h + 1))
                gp.wait_ge(s_ha, ha_base[n] + h * HA_M + 5)
                if h >= 2:
                    gp.wait_ge(s_ha, ha_base[n] + (h - 2) * HA_M + 6)
                gp.tensor_mul(tmul[:, (h % 2) * 128:(h % 2) * 128 + 128],
                              ynf[:, (h % 2) * 128:(h % 2) * 128 + 128],
                              sgt[:, (h % 2) * 128:(h % 2) * 128 + 128]).then_inc(s_hg, 1)

    def pe_fn(t):
        t.wait_ge(s_dta, 3)
        t.wait_ge(s_dtv, 4)
        for n in range(NT):
            m = is_main(n)
            par = n % 2
            nl = 5 if m else 2
            t.wait_ge(s_ld[par], ld_base[par][n + 1])
            if n >= 1:
                t.wait_ge(s_hv, hv_base[n])
                if is_main(n - 1):
                    t.wait_ge(s_ha, ha_base[n])
            t.transpose(pb[0][:, 0:32], spb[0:32, n * 128:(n + 1) * 128], ident_sb[0:32, 0:32]).then_inc(s_clp, 1)
            t.transpose(pb[1][:, 0:32], gall[0:32, n * 128:(n + 1) * 128], ident_sb[0:32, 0:32]).then_inc(s_clp, 1)
            t.wait_ge(s_clv, NCLV * n + 2)
            t.matmul(pb[2][:, 0:32], u_sb[:], gt[:], start=True, stop=True).then_inc(s_clp, 1)
            t.wait_ge(s_clv, NCLV * n + 1)
            t.matmul(pb[0][0:1, 0:32], ones1_sb[:], gt[:], start=True, stop=True).then_inc(s_clp, 1)
            t.wait_ge(s_cla, NCLA * n + 2)
            t.matmul(pb[1][:, 0:32], onesr_sb[:], egrow[0:1, :], start=True, stop=True).then_inc(s_clp, 1)
            t.wait_ge(s_clv, NCLV * n + 3)
            t.matmul(pb[2][:, 0:32], onesr_sb[:], glrow[0:1, :], start=True, stop=True).then_inc(s_clp, 1)
            t.wait_ge(s_cla, NCLA * n + 1)
            t.transpose(pb[0][0:32, 0:128], GT[:], ident_sb[:]).then_inc(s_clp, 1)
            if m:
                for h in range(32):
                    hb = hp_base[n] + h * HP_M
                    if h >= 1:
                        t.wait_ge(s_hv, hv_base[n] + (h - 1) * HV_M + 1)
                    t.wait_ge(s_clv, NCLV * n + 6)
                    t.matmul(pb[3][:, 0:128], rsel_sb[0:32, h * 128:(h + 1) * 128], gth[:],
                             start=True, stop=True).then_inc(s_hp, 1)
                    if h >= 2:
                        t.wait_ge(s_hv, hv_base[n] + (h - 2) * HV_M + 2)
                    t.matmul(pb[h % 2][:, 0:128], kfsl(n)[:, (h // 4) * 128:(h // 4 + 1) * 128],
                             qsl(n)[:, h * 128:(h + 1) * 128], start=True, stop=True).then_inc(s_hp, 1)
                    if h >= 2:
                        t.wait_ge(s_hv, hv_base[n] + (h - 2) * HV_M + 3)
                    t.matmul(pb[4 + (h % 2) * 2][:, 0:128], qsl(n)[:, h * 128:(h + 1) * 128],
                             Sb[:, h * 128:(h + 1) * 128], start=True, stop=True).then_inc(s_hp, 1)
                    t.wait_ge(s_hv, hv_base[n] + h * HV_M + 2)
                    t.wait_ge(s_xk, xk_base[n] + h + 1)
                    if h >= 2:
                        t.wait_ge(s_hv, hv_base[n] + (h - 2) * HV_M + 4)
                    t.matmul(pb[5 + (h % 2) * 2][:, 0:128], pbf[:, (h % 2) * 128:(h % 2) * 128 + 128],
                             xc_all[:, h * 128:(h + 1) * 128], start=True, stop=True).then_inc(s_hp, 1)
                    if h >= 1:
                        t.wait_ge(s_hv, hv_base[n] + (h - 1) * HV_M + 5)
                    t.matmul(pb[2][:, 0:128], kd_all[:, h * 128:(h + 1) * 128],
                             xc_all[:, h * 128:(h + 1) * 128], start=True, stop=True).then_inc(s_hp, 1)
            else:
                for h in range(32):
                    t.wait_ge(s_xk, xk_base[n] + h + 1)
                    if h >= 2:
                        t.wait_ge(s_hv, hv_base[n] + (h - 2) * HV_P + 1)
                    t.matmul(pb[2 + h % 2][:, 0:128], kd_all[:, h * 128:(h + 1) * 128],
                             xc_all[:, h * 128:(h + 1) * 128], start=True, stop=True).then_inc(s_hp, 1)

    p.phase("att", sync=sync_fn, vector=v_fn, scalar=sc_fn, tensor=pe_fn, gpsimd=gp_fn)
    return dict(frees=frees)



def wcomb_phase(p, TM, wo_r, ffn_s, y_s, x_main, out_d, R):
    ST = TM // 512
    pb = R["pb"]
    s_y = p.sem("wc_y")
    s_w = [p.sem("wc_wa"), p.sem("wc_wb")]
    s_pe = p.sem("wc_pe")
    s_ev = p.sem("wc_ev")
    s_ftr = [p.sem("wc_ftra"), p.sem("wc_ftrb")]
    s_fl = p.sem("wc_fl")
    s_xl = s_fl
    s_ot = p.sem("wc_ot")
    s_os = [p.sem("wc_osa"), p.sem("wc_osb")]

    ybuf, cy = p.sb("wc_y", [128, 32 * TM], BF16)
    wob, cw = p.sb("wc_wob", [128, 2 * 4096], BF16)
    fbuf, cf = p.sb("wc_f", [128, 2 * 512], BF16)
    xbuf, cx = p.sb("wc_x", [128, 2 * 512], F32)
    sumb, cs = p.sb("wc_sum", [128, 2 * 512], BF16)
    sumT, ct_ = p.sb("wc_sumT", [128, 2 * 512], BF16)
    outb, co = p.sb("wc_out", [128, 2 * 512], F32)
    frees = [cy, cw, cf, cx, cs, ct_, co]

    items = [(fo, st) for fo in range(32) for st in range(ST)]

    def it(fo, st):
        return fo * ST + st

    def sync_fn(q):
        for dq in range(32):
            q.dma_start(out=ybuf[:, dq * TM:(dq + 1) * TM], in_=y_s[dq]).then_inc(s_y, 16)
        for fo, st in items:
            i = it(fo, st)
            sl = i % 2
            if i >= 2:
                q.wait_ge(s_ev, i - 1)
            q.dma_start(out=fbuf[:, sl * 512:(sl + 1) * 512],
                        in_=ffn_s[fo][:, st * 512:(st + 1) * 512]).then_inc(s_fl, 16)
            if i >= 2:
                q.wait_ge(s_ot, i - 1)
            xs = x_main.rearrange("(s u t) f -> s u t f", u=4, t=128)[st, :, :, :] \
                [:, :, fo * 128:(fo + 1) * 128]
            q.dma_start(out=xbuf[:, sl * 512:(sl + 1) * 512],
                        in_=xs.transpose([1, 0, 2])).then_inc(s_xl, 16)
            q.wait_ge(s_ev, i + 1)
            for u in range(4):
                q.dma_start(out=sumT[:, sl * 512 + u * 128:sl * 512 + (u + 1) * 128],
                            in_=sumb[:, sl * 512 + u * 128:sl * 512 + (u + 1) * 128],
                            transpose=True).then_inc(s_ftr[i % 2], 16)
            q.wait_ge(s_ot, i + 1)
            for u in range(4):
                q.dma_start(out=out_d[st * 512 + u * 128:st * 512 + (u + 1) * 128,
                                      fo * 128:(fo + 1) * 128],
                            in_=outb[:, sl * 512 + u * 128:sl * 512 + (u + 1) * 128]).then_inc(s_os[i % 2], 16)
        ni = len(items)
        q.wait_ge(s_os[0], 16 * 4 * ((ni + 1) // 2))
        if ni >= 2:
            q.wait_ge(s_os[1], 16 * 4 * (ni // 2))

    def gp_fn(gp):
        for fo in range(32):
            if fo >= 2:
                gp.wait_ge(s_pe, ST * (fo - 1))
            gp.dma_start(out=wob[:, (fo % 2) * 4096:(fo % 2) * 4096 + 4096],
                         in_=wo_r[fo].rearrange("p hc c -> p (hc c)")).then_inc(s_w[fo % 2], 16)

    def pe_fn(t):
        t.wait_ge(s_y, 16 * 32)
        for fo in range(32):
            t.wait_ge(s_w[fo % 2], 16 * (fo // 2 + 1))
            if fo >= 2:
                t.wait_ge(s_ev, ST * (fo - 1))
            for st in range(ST):
                bank = pb[(fo % 2) * 4 + st]
                for dq in range(32):
                    inst = t.matmul(bank[:, :],
                                    wob[:, (fo % 2) * 4096 + dq * 128:(fo % 2) * 4096 + (dq + 1) * 128],
                                    ybuf[:, dq * TM + st * 512:dq * TM + (st + 1) * 512],
                                    start=(dq == 0), stop=(dq == 31))
                inst.then_inc(s_pe, 1)

    def v_fn(v):
        for fo, st in items:
            i = it(fo, st)
            sl = i % 2
            v.wait_ge(s_pe, i + 1)
            v.wait_ge(s_fl, 32 * (i + 1))
            if i >= 2:
                v.wait_ge(s_ftr[i % 2], 16 * 4 * (i // 2))
            v.tensor_add(sumb[:, sl * 512:(sl + 1) * 512],
                         pb[(fo % 2) * 4 + st][:, :],
                         fbuf[:, sl * 512:(sl + 1) * 512]).then_inc(s_ev, 1)
            v.wait_ge(s_ftr[i % 2], 16 * 4 * (i // 2 + 1))
            v.wait_ge(s_xl, 32 * (i + 1))
            if i >= 2:
                zz = i - 2
                v.wait_ge(s_os[zz % 2], 16 * 4 * (zz // 2 + 1))
            v.tensor_add(outb[:, sl * 512:(sl + 1) * 512],
                         sumT[:, sl * 512:(sl + 1) * 512],
                         xbuf[:, sl * 512:(sl + 1) * 512]).then_inc(s_ot, 1)

    p.phase("wcomb", sync=sync_fn, gpsimd=gp_fn, tensor=pe_fn, vector=v_fn)
    return dict(frees=frees)



def build(n_main=16, n_prefix=16, debug=False):
    p = P(n_main, n_prefix, debug)
    TM, TP = p.TM, p.TP
    NM, NP = n_main, n_prefix
    NT = NM + NP

    x_main = p.din("x_main", [TM, HID], F32)
    x_prefix = p.din("x_prefix", [TP, HID], F32) if NP else None
    wq_r = p.din("wq_r", [32, 128, 32, 128], F32)
    wk_r = p.din("wk_r", [8, 128, 32, 128], F32)
    wv_r = p.din("wv_r", [8, 128, 32, 128], F32)
    wg_r = p.din("wg_r", [32, 128, 32, 128], F32)
    wo_r = p.din("wo_r", [32, 128, 32, 128], F32)
    win_r = p.din("win_r", [1, 128, 32, 32], F32)
    dt_bias_col = p.din("dt_bias_col", [32, 1], F32)
    albias_col = p.din("albias_col", [32, 1], F32)
    gnw_col = p.din("gnw_col", [128, 1], F32)
    ropec_m = p.din("ropec_m", [128, TM], BF16)
    ropes_m = p.din("ropes_m", [128, TM], BF16)
    ropec_p = p.din("ropec_p", [128, TP], BF16) if NP else None
    ropes_p = p.din("ropes_p", [128, TP], BF16) if NP else None
    c_ident = p.din("c_ident", [128, 128], F32)
    c_ones1 = p.din("c_ones1", [128, 1], F32)
    c_onesr = p.din("c_onesr", [1, 128], F32)
    c_u = p.din("c_u", [128, 128], F32)
    c_mneg = p.din("c_mneg", [128, 128], F32)
    c_eps = p.din("c_eps", [128, 2], F32)
    c_rsel = p.din("c_rsel", [32, 32 * 128], F32)
    qguT = p.din("qguT", [GU_CT, 128, HID], I32)
    sgu_cols = p.din("sgu_cols", [128, GU_CT * NPLANE * 32], F32)
    zguT = p.din("zguT", [GU_CT, 128, 32], I32)
    qdnT = p.din("qdnT", [DN_CT, 128, HB * 128], I32)
    sdn_cols = p.din("sdn_cols", [128, DN_CT * NPLANE * HB], F32)
    zdnT = p.din("zdnT", [512, HB], I32)

    out_d = p.dout("out", [TM, HID], F32)

    q_s = p.dtmp("q_s", [NM, 128, 4096], BF16)
    g_s = p.dtmp("g_s", [NM, 128, 4096], BF16)
    kfm_s = p.dtmp("kfm_s", [NM, 128, 1024], BF16)
    ktm_s = p.dtmp("ktm_s", [NT, 128, 1024], BF16)
    vtm_s = p.dtmp("vtm_s", [NT, 128, 1024], BF16)
    dtraw_s = p.dtmp("dtraw_s", [32, NT * 128], F32)
    h_s = p.dtmp("h_s", [HB, 128, TM], BF16)
    ffn_s = p.dtmp("ffn_s", [32, 128, TM], BF16)
    y_s = p.dtmp("y_s", [32, 128, TM], BF16)
    xn_dbg = p.dtmp("xn_dbg", [128, 32 * TM], BF16) if debug else None

    ident_sb, _ = p.sb("ident_sb", [128, 128], F32)
    ones1_sb, _ = p.sb("ones1_sb", [128, 1], F32)
    onesr_sb, _ = p.sb("onesr_sb", [1, 128], F32)
    u_sb, _ = p.sb("u_sb", [128, 128], F32)
    mneg_sb, _ = p.sb("mneg_sb", [128, 128], F32)
    dtb_sb, _ = p.sb("dtb_sb", [32, 1], F32)
    alb_sb, _ = p.sb("alb_sb", [32, 1], F32)
    acol_sb, _ = p.sb("acol_sb", [32, 1], F32)
    gnw_sb, _ = p.sb("gnw_sb", [128, 1], F32)
    eps_sb, _ = p.sb("eps_sb", [128, 2], F32)

    pb = [p.ctx.enter_context(p.nc.psum_tensor(f"pb{i}", [128, 512], F32)) for i in range(8)]
    R = dict(pb=pb, dtb_sb=dtb_sb, c_rsel=c_rsel, ident_sb=ident_sb, ones1_sb=ones1_sb,
             onesr_sb=onesr_sb, u_sb=u_sb, mneg_sb=mneg_sb, acol_sb=acol_sb,
             gnw_sb=gnw_sb, eps_sb=eps_sb)

    s_ic = p.sem("s_ic")
    s_ac = p.sem("s_ac")

    def init_sync(q):
        for t, d in [(ident_sb, c_ident), (ones1_sb, c_ones1), (onesr_sb, c_onesr),
                     (u_sb, c_u), (mneg_sb, c_mneg), (dtb_sb, dt_bias_col),
                     (alb_sb, albias_col), (gnw_sb, gnw_col), (eps_sb, c_eps)]:
            q.dma_start(out=t[:], in_=d).then_inc(s_ic, 16)

    def init_scalar(sc):
        sc.wait_ge(s_ic, 16 * 9)
        sc.activation(acol_sb[:], alb_sb[:], AF.Exp, bias=0.0, scale=1.0).then_inc(s_ac, 1)

    def init_vector(v):
        v.wait_ge(s_ac, 1)
        v.tensor_scalar_mul(acol_sb[:], acol_sb[:], -1.0)

    p.phase("init", sync=init_sync, scalar=init_scalar, vector=init_vector)
    p.barrier("init")

    xn_fm, cm_xnfm = p.sb("xn_fm", [128, 32 * max(TM, TP if NP else TM)], BF16)
    dt_raw, cm_dtraw = p.sb("dt_raw", [32, NT * 128], F32)
    R["dt_raw"] = dt_raw

    if NP:
        ln0 = ln_phase(p, "p0a", NP, x_prefix, xn_fm, TP, eps_sb)
        p.barrier("p0a")
        for cm in reversed(ln0["frees"]):
            p.free(cm)
        groups0 = [
            dict(kind="k", w=wk_r, n_fo=8, fo_cols=128, dst=kfm_s, dst2=ktm_s, tc0=0, ktc0=0),
            dict(kind="v", w=wv_r, n_fo=8, fo_cols=128, dst2=vtm_s, ktc0=0),
            dict(kind="win", w=win_r, n_fo=1, fo_cols=32, dt_off=0),
        ]
        groups0[0]["dst"] = None
        pr0 = proj_phase(p, "p0b", NP, xn_fm, TP, groups0, ln0, ropec_p, ropes_p, R)
        p.barrier("p0")
        for cm in reversed(pr0["frees"]):
            p.free(cm)

    ln1 = ln_phase(p, "p1a", NM, x_main, xn_fm, TM, eps_sb)
    p.barrier("p1a")
    for cm in reversed(ln1["frees"]):
        p.free(cm)
    groups1 = [
        dict(kind="q", w=wq_r, n_fo=32, fo_cols=128, dst=q_s, tc0=0),
        dict(kind="k", w=wk_r, n_fo=8, fo_cols=128, dst=kfm_s, dst2=ktm_s, tc0=0, ktc0=NP),
        dict(kind="v", w=wv_r, n_fo=8, fo_cols=128, dst2=vtm_s, ktc0=NP),
        dict(kind="g", w=wg_r, n_fo=32, fo_cols=128, dst=g_s, tc0=0),
        dict(kind="win", w=win_r, n_fo=1, fo_cols=32, dt_off=TP),
    ]
    pr1 = proj_phase(p, "p1b", NM, xn_fm, TM, groups1, ln1, ropec_m, ropes_m, R)

    s_dtsp = p.sem("s_dtsp")

    def dtspill_sync(q):
        q.wait_ge(pr1["s_ev"], pr1["n_ev"])
        if NP:
            q.wait_ge(pr0["s_ev"], pr0["n_ev"])
        q.dma_start(out=dtraw_s, in_=dt_raw[:]).then_inc(s_dtsp, 16)
        q.wait_ge(s_dtsp, 16)
        if p.debug:
            pass

    p.phase("dtspill", sync=dtspill_sync)
    p.barrier("p1")
    for cm in reversed(pr1["frees"]):
        p.free(cm)
    p.free(cm_dtraw)

    if debug:
        s_dbg = p.sem("s_xdbg")

        def dbg_sync(q):
            q.dma_start(out=xn_dbg, in_=xn_fm[:, 0:32 * TM]).then_inc(s_dbg, 16)
            q.wait_ge(s_dbg, 16)
        p.phase("xdbg", sync=dbg_sync)
        p.barrier("xdbg")

    fg = fgu_phase(p, xn_fm, TM, qguT, sgu_cols, zguT, h_s, R)
    p.barrier("fgu")
    for cm in reversed(fg["frees"]):
        p.free(cm)
    p.free(cm_xnfm)

    fd = fdn_phase(p, TM, qdnT, sdn_cols, zdnT, h_s, ffn_s, R)
    p.barrier("fdn")
    for cm in reversed(fd["frees"]):
        p.free(cm)

    at = att_phase(p, NM, NP, q_s, g_s, kfm_s, ktm_s, vtm_s, dtraw_s, y_s, R)
    p.barrier("att")
    for cm in reversed(at["frees"]):
        p.free(cm)

    wc = wcomb_phase(p, TM, wo_r, ffn_s, y_s, x_main, out_d, R)
    p.barrier("wcomb")

    p.emit()
    return p


def fgu_phase(p, xn_buf, Ttok, qguT, sgu_cols, zguT, h_s, R):
    STn = Ttok // 512
    pb = R["pb"]
    s_q = p.sem("fgu_q")
    s_zs = p.sem("fgu_zs")
    s_nib = p.sem("fgu_nib")
    s_deq = p.sem("fgu_deq")
    s_wtr = [p.sem(f"fgu_wtr{i}") for i in range(4)]
    s_peh = p.sem("fgu_peh")
    s_sil = p.sem("fgu_sil")
    s_ue = p.sem("fgu_ue")
    s_h = p.sem("fgu_h")
    s_sph = p.sem("fgu_sph")

    qg, cqg = p.sb("fgu_qg", [128, HID], I32)
    qu, cqu = p.sb("fgu_qu", [128, HID], I32)
    scg, csc1 = p.sb("fgu_scg", [128, 256], F32)
    scu, csc2 = p.sb("fgu_scu", [128, 256], F32)
    zg, cz1 = p.sb("fgu_zg", [128, 32], I32)
    zu, cz2 = p.sb("fgu_zu", [128, 32], I32)
    zsg, czs1 = p.sb("fgu_zsg", [128, 256], F32)
    zsu, czs2 = p.sb("fgu_zsu", [128, 256], F32)
    nibz, cnz = p.sb("fgu_nibz", [128, 256], I32)
    nib, cnib = p.sb("fgu_nib", [128, 2 * 2 * 512], I32)
    wtT, cwt = p.sb("fgu_wtT", [128, 2 * 2 * 512], BF16)
    wk, cwk = p.sb("fgu_wk", [128, 2 * 8 * 128], BF16)
    sg, csg = p.sb("fgu_sg", [128, Ttok], BF16)
    ub, cub = p.sb("fgu_ub", [128, Ttok], BF16)
    hst, chst = p.sb("fgu_hst", [128, Ttok], BF16)
    frees = [cqg, cqu, csc1, csc2, cz1, cz2, czs1, czs2, cnz, cnib, cwt, cwk,
             csg, cub, chst]

    NPAIR = GU_PAIR = 11

    def peh_idx(pair, sp, side, hc):
        return ((pair * 8 + sp) * 2 + side) * 32 + hc + 1

    def ks_idx(pair, sp, side, ks):
        return ((pair * 8 + sp) * 2 + side) * 8 + ks + 1

    def zs_idx(pair, side, sp):
        return pair * 16 + side * 8 + sp + 1

    def nibsl(side, slot):
        return nib[:, (side * 2 + slot) * 512:(side * 2 + slot + 1) * 512]

    def wtsl(side, slot):
        return wtT[:, (side * 2 + slot) * 512:(side * 2 + slot + 1) * 512]

    def wksl(side, slot):
        return wk[:, (side * 8 + slot) * 128:(side * 8 + slot + 1) * 128]

    def sync_fn(q):
        for pair in range(NPAIR):
            if pair >= 1:
                q.wait_ge(s_peh, peh_idx(pair - 1, 7, 1, 31))
            for dst, src in [(qg[:], qguT[pair]), (qu[:], qguT[pair + 11]),
                             (scg[:], sgu_cols[:, pair * 256:(pair + 1) * 256]),
                             (scu[:], sgu_cols[:, (pair + 11) * 256:(pair + 12) * 256]),
                             (zg[:], zguT[pair]), (zu[:], zguT[pair + 11])]:
                q.dma_start(out=dst, in_=src).then_inc(s_q, 16)
            for sp in range(8):
                for side in range(2):
                    for ks in range(8):
                        zks = ks_idx(pair, sp, side, ks)
                        zz = (pair * 8 + sp) * 8 + ks
                        q.wait_ge(s_deq, zks)
                        for j in range(4):
                            hc = ks * 4 + j
                            if peh_idx(pair, sp, side, hc) - 8 >= 1:
                                q.wait_ge(s_peh, peh_idx(pair, sp, side, hc) - 8)
                            q.dma_start(out=wksl(side, hc % 8),
                                        in_=wtsl(side, ks % 2)[:, j * 128:(j + 1) * 128],
                                        transpose=True).then_inc(s_wtr[side * 2 + zz % 2], 16)
                gsp = pair * 8 + sp
                q.wait_ge(s_h, gsp + 1)
                q.dma_start(out=h_s[gsp], in_=hst[:, 0:Ttok]).then_inc(s_sph, 16)
        q.wait_ge(s_sph, 16 * NPAIR * 8)
        tot = NPAIR * 8 * 8
        for side in range(2):
            for par in range(2):
                n = len([z for z in range(tot) if z % 2 == par])
                q.wait_ge(s_wtr[side * 2 + par], 16 * 4 * n)

    def v_fn(v):
        for pair in range(NPAIR):
            v.wait_ge(s_q, 16 * 6 * (pair + 1))
            if pair >= 1:
                v.wait_ge(s_deq, ks_idx(pair - 1, 7, 1, 7))
            for side in range(2):
                zt = zg if side == 0 else zu
                sct = scg if side == 0 else scu
                zst = zsg if side == 0 else zsu
                base = (pair * 2 + side) * 2
                if base >= 1:
                    v.wait_ge(s_zs, base)
                for sp in range(8):
                    inst = v.tensor_scalar(nibz[:, sp * 32:(sp + 1) * 32], zt[:], 4 * sp, 15,
                                           ALU.logical_shift_right, ALU.bitwise_and)
                inst.then_inc(s_zs, 1)
                v.wait_ge(s_zs, base + 1)
                for sp in range(8):
                    inst = v.scalar_tensor_tensor(zst[:, sp * 32:(sp + 1) * 32],
                                                  nibz[:, sp * 32:(sp + 1) * 32], -1.0,
                                                  sct[:, sp * 32:(sp + 1) * 32],
                                                  ALU.mult, ALU.mult)
                inst.then_inc(s_zs, 1)
            for sp in range(8):
                for side in range(2):
                    qt = qg if side == 0 else qu
                    for ks in range(8):
                        gks = ks_idx(pair, sp, side, ks)
                        if gks >= 3:
                            v.wait_ge(s_deq, gks - 2)
                        v.tensor_scalar(nibsl(side, ks % 2), qt[:, ks * 512:(ks + 1) * 512],
                                        4 * sp, 15, ALU.logical_shift_right,
                                        ALU.bitwise_and).then_inc(s_nib, 1)
                gsp = pair * 8 + sp
                v.wait_ge(s_peh, peh_idx(pair, sp, 1, 31))
                if gsp >= 1:
                    v.wait_ge(s_h, gsp)
                for st in range(STn):
                    v.tensor_copy(ub[:, st * 512:(st + 1) * 512], pb[4 + st][:, :]).then_inc(s_ue, 1)

    def sc_fn(sc):
        for pair in range(NPAIR):
            for sp in range(8):
                for side in range(2):
                    zst = zsg if side == 0 else zsu
                    sct = scg if side == 0 else scu
                    for ks in range(8):
                        gks = ks_idx(pair, sp, side, ks)
                        zz = (pair * 8 + sp) * 8 + ks
                        sc.wait_ge(s_nib, gks)
                        sc.wait_ge(s_zs, (pair * 2 + side) * 2 + 2)
                        if zz >= 2:
                            zp = zz - 2
                            cls = (zp % 2) if side == 0 else (2 + zp % 2)
                            sc.wait_ge(s_wtr[cls], 16 * 4 * (zp // 2 + 1))
                        for j in range(4):
                            g = ks * 4 + j
                            inst = sc.activation(wtsl(side, ks % 2)[:, j * 128:(j + 1) * 128],
                                                 nibsl(side, ks % 2)[:, j * 128:(j + 1) * 128],
                                                 AF.Identity,
                                                 bias=zst[:, sp * 32 + g:sp * 32 + g + 1],
                                                 scale=sct[:, sp * 32 + g:sp * 32 + g + 1])
                        inst.then_inc(s_deq, 1)
                gsp = pair * 8 + sp
                sc.wait_ge(s_peh, peh_idx(pair, sp, 0, 31))
                if gsp >= 1:
                    sc.wait_ge(s_h, gsp)
                for st in range(STn):
                    sc.activation(sg[:, st * 512:(st + 1) * 512], pb[st][:, :],
                                  AF.Silu, bias=0.0, scale=1.0).then_inc(s_sil, 1)


    def pe_fn(t):
        for pair in range(NPAIR):
            for sp in range(8):
                gsp = pair * 8 + sp
                for side in range(2):
                    if gsp >= 1:
                        if side == 0:
                            t.wait_ge(s_sil, STn * gsp)
                        else:
                            t.wait_ge(s_ue, STn * gsp)
                    for hc in range(32):
                        zz = (pair * 8 + sp) * 8 + hc // 4
                        cls = (zz % 2) if side == 0 else (2 + zz % 2)
                        t.wait_ge(s_wtr[cls], 16 * 4 * (zz // 2 + 1))
                        for st in range(STn):
                            inst = t.matmul(pb[side * 4 + st][:, :],
                                            wksl(side, hc % 8),
                                            xn_buf[:, hc * Ttok + st * 512:hc * Ttok + (st + 1) * 512],
                                            start=(hc == 0), stop=(hc == 31))
                        inst.then_inc(s_peh, 1)

    def gp_fn(gp):
        for pair in range(NPAIR):
            for sp in range(8):
                gsp = pair * 8 + sp
                gp.wait_ge(s_sil, STn * (gsp + 1))
                gp.wait_ge(s_ue, STn * (gsp + 1))
                if gsp >= 1:
                    gp.wait_ge(s_sph, 16 * gsp)
                gp.tensor_mul(hst[:, 0:Ttok], sg[:, 0:Ttok], ub[:, 0:Ttok]).then_inc(s_h, 1)

    p.phase("fgu", sync=sync_fn, vector=v_fn, scalar=sc_fn, tensor=pe_fn, gpsimd=gp_fn)
    return dict(frees=frees)



def kernel(**inputs):
    n_main, n_prefix = 16, 16
    in_maps = host_prep(inputs, n_main, n_prefix)
    p = build(n_main, n_prefix, debug=False)
    res = run_bass_kernel_spmd(p.nc, in_maps, core_ids=list(range(8)))
    out = np.empty((4, 4096, 4096), np.float32)
    for core in range(8):
        b, hf = core // 2, core % 2
        out[b, 2048 * hf:2048 * (hf + 1)] = res.results[core]["out"]
    return out


# version 08
# speedup vs baseline: 1.2672x; 1.2672x over previous
import numpy as np
import ml_dtypes
from contextlib import ExitStack

import concourse.bass as bass
import concourse.mybir as mybir
from concourse.bass_utils import run_bass_kernel_spmd

F32 = mybir.dt.float32
BF16 = mybir.dt.bfloat16
I32 = mybir.dt.int32
AF = mybir.ActivationFunctionType
ALU = mybir.AluOpType
AX = mybir.AxisListType

H, KVH, D, CHUNK = 32, 8, 128, 128
HID = 4096
L = 4096
ROPE_THETA = 10000.0
RMS_EPS = 1e-5
LN_EPS = 1e-5
NEG = -1e30

GU_CT = 22
NPLANE = 8
DN_CT = 4
HB = 88

BFnp = ml_dtypes.bfloat16



def _rope_tables_fm():
    inv_freq = (1.0 / ROPE_THETA ** (np.arange(0, D, 2, dtype=np.float32) / D)).astype(np.float32)
    ang = np.arange(L, dtype=np.float32)[:, None] * inv_freq[None, :]
    cos = np.concatenate([np.cos(ang), np.cos(ang)], axis=-1)
    sin = np.concatenate([np.sin(ang), np.sin(ang)], axis=-1)
    sinm = sin.copy()
    sinm[:, :64] *= -1.0
    return (np.ascontiguousarray(cos.T).astype(BFnp),
            np.ascontiguousarray(sinm.T).astype(BFnp))


def _wblocks(W):
    FO, K = W.shape
    return np.ascontiguousarray(W.reshape(FO // 128, 128, K // 128, 128).transpose(0, 3, 2, 1)).astype(np.float32)


def _gu_perm_c():
    perm = np.full(GU_CT * 128, -1, np.int64)
    perm[0:1376] = np.arange(0, 1376)
    perm[1408:1408 + 1376] = np.arange(1376, 2752)
    return perm


def _dn_perm_i():
    pos = np.arange(HB * 128)
    ct, sp, c = pos // 1024, (pos // 128) % 8, pos % 128
    corig = 128 * ct + c
    return np.where(corig < 1376, 8 * corig + sp, -1)


def host_prep(inputs, n_main=16, n_prefix=16):
    x = np.asarray(inputs["L0_attn_hidden_states"])
    Wq = np.asarray(inputs["L0_attn_Wq"]); Wk = np.asarray(inputs["L0_attn_Wk"])
    Wv = np.asarray(inputs["L0_attn_Wv"]); Wo = np.asarray(inputs["L0_attn_Wo"])
    Wg = np.asarray(inputs["L0_attn_Wg"]); Win = np.asarray(inputs["L0_attn_Win"])

    sh = {}
    sh["wq_r"] = _wblocks(Wq)
    sh["wk_r"] = _wblocks(Wk)
    sh["wv_r"] = _wblocks(Wv)
    sh["wg_r"] = _wblocks(Wg)
    sh["wo_r"] = _wblocks(Wo.T)
    sh["win_r"] = np.ascontiguousarray(Win.reshape(32, 32, 128).transpose(2, 1, 0)).astype(np.float32)[None]
    sh["dt_bias_col"] = np.asarray(inputs["L0_attn_dt_bias"]).reshape(32, 1).astype(np.float32)
    sh["albias_col"] = np.asarray(inputs["L0_attn_A_log_bias"]).reshape(32, 1).astype(np.float32)
    sh["gnw_col"] = np.asarray(inputs["L0_attn_gnorm_w"]).reshape(128, 1).astype(np.float32)
    sh["c_ident"] = np.eye(128, dtype=np.float32)
    sh["c_identb"] = np.eye(128, dtype=np.float32).astype(BFnp)
    sh["c_ones1"] = np.ones((128, 1), np.float32)
    sh["c_onesr"] = np.ones((1, 128), np.float32)
    ij = np.arange(128)
    sh["c_u"] = (ij[:, None] <= ij[None, :]).astype(np.float32)
    sh["c_mneg"] = np.where(ij[:, None] <= ij[None, :], 0.0, NEG).astype(np.float32)
    sh["c_eps"] = np.full((128, 2), 1.0, np.float32)
    sh["c_eps"][:, 0] = LN_EPS
    rsel = np.zeros((32, 32 * 128), np.float32)
    for k in range(32):
        rsel[k, 128 * k:128 * (k + 1)] = 1.0
    sh["c_rsel"] = rsel

    qgu = np.asarray(inputs["L0_ffn_qweight_gu"])
    sgu = np.asarray(inputs["L0_ffn_scales_gu"], np.float32)
    zgu = np.asarray(inputs["L0_ffn_qzeros_gu"])
    permc = _gu_perm_c()
    valid = permc >= 0
    qguT = np.zeros((GU_CT * 128, HID), np.int32)
    qguT[valid] = np.ascontiguousarray(qgu.T)[permc[valid]]
    sh["qguT"] = qguT.reshape(GU_CT, 128, HID)
    scol = np.zeros((128, GU_CT, NPLANE, 32), np.float32)
    pm = permc.reshape(GU_CT, 128)
    for ct in range(GU_CT):
        v = pm[ct] >= 0
        n = 8 * pm[ct][v][:, None] + np.arange(8)[None, :]
        scol[v, ct, :, :] = sgu.T[n]
    sh["sgu_cols"] = np.ascontiguousarray(scol.reshape(128, GU_CT * NPLANE * 32))
    zguT = np.zeros((GU_CT * 128, 32), np.int32)
    zguT[valid] = np.ascontiguousarray(zgu.T)[permc[valid]]
    sh["zguT"] = zguT.reshape(GU_CT, 128, 32)

    qdn = np.asarray(inputs["L0_ffn_qweight_dn"])
    sdn = np.asarray(inputs["L0_ffn_scales_dn"], np.float32)
    zdn = np.asarray(inputs["L0_ffn_qzeros_dn"])
    permi = _dn_perm_i()
    vi = permi >= 0
    qdnp = np.zeros((HB * 128, 512), np.int32)
    qdnp[vi] = qdn[permi[vi]]
    sh["qdnT"] = np.ascontiguousarray(qdnp.T).reshape(DN_CT, 128, HB * 128)
    sdnp = np.zeros((HB, HID), np.float32)
    sdnp[:86] = sdn
    dcol = np.zeros((128, DN_CT, NPLANE, HB), np.float32)
    for ct in range(DN_CT):
        n = 8 * (128 * ct + np.arange(128))[:, None] + np.arange(8)[None, :]
        dcol[:, ct, :, :] = sdnp.T[n]
    sh["sdn_cols"] = np.ascontiguousarray(dcol.reshape(128, DN_CT * NPLANE * HB))
    zdnp = np.zeros((HB, 512), np.int32)
    zdnp[:86] = zdn
    sh["zdnT"] = np.ascontiguousarray(zdnp.T)

    cos, sinm = _rope_tables_fm()
    TM, TP = n_main * CHUNK, n_prefix * CHUNK
    in_maps = []
    for core in range(8):
        b, hf = core // 2, core % 2
        off = 2048 * hf
        m = dict(sh)
        m["x_main"] = np.ascontiguousarray(x[b, off:off + TM]).astype(np.float32)
        m["ropec_m"] = np.ascontiguousarray(cos[:, off:off + TM])
        m["ropes_m"] = np.ascontiguousarray(sinm[:, off:off + TM])
        if TP > 0:
            poff = off - TP if hf == 1 else 0
            if hf == 1:
                m["x_prefix"] = np.ascontiguousarray(x[b, poff:off]).astype(np.float32)
            else:
                m["x_prefix"] = np.zeros((TP, HID), np.float32)
            m["ropec_p"] = np.ascontiguousarray(cos[:, poff:poff + TP])
            m["ropes_p"] = np.ascontiguousarray(sinm[:, poff:poff + TP])
        in_maps.append(m)
    return in_maps



class P:
    def __init__(self, n_main, n_prefix, debug):
        self.nc = bass.Bass("TRN2", target_bir_lowering=False, debug=False)
        self.ctx = ExitStack()
        self.n_main, self.n_prefix, self.debug = n_main, n_prefix, debug
        self.TM, self.TP = n_main * CHUNK, n_prefix * CHUNK
        self.phases = []
        self._frees = []

    def sb(self, name, shape, dtype):
        cm = self.nc.sbuf_tensor(name, shape, dtype)
        t = cm.__enter__()
        self._frees.append(cm)
        return t, cm

    @staticmethod
    def free(cm):
        cm.__exit__(None, None, None)

    def sem(self, name):
        return self.ctx.enter_context(self.nc.semaphore(name))

    def din(self, name, shape, dtype):
        return self.nc.dram_tensor(name, shape, dtype, kind="ExternalInput").ap()

    def dout(self, name, shape, dtype):
        return self.nc.dram_tensor(name, shape, dtype, kind="ExternalOutput").ap()

    def dtmp(self, name, shape, dtype):
        kind = "ExternalOutput" if self.debug else "Internal"
        return self.nc.dram_tensor(name, shape, dtype, kind=kind).ap()

    def phase(self, name, **emits):
        self.phases.append((name, emits))

    def barrier(self, name):
        if not hasattr(self, "_barsems"):
            self._barsems = [self.sem(f"bar{i}") for i in range(5)]
            self._barcnt = 0
        self._barcnt += 1
        k = self._barcnt
        sems = self._barsems

        def mk(i):
            def f(eng):
                eng.sem_inc(sems[i], 1)
                for j in range(5):
                    eng.wait_ge(sems[j], k)
            return f
        self.phase("bar_" + name, sync=mk(0), vector=mk(1), scalar=mk(2),
                   tensor=mk(3), gpsimd=mk(4))

    def emit(self):
        with self.nc.Block() as block:
            @block.sync
            def _(q):
                for _n, em in self.phases:
                    if "sync" in em:
                        em["sync"](q)

            @block.vector
            def _(v):
                for _n, em in self.phases:
                    if "vector" in em:
                        em["vector"](v)

            @block.scalar
            def _(sc):
                for _n, em in self.phases:
                    if "scalar" in em:
                        em["scalar"](sc)

            @block.tensor
            def _(t):
                for _n, em in self.phases:
                    if "tensor" in em:
                        em["tensor"](t)

            @block.gpsimd
            def _(gp):
                for _n, em in self.phases:
                    if "gpsimd" in em:
                        em["gpsimd"](gp)



def ln_phase(p, tag, nchunks, x_dram, xn_buf, Ttok, eps_sb=None, R=None):
    s_x1 = [p.sem(f"{tag}_x1a"), p.sem(f"{tag}_x1b")]
    s_x2 = s_x1
    npar = [(nchunks + 1) // 2, nchunks // 2]
    s_sum = p.sem(f"{tag}_sum")
    s_ssq = p.sem(f"{tag}_ssq")
    s_ch = p.sem(f"{tag}_ch")
    s_ln = p.sem(f"{tag}_ln")
    s_pt = p.sem(f"{tag}_pt")
    s_tr = p.sem(f"{tag}_tr")

    xbuf, cm_xb = p.sb(f"{tag}_xb", [128, 2 * HID], F32)
    sums, cm1 = p.sb(f"{tag}_sums", [128, nchunks], F32)
    ssqs, cm2 = p.sb(f"{tag}_ssqs", [128, nchunks], F32)
    mean, cm3 = p.sb(f"{tag}_mean", [128, nchunks], F32)
    var, cm4 = p.sb(f"{tag}_var", [128, nchunks], F32)
    rstd, cm5 = p.sb(f"{tag}_rstd", [128, nchunks], F32)
    nbias, cm6 = p.sb(f"{tag}_nbias", [128, nchunks], F32)
    xn_tm, cm_xn = p.sb(f"{tag}_xntm", [128, 2 * HID], BF16)
    sqscr = xn_tm
    frees = [cm_xb, cm1, cm2, cm3, cm4, cm5, cm6, cm_xn]

    def xsl(i):
        return xbuf[:, (i % 2) * HID:(i % 2 + 1) * HID]

    def xnsl(i):
        return xn_tm[:, (i % 2) * HID:(i % 2 + 1) * HID]

    def sync_fn(q):
        for tc in range(nchunks):
            if tc >= 2:
                q.wait_ge(s_ssq, tc - 1)
                q.wait_ge(s_sum, tc - 1)
            q.dma_start(out=xsl(tc), in_=x_dram[tc * 128:(tc + 1) * 128, :]).then_inc(s_x1[tc % 2], 16)
        for tc in range(nchunks):
            if tc == 0:
                q.wait_ge(s_sum, nchunks)
                q.wait_ge(s_ssq, nchunks)
            if tc >= 2:
                q.wait_ge(s_ln, tc - 1)
            q.dma_start(out=xsl(tc), in_=x_dram[tc * 128:(tc + 1) * 128, :]).then_inc(s_x2[tc % 2], 16)

    def v_fn(v):
        pbt = [R["pb"][4 + k].ap().bitcast(BF16) for k in range(4)]
        for tc in range(nchunks):
            v.wait_ge(s_x1[tc % 2], 16 * (tc // 2 + 1))
            v.reduce_sum(sums[:, tc:tc + 1], xsl(tc), axis=AX.X).then_inc(s_sum, 1)
        v.wait_ge(s_ssq, nchunks)
        v.wait_ge(s_sum, nchunks)
        v.tensor_scalar_mul(mean[:], sums[:], 1.0 / HID).then_inc(s_ch, 1)
        v.wait_ge(s_ch, 1)
        v.tensor_mul(var[:], mean[:], mean[:]).then_inc(s_ch, 1)
        v.wait_ge(s_ch, 2)
        v.scalar_tensor_tensor(var[:], ssqs[:], 1.0 / HID, var[:], ALU.mult, ALU.subtract).then_inc(s_ch, 1)
        v.wait_ge(s_ch, 5)
        v.scalar_tensor_tensor(nbias[:], mean[:], -1.0, rstd[:], ALU.mult, ALU.mult).then_inc(s_ch, 1)
        for tc in range(nchunks):
            for hc in range(32):
                tt = tc * 32 + hc
                v.wait_ge(s_pt, tt + 1)
                v.tensor_copy(xn_buf[:, hc * Ttok + tc * 128:hc * Ttok + (tc + 1) * 128],
                              pbt[tt % 4][:, 0:128]).then_inc(s_tr, 1)

    def sc_fn(sc):
        for tc in range(nchunks):
            sc.wait_ge(s_x1[tc % 2], 16 * (tc // 2 + 1))
            if tc >= 1:
                sc.wait_ge(s_ssq, tc)
            sc.activation(sqscr[:, 0:HID], xsl(tc), AF.Square, bias=0.0, scale=1.0,
                          accum_out=ssqs[:, tc:tc + 1]).then_inc(s_ssq, 1)
        sc.wait_ge(s_ch, 3)
        sc.activation(nbias[:], var[:], AF.Ln, bias=eps_sb[:, 0:1], scale=1.0).then_inc(s_ch, 1)
        sc.wait_ge(s_ch, 4)
        sc.activation(rstd[:], nbias[:], AF.Exp, bias=0.0, scale=-0.5).then_inc(s_ch, 1)
        sc.wait_ge(s_ch, 6)
        sc.wait_ge(s_ssq, nchunks)
        for tc in range(nchunks):
            sc.wait_ge(s_x2[tc % 2], 16 * (npar[tc % 2] + tc // 2 + 1))
            if tc >= 2:
                sc.wait_ge(s_pt, 32 * (tc - 1))
            sc.activation(xnsl(tc), xsl(tc), AF.Identity,
                          bias=nbias[:, tc:tc + 1], scale=rstd[:, tc:tc + 1]).then_inc(s_ln, 1)

    def pe_fn(t):
        identb = R["identb_sb"]
        for tc in range(nchunks):
            t.wait_ge(s_ln, tc + 1)
            for hc in range(32):
                tt = tc * 32 + hc
                if tt >= 4:
                    t.wait_ge(s_tr, tt - 3)
                t.transpose(pbt_pe[tt % 4][:, 0:128],
                            xnsl(tc)[:, hc * 128:(hc + 1) * 128],
                            identb[:]).then_inc(s_pt, 1)

    pbt_pe = [R["pb"][4 + k].ap().bitcast(BF16) for k in range(4)]
    p.phase(tag, sync=sync_fn, vector=v_fn, scalar=sc_fn, tensor=pe_fn)
    return dict(sem=s_tr, n=32 * nchunks, frees=frees)



def proj_phase(p, tag, nchunks, xn_buf, Ttok, groups, ln_dep, ropec_d, ropes_d, R):
    STn = Ttok // 512
    assert Ttok % 512 == 0 and STn <= 4
    s_w = [p.sem(f"{tag}_wa"), p.sem(f"{tag}_wb2")]
    s_pe = p.sem(f"{tag}_pe")
    s_ev = p.sem(f"{tag}_ev")
    s_rp1 = p.sem(f"{tag}_rp1")
    s_rp = p.sem(f"{tag}_rp")
    s_sp = p.sem(f"{tag}_sp")
    s_ktr = p.sem(f"{tag}_ktr")
    s_rop = p.sem(f"{tag}_rop")

    wbuf, cw = p.sb(f"{tag}_wb", [128, 2 * 32 * 128], BF16)
    ev_scr, ce = p.sb(f"{tag}_ev", [128, 512], F32)
    t1, c1 = p.sb(f"{tag}_t1", [128, 512], F32)
    t2, c2 = p.sb(f"{tag}_t2", [128, 512], F32)
    sw, c2b = p.sb(f"{tag}_sw", [128, 512], F32)
    stage, c3 = p.sb(f"{tag}_stg", [128, Ttok], BF16)
    ktm_stage, c4 = p.sb(f"{tag}_ktms", [128, Ttok], BF16)
    ropec, c5 = p.sb(f"{tag}_ropec", [128, Ttok], BF16)
    ropes, c6 = p.sb(f"{tag}_ropes", [128, Ttok], BF16)
    frees = [cw, ce, c1, c2, c2b, c3, c4, c5, c6]

    pb = R["pb"]

    seq = []
    for g in groups:
        for fo in range(g["n_fo"]):
            seq.append((g, fo))

    ev_cnt, rp_cnt, sp_cnt, ktr_cnt = {}, {}, {}, {}
    c_ev = c_rp = c_sp = c_ktr = 0
    for si, (g, fo) in enumerate(seq):
        for st in range(STn):
            c_ev += 1
            ev_cnt[(si, st)] = c_ev
            if g["kind"] in ("q", "k"):
                c_rp += 1
                rp_cnt[(si, st)] = c_rp
        k = g["kind"]
        if k in ("q", "g"):
            c_sp += 16
        elif k == "k":
            c_sp += 32 if g.get("dst") is not None else 16
            c_ktr += 16 * nchunks
        elif k == "v":
            c_sp += 16
            c_ktr += 16 * nchunks
        sp_cnt[si] = c_sp
        ktr_cnt[si] = c_ktr

    def prev_sp(si):
        return sp_cnt[si - 1] if si >= 1 else 0

    def gp_fn(gp):
        for si, (g, fo) in enumerate(seq):
            if si >= 2:
                gp.wait_ge(s_pe, (si - 1) * STn)
            fw = g["fo_cols"]
            gp.dma_start(out=wbuf[:, (si % 2) * 4096:(si % 2) * 4096 + 32 * fw],
                         in_=g["w"][fo].rearrange("p hc c -> p (hc c)")).then_inc(s_w[si % 2], 16)

    def pe_fn(t):
        t.wait_ge(ln_dep["sem"], ln_dep["n"])
        for si, (g, fo) in enumerate(seq):
            t.wait_ge(s_w[si % 2], 16 * (si // 2 + 1))
            if si >= 2:
                t.wait_ge(s_ev, ev_cnt[(si - 2, STn - 1)])
            fw = g["fo_cols"]
            for st in range(STn):
                bank = pb[(si % 2) * 4 + st]
                for hc in range(32):
                    inst = t.matmul(bank[0:fw, :],
                                    wbuf[:, (si % 2) * 4096 + hc * fw:(si % 2) * 4096 + (hc + 1) * fw],
                                    xn_buf[:, hc * Ttok + st * 512:hc * Ttok + (st + 1) * 512],
                                    start=(hc == 0), stop=(hc == 31))
                inst.then_inc(s_pe, 1)

    def sc_fn(sc):
        for si, (g, fo) in enumerate(seq):
            kind = g["kind"]
            for st in range(STn):
                sc.wait_ge(s_pe, si * STn + st + 1)
                bank = pb[(si % 2) * 4 + st]
                if kind in ("q", "k"):
                    i = rp_cnt[(si, st)]
                    if i >= 2:
                        sc.wait_ge(s_rp1, 2 * i - 3)
                    sc.activation(ev_scr[:], bank[:, :], AF.Identity,
                                  bias=0.0, scale=1.0).then_inc(s_ev, 1)
                elif kind in ("v", "g"):
                    if st == 0:
                        sc.wait_ge(s_sp, prev_sp(si))
                    sc.activation(stage[:, st * 512:(st + 1) * 512], bank[:, :],
                                  AF.Identity, bias=0.0, scale=1.0).then_inc(s_ev, 1)
                else:
                    sc.activation(R["dt_raw"][0:32, g["dt_off"] + st * 512:g["dt_off"] + (st + 1) * 512],
                                  bank[0:32, :], AF.Identity,
                                  bias=R["dtb_sb"][:, 0:1], scale=1.0).then_inc(s_ev, 1)

    def v_fn(v):
        v.wait_ge(s_rop, 32)
        for si, (g, fo) in enumerate(seq):
            if g["kind"] not in ("q", "k"):
                continue
            for st in range(STn):
                i = rp_cnt[(si, st)]
                v.wait_ge(s_ev, ev_cnt[(si, st)])
                if i >= 2:
                    v.wait_ge(s_rp, i - 1)
                    v.wait_ge(s_rp1, 2 * (i - 1))
                if st == 0:
                    v.wait_ge(s_sp, prev_sp(si))
                cs = ropec[:, st * 512:(st + 1) * 512]
                ss = ropes[:, st * 512:(st + 1) * 512]
                v.tensor_copy(sw[0:64, :], ev_scr[64:128, :])
                v.tensor_copy(sw[64:128, :], ev_scr[0:64, :])
                v.tensor_mul(t1[:], ev_scr[:], cs).then_inc(s_rp1, 1)
                v.wait_ge(s_rp1, 2 * i - 1)
                v.tensor_mul(t2[:], sw[:], ss).then_inc(s_rp1, 1)
                v.wait_ge(s_rp1, 2 * i)
                v.tensor_add(stage[:, st * 512:(st + 1) * 512], t1[:], t2[:]).then_inc(s_rp, 1)

    def sync_fn(q):
        q.dma_start(out=ropec[:], in_=ropec_d[:, 0:Ttok]).then_inc(s_rop, 16)
        q.dma_start(out=ropes[:], in_=ropes_d[:, 0:Ttok]).then_inc(s_rop, 16)
        for si, (g, fo) in enumerate(seq):
            kind = g["kind"]
            if kind == "win":
                continue
            if kind in ("q", "k"):
                q.wait_ge(s_rp, rp_cnt[(si, STn - 1)])
            else:
                q.wait_ge(s_ev, ev_cnt[(si, STn - 1)])
            src = stage.rearrange("p (t c) -> p t c", c=128)
            if kind in ("q", "g", "k") and g.get("dst") is not None:
                dst = (g["dst"].rearrange("t p (f c) -> t p f c", c=128)
                       [g["tc0"]:g["tc0"] + nchunks, :, fo, :].transpose([1, 0, 2]))
                q.dma_start(out=dst, in_=src).then_inc(s_sp, 16)
            if kind in ("k", "v"):
                if si >= 1:
                    q.wait_ge(s_sp, prev_sp(si))
                for tc in range(nchunks):
                    q.dma_start(out=ktm_stage[:, tc * 128:(tc + 1) * 128],
                                in_=stage[:, tc * 128:(tc + 1) * 128],
                                transpose=True).then_inc(s_ktr, 16)
                q.wait_ge(s_ktr, ktr_cnt[si])
                dst2 = (g["dst2"].rearrange("t p (f c) -> t p f c", c=128)
                        [g["ktc0"]:g["ktc0"] + nchunks, :, fo, :].transpose([1, 0, 2]))
                q.dma_start(out=dst2, in_=ktm_stage.rearrange("p (t c) -> p t c", c=128)).then_inc(s_sp, 16)
        q.wait_ge(s_sp, c_sp)
        if c_ktr:
            q.wait_ge(s_ktr, c_ktr)

    p.phase(tag, sync=sync_fn, gpsimd=gp_fn, scalar=sc_fn, tensor=pe_fn, vector=v_fn)
    return dict(frees=frees, s_sp=s_sp, n_sp=c_sp, s_ev=s_ev, n_ev=c_ev)




def fdn_phase(p, TM, qdnT, sdn_cols, zdnT, h_s, ffn_s, R):
    TH = TM // 2 if TM >= 1024 else TM
    NHALF = TM // TH
    ST2 = TH // 512
    assert TH % 512 == 0 and ST2 <= 2
    pb = R["pb"]
    identb = R["identb_sb"]
    pbt = [pb[4 + k].ap().bitcast(BF16) for k in range(4)]
    s_hld = p.sem("fdn_hld")
    s_qd = [p.sem("fdn_qda"), p.sem("fdn_qdb")]
    s_zc = p.sem("fdn_zc")
    s_zs = p.sem("fdn_zs")
    s_nx = p.sem("fdn_nx")
    s_dq = p.sem("fdn_dq")
    s_pet = p.sem("fdn_pet")
    s_te = p.sem("fdn_te")
    s_pe = p.sem("fdn_pe")
    s_ev = p.sem("fdn_ev")
    s_sp = [p.sem("fdn_spa"), p.sem("fdn_spb")]

    hbuf, ch = p.sb("fdn_h", [128, HB * TH], BF16)
    qsub, cq = p.sb("fdn_qsub", [128, 2 * 1024], I32)
    nib, cn = p.sb("fdn_nib", [128, 2 * 1024], I32)
    wtT, cw = p.sb("fdn_wtT", [128, 2 * 1024], BF16)
    wkd, ck = p.sb("fdn_wkd", [128, 2 * 8 * 128], BF16)
    scc, cs1 = p.sb("fdn_scc", [128, 2 * HB], F32)
    zc, cs2 = p.sb("fdn_zc", [128, HB], I32)
    nibz = nib
    zsc, cs4 = p.sb("fdn_zsc", [128, 2 * HB], F32)
    stage, cst = p.sb("fdn_stage", [128, 2 * 512], BF16)
    frees = [ch, cq, cn, cw, ck, cs1, cs2, cs4, cst]

    NIS = 11
    npass = NHALF * DN_CT * 4
    nb = 2 * ST2

    def it_idx(half, ct, pg, isub):
        return ((half * DN_CT + ct) * 4 + pg) * NIS + isub

    def pass_idx(half, ct, pg):
        return (half * DN_CT + ct) * 4 + pg

    def sync_fn(q):
        for half in range(NHALF):
            if half >= 1:
                q.wait_ge(s_pe, 4 * (it_idx(half - 1, DN_CT - 1, 3, NIS - 1) + 1))
            for hb in range(HB):
                q.dma_start(out=hbuf[:, hb * TH:(hb + 1) * TH],
                            in_=h_s[hb][:, half * TH:half * TH + TH]).then_inc(s_hld, 16)
            for ct in range(DN_CT):
                for pg in range(4):
                    pidx = pass_idx(half, ct, pg)
                    if pidx >= 1:
                        q.wait_ge(s_dq, 2 * NIS * pidx)
                    if pg == 0:
                        q.dma_start(out=scc[:], in_=sdn_cols[:, ct * NPLANE * HB + pg * 2 * HB:
                                                             ct * NPLANE * HB + (pg * 2 + 2) * HB]).then_inc(s_zc, 16)
                        q.dma_start(out=zc[:], in_=zdnT[ct * 128:(ct + 1) * 128, :]).then_inc(s_zc, 16)
                    else:
                        q.dma_start(out=scc[:], in_=sdn_cols[:, ct * NPLANE * HB + pg * 2 * HB:
                                                             ct * NPLANE * HB + (pg * 2 + 2) * HB]).then_inc(s_zc, 32)
                    for isub in range(NIS):
                        gi = it_idx(half, ct, pg, isub)
                        if gi >= 2:
                            q.wait_ge(s_nx, 2 * (gi - 1))
                        q.dma_start(out=qsub[:, (gi % 2) * 1024:(gi % 2 + 1) * 1024],
                                    in_=qdnT[ct][:, isub * 1024:(isub + 1) * 1024]).then_inc(s_qd[gi % 2], 16)
                    for spl in range(2):
                        sp_ = pg * 2 + spl
                        for st in range(ST2):
                            t0 = half * TH + st * 512
                            dst2 = ffn_s[8 * ct:8 * ct + 8, :, t0:t0 + 512] \
                                .rearrange("a (pl e) t -> a pl e t", e=8)[:, :, sp_, :]
                            ev_i = pidx * nb + spl * ST2 + st
                            src = stage[:, (ev_i % 2) * 512:(ev_i % 2 + 1) * 512]
                            q.wait_ge(s_ev, ev_i + 1)
                            q.dma_start(out=dst2, in_=src).then_inc(s_sp[ev_i % 2], 16)
        ntot = npass * nb
        q.wait_ge(s_sp[0], 16 * ((ntot + 1) // 2))
        q.wait_ge(s_sp[1], 16 * (ntot // 2))

    def _evacs(v, ei):
        for jj in range(8):
            tt = ei * 8 + jj
            v.wait_ge(s_pet, tt + 1)
            if ei >= 2:
                v.wait_ge(s_pe, 2 * (ei - 1))
            v.tensor_copy(wkd[:, ((ei % 2) * 8 + jj) * 128:((ei % 2) * 8 + jj + 1) * 128],
                          pbt[tt % 4][:, 0:128]).then_inc(s_te, 1)

    def v_fn(v):
        pend = [None]

        def flush(v):
            if pend[0] is not None:
                _evacs(v, pend[0])
                pend[0] = None

        for half in range(NHALF):
            for ct in range(DN_CT):
                for pg in range(4):
                    pidx = pass_idx(half, ct, pg)
                    flush(v)
                    v.wait_ge(s_zc, 32 * (pidx + 1))
                    if pidx >= 1:
                        v.wait_ge(s_dq, 2 * NIS * pidx)
                    for spl in range(2):
                        sp_ = pg * 2 + spl
                        inst = v.tensor_scalar(nibz[:, spl * HB:(spl + 1) * HB], zc[:], 4 * sp_, 15,
                                               ALU.logical_shift_right, ALU.bitwise_and)
                    inst.then_inc(s_zs, 1)
                    v.wait_ge(s_zs, 2 * pidx + 1)
                    for spl in range(2):
                        inst = v.scalar_tensor_tensor(zsc[:, spl * HB:(spl + 1) * HB],
                                                      nibz[:, spl * HB:(spl + 1) * HB], -1.0,
                                                      scc[:, spl * HB:(spl + 1) * HB],
                                                      ALU.mult, ALU.mult)
                    inst.then_inc(s_zs, 1)

                    for isub in range(NIS):
                        gi = it_idx(half, ct, pg, isub)
                        v.wait_ge(s_qd[gi % 2], 16 * (gi // 2 + 1))
                        if isub == 0:
                            v.wait_ge(s_zs, 2 * pidx + 2)
                        for spl in range(2):
                            sp_ = pg * 2 + spl
                            ei = gi * 2 + spl
                            if ei >= 2:
                                v.wait_ge(s_dq, ei - 1)
                            v.tensor_scalar(nib[:, (ei % 2) * 1024:(ei % 2 + 1) * 1024],
                                            qsub[:, (gi % 2) * 1024:(gi % 2 + 1) * 1024],
                                            4 * sp_, 15, ALU.logical_shift_right,
                                            ALU.bitwise_and).then_inc(s_nx, 1)
                            flush(v)
                            pend[0] = ei
        flush(v)

    def sc_fn(sc):
        for half in range(NHALF):
            for ct in range(DN_CT):
                for pg in range(4):
                    pidx = pass_idx(half, ct, pg)
                    for isub in range(NIS):
                        gi = it_idx(half, ct, pg, isub)
                        for spl in range(2):
                            ei = gi * 2 + spl
                            sc.wait_ge(s_nx, ei + 1)
                            sc.wait_ge(s_zs, 2 * pidx + 2)
                            if ei >= 2:
                                sc.wait_ge(s_pet, 8 * (ei - 1))
                            for q8 in range(8):
                                g = 8 * isub + q8
                                src = nib[:, (ei % 2) * 1024:(ei % 2 + 1) * 1024] \
                                    .rearrange("p (s c) -> p s c", c=128)[:, :, q8 * 16:(q8 + 1) * 16]
                                dst = wtT[:, (ei % 2) * 1024:(ei % 2 + 1) * 1024] \
                                    .rearrange("p (s c) -> p s c", c=128)[:, :, q8 * 16:(q8 + 1) * 16]
                                inst = sc.activation(dst, src, AF.Identity,
                                                     bias=zsc[:, spl * HB + g:spl * HB + g + 1],
                                                     scale=scc[:, spl * HB + g:spl * HB + g + 1])
                            inst.then_inc(s_dq, 1)
                    sc.wait_ge(s_pe, 4 * NIS * (pidx + 1))
                    for spl in range(2):
                        for st in range(ST2):
                            ev_i = pidx * nb + spl * ST2 + st
                            if ev_i >= 2:
                                zz = ev_i - 2
                                sc.wait_ge(s_sp[zz % 2], 16 * (zz // 2 + 1))
                            sc.activation(stage[:, (ev_i % 2) * 512:(ev_i % 2 + 1) * 512],
                                          pb[spl * ST2 + st][:, :], AF.Identity,
                                          bias=0.0, scale=1.0).then_inc(s_ev, 1)

    def pe_fn(t):
        def transposes(ei):
            t.wait_ge(s_dq, ei + 1)
            for jj in range(8):
                tt = ei * 8 + jj
                if tt >= 4:
                    t.wait_ge(s_te, tt - 3)
                t.transpose(pbt[tt % 4][:, 0:128],
                            wtT[:, (ei % 2) * 1024 + jj * 128:(ei % 2) * 1024 + (jj + 1) * 128],
                            identb[:]).then_inc(s_pet, 1)

        def mms(half, pidx, gi, isub, spl):
            ei = gi * 2 + spl
            t.wait_ge(s_te, 8 * (ei + 1))
            for jj in range(8):
                hb = isub * 8 + jj
                for st in range(ST2):
                    inst = t.matmul(
                        pb[spl * ST2 + st][:, :],
                        wkd[:, ((ei % 2) * 8 + jj) * 128:((ei % 2) * 8 + jj + 1) * 128],
                        hbuf[:, hb * TH + st * 512:hb * TH + st * 512 + 512],
                        start=(isub == 0 and jj == 0), stop=(isub == NIS - 1 and jj == 7))
            inst.then_inc(s_pe, 2)

        prev = None
        for half in range(NHALF):
            if prev is not None:
                mms(*prev)
                prev = None
            t.wait_ge(s_hld, 16 * HB * (half + 1))
            for ct in range(DN_CT):
                for pg in range(4):
                    pidx = pass_idx(half, ct, pg)
                    if prev is not None:
                        mms(*prev)
                        prev = None
                    if pidx >= 1:
                        t.wait_ge(s_ev, pidx * nb)
                    for isub in range(NIS):
                        gi = it_idx(half, ct, pg, isub)
                        for spl in range(2):
                            transposes(gi * 2 + spl)
                            if prev is not None:
                                mms(*prev)
                            prev = (half, pidx, gi, isub, spl)
        mms(*prev)

    p.phase("fdn", sync=sync_fn, vector=v_fn, scalar=sc_fn, tensor=pe_fn)
    return dict(frees=frees)


def att_phase(p, NM, NP, q_s, g_s, kfm_s, ktm_s, vtm_s, dtraw_s, y_s, R):
    NT = NM + NP
    pb = R["pb"]
    ident_sb, ones1_sb, onesr_sb = R["ident_sb"], R["ones1_sb"], R["onesr_sb"]
    u_sb, mneg_sb, acol_sb, gnw_sb = R["u_sb"], R["mneg_sb"], R["acol_sb"], R["gnw_sb"]
    eps_sb, c_rsel = R["eps_sb"], R["c_rsel"]

    s_dt = p.sem("att_dt")
    s_dta = p.sem("att_dta")
    s_dtv = p.sem("att_dtv")
    s_ld = [p.sem("att_lda"), p.sem("att_ldb")]
    s_clp = p.sem("att_clp")
    s_clv = p.sem("att_clv")
    s_cla = p.sem("att_cla")
    s_xk = p.sem("att_xk")
    s_hp = p.sem("att_hp")
    s_hv = p.sem("att_hv")
    s_ha = p.sem("att_ha")
    s_hg = p.sem("att_hg")
    s_ytr = p.sem("att_ytr")
    s_ysp = p.sem("att_ysp")

    TPc = NP * 128
    NTT = NT * 128
    rsel_sb, c0 = p.sb("att_rsel", [32, 32 * 128], F32)
    dtb, c1 = p.sb("att_dtraw", [32, NTT], F32)
    spb, c2 = p.sb("att_spb", [32, NTT], F32)
    gall, c3 = p.sb("att_gall", [32, NTT], F32)
    qc, c4 = p.sb("att_qc", [128, 2 * 4096], BF16)
    gc, c5 = p.sb("att_gc", [128, 2 * 4096], BF16)
    kf, c6 = p.sb("att_kf", [128, 2 * 1024], BF16)
    kt, c7 = p.sb("att_kt", [128, 2 * 1024], BF16)
    vt, c8 = p.sb("att_vt", [128, 2 * 1024], BF16)
    Sf, c9 = p.sb("att_Sf", [128, 32 * 128], F32)
    Sb, c10 = p.sb("att_Sb", [128, 32 * 128], BF16)
    xc_all, c11 = p.sb("att_xc", [128, 32 * 128], BF16)
    kd_all, c12 = p.sb("att_kd", [128, 32 * 128], BF16)
    dtT, c13 = p.sb("att_dtT", [128, 32], F32)
    gt, c14 = p.sb("att_gt", [128, 32], F32)
    GT, c15 = p.sb("att_GT", [128, 32], F32)
    glrow, c16 = p.sb("att_glrow", [1, 32], F32)
    egrow, c17 = p.sb("att_egrow", [1, 32], F32)
    EGB, c18 = p.sb("att_EGB", [128, 32], F32)
    d1, c19 = p.sb("att_d1", [128, 32], F32)
    EJL, c20 = p.sb("att_EJL", [128, 32], F32)
    EI, c21 = p.sb("att_EI", [128, 32], F32)
    gth, c22 = p.sb("att_gth", [32, 128], F32)
    d2, c23 = p.sb("att_d2", [128, 2 * 128], F32)
    dec, c24 = p.sb("att_dec", [128, 2 * 128], BF16)
    pbf, c25 = p.sb("att_pbf", [128, 2 * 128], BF16)
    yscr, c26 = p.sb("att_yscr", [128, 2 * 128], F32)
    ytm, c27 = p.sb("att_ytm", [128, 2 * 128], F32)
    ynb, c28 = p.sb("att_ynb", [128, 2 * 128], BF16)
    ynf, c29 = p.sb("att_ynf", [128, 2 * 128], BF16)
    sgt, c30 = p.sb("att_sgt", [128, 32 * 128], F32)
    tmul, c31 = p.sb("att_tmul", [128, 2 * 128], BF16)
    ystg, c32 = p.sb("att_ystg", [128, 32 * 128], BF16)
    sqt, c33 = p.sb("att_sqt", [128, 2 * 128], F32)
    ssq, c34 = p.sb("att_ssq", [128, 2], F32)
    rms, c35 = p.sb("att_rms", [128, 2], F32)
    frees = [c0, c1, c2, c3, c4, c5, c6, c7, c8, c9, c10, c11, c12, c13, c14, c15,
             c16, c17, c18, c19, c20, c21, c22, c23, c24, c25, c26, c27, c28,
             c29, c30, c31, c32, c33, c34, c35]

    NCLP, NCLV, NCLA = 7, 6, 5
    def is_main(n):
        return n >= NP
    HP_M, HV_M, HA_M, HG_M = 5, 9, 4, 1
    HP_P, HV_P = 1, 2

    hp_base = [0]; hv_base = [0]; ha_base = [0]; hg_base = [0]; xk_base = [0]
    ytr_base = [0]; ysp_base = [0]; ld_base = [[0], [0]]
    for n in range(NT):
        m = is_main(n)
        hp_base.append(hp_base[-1] + 32 * (HP_M if m else HP_P))
        hv_base.append(hv_base[-1] + 32 * (HV_M if m else HV_P))
        ha_base.append(ha_base[-1] + 32 * (HA_M if m else 0))
        hg_base.append(hg_base[-1] + 32 * (HG_M if m else 0))
        xk_base.append(xk_base[-1] + 32)
        ytr_base.append(ytr_base[-1] + (16 * 32 if m else 0))
        ysp_base.append(ysp_base[-1] + (16 * 32 if m else 0))
        nl = 5 if m else 2
        for par in range(2):
            ld_base[par].append(ld_base[par][-1] + (16 * nl if n % 2 == par else 0))

    def qsl(n):
        return qc[:, (n % 2) * 4096:(n % 2) * 4096 + 4096]

    def gsl(n):
        return gc[:, (n % 2) * 4096:(n % 2) * 4096 + 4096]

    def kfsl(n):
        return kf[:, (n % 2) * 1024:(n % 2) * 1024 + 1024]

    def ktsl(n):
        return kt[:, (n % 2) * 1024:(n % 2) * 1024 + 1024]

    def vtsl(n):
        return vt[:, (n % 2) * 1024:(n % 2) * 1024 + 1024]

    def chunk_consumed_waits(eng, n):
        eng.wait_ge(s_hp, hp_base[n + 1])
        eng.wait_ge(s_hv, hv_base[n + 1])
        if is_main(n):
            eng.wait_ge(s_ha, ha_base[n + 1])
            eng.wait_ge(s_hg, hg_base[n + 1])

    def sync_fn(q):
        q.dma_start(out=rsel_sb[:], in_=c_rsel).then_inc(s_dt, 16)
        q.dma_start(out=dtb[:], in_=dtraw_s).then_inc(s_dt, 16)

        def loads(n):
            par = n % 2
            if n >= 2:
                chunk_consumed_waits(q, n - 2)
            if is_main(n):
                nm = n - NP
                q.dma_start(out=qsl(n), in_=q_s[nm]).then_inc(s_ld[par], 16)
                q.dma_start(out=gsl(n), in_=g_s[nm]).then_inc(s_ld[par], 16)
                q.dma_start(out=kfsl(n), in_=kfm_s[nm]).then_inc(s_ld[par], 16)
            q.dma_start(out=ktsl(n), in_=ktm_s[n]).then_inc(s_ld[par], 16)
            q.dma_start(out=vtsl(n), in_=vtm_s[n]).then_inc(s_ld[par], 16)

        loads(0)
        for n in range(NT):
            if n + 1 < NT:
                loads(n + 1)
            if is_main(n):
                nm = n - NP
                for h in range(32):
                    q.wait_ge(s_hv, hv_base[n] + h * HV_M + 9)
                    if h >= 2:
                        q.wait_ge(s_hg, hg_base[n] + h - 1)
                    q.dma_start(out=ynf[:, (h % 2) * 128:(h % 2) * 128 + 128],
                                in_=ynb[:, (h % 2) * 128:(h % 2) * 128 + 128],
                                transpose=True).then_inc(s_ytr, 16)
                    q.wait_ge(s_ha, ha_base[n] + h * HA_M + 4)
                    q.dma_start(out=y_s[h][:, nm * 128:(nm + 1) * 128],
                                in_=ystg[:, h * 128:(h + 1) * 128]).then_inc(s_ysp, 16)
        q.wait_ge(s_ysp, ysp_base[NT])
        q.wait_ge(s_ytr, ytr_base[NT])

    def v_fn(v):
        v.memzero(Sf[:])
        v.memzero(Sb[:]).then_inc(s_dtv, 1)
        v.wait_ge(s_dt, 32)
        v.tensor_scalar_max(gall[:], dtb[:], 0.0).then_inc(s_dtv, 1)
        v.wait_ge(s_dta, 3)
        v.wait_ge(s_dtv, 2)
        v.tensor_add(spb[:], spb[:], gall[:]).then_inc(s_dtv, 1)
        v.wait_ge(s_dtv, 3)
        v.tensor_scalar(gall[:], spb[:], acol_sb[0:32, 0:1], None, ALU.mult).then_inc(s_dtv, 1)
        for n in range(NT):
            m = is_main(n)
            if n >= 1:
                v.wait_ge(s_hp, hp_base[n])
                v.wait_ge(s_xk, xk_base[n])
                v.wait_ge(s_cla, NCLA * n)
            v.wait_ge(s_clp, NCLP * n + 1)
            v.tensor_copy(dtT[:], pb[0][:, 0:32]).then_inc(s_clv, 1)
            v.wait_ge(s_clp, NCLP * n + 2)
            v.tensor_copy(gt[:], pb[1][:, 0:32]).then_inc(s_clv, 1)
            v.wait_ge(s_clp, NCLP * n + 3)
            v.tensor_copy(GT[:], pb[2][:, 0:32]).then_inc(s_clv, 1)
            v.wait_ge(s_clp, NCLP * n + 5)
            v.tensor_copy(EGB[:], pb[1][:, 0:32]).then_inc(s_clv, 1)
            v.wait_ge(s_clp, NCLP * n + 6)
            v.tensor_sub(d1[:], pb[2][:, 0:32], GT[:]).then_inc(s_clv, 1)
            v.wait_ge(s_clp, NCLP * n + 7)
            v.tensor_copy(gth[:], pb[0][0:32, 0:128]).then_inc(s_clv, 1)
            v.wait_ge(s_ld[n % 2], ld_base[n % 2][n + 1])
            for h in range(32):
                kv = h // 4
                v.wait_ge(s_cla, NCLA * n + 3)
                v.tensor_scalar(xc_all[:, h * 128:(h + 1) * 128], vtsl(n)[:, kv * 128:(kv + 1) * 128],
                                dtT[:, h:h + 1], None, ALU.mult)
                v.tensor_scalar(kd_all[:, h * 128:(h + 1) * 128], ktsl(n)[:, kv * 128:(kv + 1) * 128],
                                EJL[:, h:h + 1], None, ALU.mult).then_inc(s_xk, 1)
            if m:
                for h in range(32):
                    hb = hv_base[n] + h * HV_M
                    v.wait_ge(s_hp, hp_base[n] + h * HP_M + 1)
                    v.scalar_tensor_tensor(d2[:, (h % 2) * 128:(h % 2) * 128 + 128],
                                           pb[3][:, 0:128], GT[:, h:h + 1], mneg_sb[:],
                                           ALU.subtract, ALU.add).then_inc(s_hv, 1)
                    v.wait_ge(s_hp, hp_base[n] + h * HP_M + 2)
                    v.wait_ge(s_ha, ha_base[n] + h * HA_M + 1)
                    v.tensor_mul(pbf[:, (h % 2) * 128:(h % 2) * 128 + 128],
                                 pb[h % 2][:, 0:128],
                                 dec[:, (h % 2) * 128:(h % 2) * 128 + 128]).then_inc(s_hv, 1)
                    v.wait_ge(s_hp, hp_base[n] + h * HP_M + 3)
                    v.tensor_scalar(yscr[:, (h % 2) * 128:(h % 2) * 128 + 128],
                                    pb[4 + (h % 2) * 2][:, 0:128], EI[:, h:h + 1], None,
                                    ALU.mult).then_inc(s_hv, 1)
                    v.wait_ge(s_hp, hp_base[n] + h * HP_M + 4)
                    v.wait_ge(s_hv, hb + 3)
                    v.tensor_add(ytm[:, (h % 2) * 128:(h % 2) * 128 + 128],
                                 pb[5 + (h % 2) * 2][:, 0:128],
                                 yscr[:, (h % 2) * 128:(h % 2) * 128 + 128]).then_inc(s_hv, 1)
                    v.wait_ge(s_hv, hb + 4)
                    if h >= 2:
                        v.wait_ge(s_hv, hv_base[n] + (h - 2) * HV_M + 6)
                    v.tensor_mul(sqt[:, (h % 2) * 128:(h % 2) * 128 + 128],
                                 ytm[:, (h % 2) * 128:(h % 2) * 128 + 128],
                                 ytm[:, (h % 2) * 128:(h % 2) * 128 + 128]).then_inc(s_hv, 1)
                    v.wait_ge(s_hv, hb + 5)
                    v.reduce_sum(ssq[:, h % 2:h % 2 + 1],
                                 sqt[:, (h % 2) * 128:(h % 2) * 128 + 128],
                                 axis=AX.X).then_inc(s_hv, 1)
                    v.wait_ge(s_hp, hp_base[n] + h * HP_M + 5)
                    v.scalar_tensor_tensor(Sf[:, h * 128:(h + 1) * 128], Sf[:, h * 128:(h + 1) * 128],
                                           EGB[:, h:h + 1], pb[2][:, 0:128],
                                           ALU.mult, ALU.add).then_inc(s_hv, 1)
                    v.wait_ge(s_hv, hb + 7)
                    v.tensor_copy(Sb[:, h * 128:(h + 1) * 128], Sf[:, h * 128:(h + 1) * 128]).then_inc(s_hv, 1)
                    v.wait_ge(s_ha, ha_base[n] + h * HA_M + 3)
                    v.tensor_scalar(ynb[:, (h % 2) * 128:(h % 2) * 128 + 128],
                                    ytm[:, (h % 2) * 128:(h % 2) * 128 + 128],
                                    rms[:, h % 2:h % 2 + 1], None, ALU.mult).then_inc(s_hv, 1)
            else:
                for h in range(32):
                    hb = hv_base[n] + h * HV_P
                    v.wait_ge(s_hp, hp_base[n] + h * HP_P + 1)
                    v.scalar_tensor_tensor(Sf[:, h * 128:(h + 1) * 128], Sf[:, h * 128:(h + 1) * 128],
                                           EGB[:, h:h + 1], pb[2 + h % 2][:, 0:128],
                                           ALU.mult, ALU.add).then_inc(s_hv, 1 if n == NP - 1 else 2)
                    if n == NP - 1:
                        v.wait_ge(s_hv, hb + 1)
                        v.tensor_copy(Sb[:, h * 128:(h + 1) * 128], Sf[:, h * 128:(h + 1) * 128]).then_inc(s_hv, 1)

    def sc_fn(sc):
        sc.wait_ge(s_dt, 32)
        sc.activation(spb[:], dtb[:], AF.Abs, bias=0.0, scale=1.0).then_inc(s_dta, 1)
        sc.wait_ge(s_dta, 1)
        sc.activation(spb[:], spb[:], AF.Exp, bias=0.0, scale=-1.0).then_inc(s_dta, 1)
        sc.wait_ge(s_dta, 2)
        sc.activation(spb[:], spb[:], AF.Ln, bias=eps_sb[0:32, 1:2], scale=1.0).then_inc(s_dta, 1)
        for n in range(NT):
            m = is_main(n)
            if n >= 1:
                sc.wait_ge(s_hv, hv_base[n])
                sc.wait_ge(s_xk, xk_base[n])
            sc.wait_ge(s_clp, NCLP * n + 4)
            sc.activation(glrow[0:1, :], pb[0][0:1, 0:32], AF.Identity, bias=0.0, scale=1.0).then_inc(s_cla, 1)
            sc.wait_ge(s_cla, NCLA * n + 1)
            sc.activation(egrow[0:1, :], glrow[0:1, :], AF.Exp, bias=0.0, scale=1.0).then_inc(s_cla, 1)
            sc.wait_ge(s_clv, NCLV * n + 5)
            sc.activation(EJL[:], d1[:], AF.Exp, bias=0.0, scale=1.0).then_inc(s_cla, 1)
            sc.wait_ge(s_clv, NCLV * n + 3)
            if not m:
                sc.activation(EI[:], GT[:], AF.Exp, bias=0.0, scale=1.0).then_inc(s_cla, 2)
                continue
            sc.activation(EI[:], GT[:], AF.Exp, bias=0.0, scale=1.0).then_inc(s_cla, 1)
            sc.wait_ge(s_ld[n % 2], ld_base[n % 2][n + 1])
            sc.wait_ge(s_ysp, ysp_base[n])
            if n >= 1:
                sc.wait_ge(s_hg, hg_base[n])
            for h in range(32):
                inst = sc.activation(sgt[:, h * 128:(h + 1) * 128],
                                     gsl(n)[:, h * 128:(h + 1) * 128], AF.Silu,
                                     bias=0.0, scale=1.0)
            inst.then_inc(s_cla, 1)
            for h in range(32):
                ab = ha_base[n] + h * HA_M
                sc.wait_ge(s_hv, hv_base[n] + h * HV_M + 1)
                if h >= 2:
                    sc.wait_ge(s_hv, hv_base[n] + (h - 2) * HV_M + 2)
                sc.activation(dec[:, (h % 2) * 128:(h % 2) * 128 + 128],
                              d2[:, (h % 2) * 128:(h % 2) * 128 + 128], AF.Exp,
                              bias=0.0, scale=1.0).then_inc(s_ha, 1)
                sc.wait_ge(s_hv, hv_base[n] + h * HV_M + 6)
                sc.activation(rms[:, h % 2:h % 2 + 1], ssq[:, h % 2:h % 2 + 1], AF.Ln,
                              bias=eps_sb[:, 0:1], scale=1.0 / 128).then_inc(s_ha, 1)
                sc.wait_ge(s_ha, ab + 2)
                sc.activation(rms[:, h % 2:h % 2 + 1], rms[:, h % 2:h % 2 + 1], AF.Exp,
                              bias=0.0, scale=-0.5).then_inc(s_ha, 1)
                sc.wait_ge(s_hg, hg_base[n] + h + 1)
                sc.activation(ystg[:, h * 128:(h + 1) * 128],
                              tmul[:, (h % 2) * 128:(h % 2) * 128 + 128], AF.Identity,
                              bias=0.0, scale=gnw_sb[:, 0:1]).then_inc(s_ha, 1)

    def gp_fn(gp):
        for n in range(NT):
            if not is_main(n):
                continue
            for h in range(32):
                gp.wait_ge(s_ytr, ytr_base[n] + 16 * (h + 1))
                gp.wait_ge(s_cla, NCLA * n + 5)
                if h >= 2:
                    gp.wait_ge(s_ha, ha_base[n] + (h - 2) * HA_M + 4)
                gp.tensor_mul(tmul[:, (h % 2) * 128:(h % 2) * 128 + 128],
                              ynf[:, (h % 2) * 128:(h % 2) * 128 + 128],
                              sgt[:, h * 128:(h + 1) * 128]).then_inc(s_hg, 1)

    def pe_fn(t):
        t.wait_ge(s_dta, 3)
        t.wait_ge(s_dtv, 4)
        for n in range(NT):
            m = is_main(n)
            par = n % 2
            nl = 5 if m else 2
            t.wait_ge(s_ld[par], ld_base[par][n + 1])
            if n >= 1:
                t.wait_ge(s_hv, hv_base[n])
                if is_main(n - 1):
                    t.wait_ge(s_ha, ha_base[n])
            t.transpose(pb[0][:, 0:32], spb[0:32, n * 128:(n + 1) * 128], ident_sb[0:32, 0:32]).then_inc(s_clp, 1)
            t.transpose(pb[1][:, 0:32], gall[0:32, n * 128:(n + 1) * 128], ident_sb[0:32, 0:32]).then_inc(s_clp, 1)
            t.wait_ge(s_clv, NCLV * n + 2)
            t.matmul(pb[2][:, 0:32], u_sb[:], gt[:], start=True, stop=True).then_inc(s_clp, 1)
            t.wait_ge(s_clv, NCLV * n + 1)
            t.matmul(pb[0][0:1, 0:32], ones1_sb[:], gt[:], start=True, stop=True).then_inc(s_clp, 1)
            t.wait_ge(s_cla, NCLA * n + 2)
            t.matmul(pb[1][:, 0:32], onesr_sb[:], egrow[0:1, :], start=True, stop=True).then_inc(s_clp, 1)
            t.wait_ge(s_clv, NCLV * n + 3)
            t.matmul(pb[2][:, 0:32], onesr_sb[:], glrow[0:1, :], start=True, stop=True).then_inc(s_clp, 1)
            t.wait_ge(s_cla, NCLA * n + 1)
            t.transpose(pb[0][0:32, 0:128], GT[:], ident_sb[:]).then_inc(s_clp, 1)
            if m:
                for h in range(32):
                    hb = hp_base[n] + h * HP_M
                    if h >= 1:
                        t.wait_ge(s_hv, hv_base[n] + (h - 1) * HV_M + 1)
                    t.wait_ge(s_clv, NCLV * n + 6)
                    t.matmul(pb[3][:, 0:128], rsel_sb[0:32, h * 128:(h + 1) * 128], gth[:],
                             start=True, stop=True).then_inc(s_hp, 1)
                    if h >= 2:
                        t.wait_ge(s_hv, hv_base[n] + (h - 2) * HV_M + 2)
                    t.matmul(pb[h % 2][:, 0:128], kfsl(n)[:, (h // 4) * 128:(h // 4 + 1) * 128],
                             qsl(n)[:, h * 128:(h + 1) * 128], start=True, stop=True).then_inc(s_hp, 1)
                    if h >= 2:
                        t.wait_ge(s_hv, hv_base[n] + (h - 2) * HV_M + 3)
                    t.matmul(pb[4 + (h % 2) * 2][:, 0:128], qsl(n)[:, h * 128:(h + 1) * 128],
                             Sb[:, h * 128:(h + 1) * 128], start=True, stop=True).then_inc(s_hp, 1)
                    t.wait_ge(s_hv, hv_base[n] + h * HV_M + 2)
                    t.wait_ge(s_xk, xk_base[n] + h + 1)
                    if h >= 2:
                        t.wait_ge(s_hv, hv_base[n] + (h - 2) * HV_M + 4)
                    t.matmul(pb[5 + (h % 2) * 2][:, 0:128], pbf[:, (h % 2) * 128:(h % 2) * 128 + 128],
                             xc_all[:, h * 128:(h + 1) * 128], start=True, stop=True).then_inc(s_hp, 1)
                    if h >= 1:
                        t.wait_ge(s_hv, hv_base[n] + (h - 1) * HV_M + 7)
                    t.matmul(pb[2][:, 0:128], kd_all[:, h * 128:(h + 1) * 128],
                             xc_all[:, h * 128:(h + 1) * 128], start=True, stop=True).then_inc(s_hp, 1)
            else:
                for h in range(32):
                    t.wait_ge(s_xk, xk_base[n] + h + 1)
                    if h >= 2:
                        t.wait_ge(s_hv, hv_base[n] + (h - 2) * HV_P + 2)
                    t.matmul(pb[2 + h % 2][:, 0:128], kd_all[:, h * 128:(h + 1) * 128],
                             xc_all[:, h * 128:(h + 1) * 128], start=True, stop=True).then_inc(s_hp, 1)

    p.phase("att", sync=sync_fn, vector=v_fn, scalar=sc_fn, tensor=pe_fn, gpsimd=gp_fn)
    return dict(frees=frees)



def wcomb_phase(p, TM, wo_r, ffn_s, y_s, x_main, out_d, R):
    ST = TM // 512
    pb = R["pb"]
    s_y = p.sem("wc_y")
    s_w = [p.sem("wc_wa"), p.sem("wc_wb")]
    s_pe = p.sem("wc_pe")
    s_ev = p.sem("wc_ev")
    s_ftr = [p.sem("wc_ftra"), p.sem("wc_ftrb")]
    s_fl = p.sem("wc_fl")
    s_xl = s_fl
    s_ot = p.sem("wc_ot")
    s_os = [p.sem("wc_osa"), p.sem("wc_osb")]

    ybuf, cy = p.sb("wc_y", [128, 32 * TM], BF16)
    wob, cw = p.sb("wc_wob", [128, 2 * 4096], BF16)
    fbuf, cf = p.sb("wc_f", [128, 2 * 512], BF16)
    xbuf, cx = p.sb("wc_x", [128, 2 * 512], F32)
    sumb, cs = p.sb("wc_sum", [128, 2 * 512], BF16)
    sumT, ct_ = p.sb("wc_sumT", [128, 2 * 512], BF16)
    outb, co = p.sb("wc_out", [128, 2 * 512], F32)
    frees = [cy, cw, cf, cx, cs, ct_, co]

    items = [(fo, st) for fo in range(32) for st in range(ST)]

    def it(fo, st):
        return fo * ST + st

    def sync_fn(q):
        for dq in range(32):
            q.dma_start(out=ybuf[:, dq * TM:(dq + 1) * TM], in_=y_s[dq]).then_inc(s_y, 16)
        for fo, st in items:
            i = it(fo, st)
            sl = i % 2
            if i >= 2:
                q.wait_ge(s_ev, i - 1)
            q.dma_start(out=fbuf[:, sl * 512:(sl + 1) * 512],
                        in_=ffn_s[fo][:, st * 512:(st + 1) * 512]).then_inc(s_fl, 16)
            if i >= 2:
                q.wait_ge(s_ot, i - 1)
            xs = x_main.rearrange("(s u t) f -> s u t f", u=4, t=128)[st, :, :, :] \
                [:, :, fo * 128:(fo + 1) * 128]
            q.dma_start(out=xbuf[:, sl * 512:(sl + 1) * 512],
                        in_=xs.transpose([1, 0, 2])).then_inc(s_xl, 16)
            q.wait_ge(s_ev, i + 1)
            for u in range(4):
                q.dma_start(out=sumT[:, sl * 512 + u * 128:sl * 512 + (u + 1) * 128],
                            in_=sumb[:, sl * 512 + u * 128:sl * 512 + (u + 1) * 128],
                            transpose=True).then_inc(s_ftr[i % 2], 16)
            q.wait_ge(s_ot, i + 1)
            for u in range(4):
                q.dma_start(out=out_d[st * 512 + u * 128:st * 512 + (u + 1) * 128,
                                      fo * 128:(fo + 1) * 128],
                            in_=outb[:, sl * 512 + u * 128:sl * 512 + (u + 1) * 128]).then_inc(s_os[i % 2], 16)
        ni = len(items)
        q.wait_ge(s_os[0], 16 * 4 * ((ni + 1) // 2))
        if ni >= 2:
            q.wait_ge(s_os[1], 16 * 4 * (ni // 2))

    def gp_fn(gp):
        for fo in range(32):
            if fo >= 2:
                gp.wait_ge(s_pe, ST * (fo - 1))
            gp.dma_start(out=wob[:, (fo % 2) * 4096:(fo % 2) * 4096 + 4096],
                         in_=wo_r[fo].rearrange("p hc c -> p (hc c)")).then_inc(s_w[fo % 2], 16)

    def pe_fn(t):
        t.wait_ge(s_y, 16 * 32)
        for fo in range(32):
            t.wait_ge(s_w[fo % 2], 16 * (fo // 2 + 1))
            if fo >= 2:
                t.wait_ge(s_ev, ST * (fo - 1))
            for st in range(ST):
                bank = pb[(fo % 2) * 4 + st]
                for dq in range(32):
                    inst = t.matmul(bank[:, :],
                                    wob[:, (fo % 2) * 4096 + dq * 128:(fo % 2) * 4096 + (dq + 1) * 128],
                                    ybuf[:, dq * TM + st * 512:dq * TM + (st + 1) * 512],
                                    start=(dq == 0), stop=(dq == 31))
                inst.then_inc(s_pe, 1)

    def v_fn(v):
        for fo, st in items:
            i = it(fo, st)
            sl = i % 2
            v.wait_ge(s_pe, i + 1)
            v.wait_ge(s_fl, 32 * (i + 1))
            if i >= 2:
                v.wait_ge(s_ftr[i % 2], 16 * 4 * (i // 2))
            v.tensor_add(sumb[:, sl * 512:(sl + 1) * 512],
                         pb[(fo % 2) * 4 + st][:, :],
                         fbuf[:, sl * 512:(sl + 1) * 512]).then_inc(s_ev, 1)
            v.wait_ge(s_ftr[i % 2], 16 * 4 * (i // 2 + 1))
            v.wait_ge(s_xl, 32 * (i + 1))
            if i >= 2:
                zz = i - 2
                v.wait_ge(s_os[zz % 2], 16 * 4 * (zz // 2 + 1))
            v.tensor_add(outb[:, sl * 512:(sl + 1) * 512],
                         sumT[:, sl * 512:(sl + 1) * 512],
                         xbuf[:, sl * 512:(sl + 1) * 512]).then_inc(s_ot, 1)

    p.phase("wcomb", sync=sync_fn, gpsimd=gp_fn, tensor=pe_fn, vector=v_fn)
    return dict(frees=frees)



def build(n_main=16, n_prefix=16, debug=False):
    p = P(n_main, n_prefix, debug)
    TM, TP = p.TM, p.TP
    NM, NP = n_main, n_prefix
    NT = NM + NP

    x_main = p.din("x_main", [TM, HID], F32)
    x_prefix = p.din("x_prefix", [TP, HID], F32) if NP else None
    wq_r = p.din("wq_r", [32, 128, 32, 128], F32)
    wk_r = p.din("wk_r", [8, 128, 32, 128], F32)
    wv_r = p.din("wv_r", [8, 128, 32, 128], F32)
    wg_r = p.din("wg_r", [32, 128, 32, 128], F32)
    wo_r = p.din("wo_r", [32, 128, 32, 128], F32)
    win_r = p.din("win_r", [1, 128, 32, 32], F32)
    dt_bias_col = p.din("dt_bias_col", [32, 1], F32)
    albias_col = p.din("albias_col", [32, 1], F32)
    gnw_col = p.din("gnw_col", [128, 1], F32)
    ropec_m = p.din("ropec_m", [128, TM], BF16)
    ropes_m = p.din("ropes_m", [128, TM], BF16)
    ropec_p = p.din("ropec_p", [128, TP], BF16) if NP else None
    ropes_p = p.din("ropes_p", [128, TP], BF16) if NP else None
    c_ident = p.din("c_ident", [128, 128], F32)
    c_identb = p.din("c_identb", [128, 128], BF16)
    c_ones1 = p.din("c_ones1", [128, 1], F32)
    c_onesr = p.din("c_onesr", [1, 128], F32)
    c_u = p.din("c_u", [128, 128], F32)
    c_mneg = p.din("c_mneg", [128, 128], F32)
    c_eps = p.din("c_eps", [128, 2], F32)
    c_rsel = p.din("c_rsel", [32, 32 * 128], F32)
    qguT = p.din("qguT", [GU_CT, 128, HID], I32)
    sgu_cols = p.din("sgu_cols", [128, GU_CT * NPLANE * 32], F32)
    zguT = p.din("zguT", [GU_CT, 128, 32], I32)
    qdnT = p.din("qdnT", [DN_CT, 128, HB * 128], I32)
    sdn_cols = p.din("sdn_cols", [128, DN_CT * NPLANE * HB], F32)
    zdnT = p.din("zdnT", [512, HB], I32)

    out_d = p.dout("out", [TM, HID], F32)

    q_s = p.dtmp("q_s", [NM, 128, 4096], BF16)
    g_s = p.dtmp("g_s", [NM, 128, 4096], BF16)
    kfm_s = p.dtmp("kfm_s", [NM, 128, 1024], BF16)
    ktm_s = p.dtmp("ktm_s", [NT, 128, 1024], BF16)
    vtm_s = p.dtmp("vtm_s", [NT, 128, 1024], BF16)
    dtraw_s = p.dtmp("dtraw_s", [32, NT * 128], F32)
    h_s = p.dtmp("h_s", [HB, 128, TM], BF16)
    ffn_s = p.dtmp("ffn_s", [32, 128, TM], BF16)
    y_s = p.dtmp("y_s", [32, 128, TM], BF16)
    xn_dbg = p.dtmp("xn_dbg", [128, 32 * TM], BF16) if debug else None

    ident_sb, _ = p.sb("ident_sb", [128, 128], F32)
    identb_sb, _ = p.sb("identb_sb", [128, 128], BF16)
    ones1_sb, _ = p.sb("ones1_sb", [128, 1], F32)
    onesr_sb, _ = p.sb("onesr_sb", [1, 128], F32)
    u_sb, _ = p.sb("u_sb", [128, 128], F32)
    mneg_sb, _ = p.sb("mneg_sb", [128, 128], F32)
    dtb_sb, _ = p.sb("dtb_sb", [32, 1], F32)
    alb_sb, _ = p.sb("alb_sb", [32, 1], F32)
    acol_sb, _ = p.sb("acol_sb", [32, 1], F32)
    gnw_sb, _ = p.sb("gnw_sb", [128, 1], F32)
    eps_sb, _ = p.sb("eps_sb", [128, 2], F32)

    pb = [p.ctx.enter_context(p.nc.psum_tensor(f"pb{i}", [128, 512], F32)) for i in range(8)]
    R = dict(pb=pb, dtb_sb=dtb_sb, c_rsel=c_rsel, ident_sb=ident_sb,
             identb_sb=identb_sb, ones1_sb=ones1_sb,
             onesr_sb=onesr_sb, u_sb=u_sb, mneg_sb=mneg_sb, acol_sb=acol_sb,
             gnw_sb=gnw_sb, eps_sb=eps_sb)

    s_ic = p.sem("s_ic")
    s_ac = p.sem("s_ac")

    def init_sync(q):
        for t, d in [(ident_sb, c_ident), (identb_sb, c_identb),
                     (ones1_sb, c_ones1), (onesr_sb, c_onesr),
                     (u_sb, c_u), (mneg_sb, c_mneg), (dtb_sb, dt_bias_col),
                     (alb_sb, albias_col), (gnw_sb, gnw_col), (eps_sb, c_eps)]:
            q.dma_start(out=t[:], in_=d).then_inc(s_ic, 16)

    def init_scalar(sc):
        sc.wait_ge(s_ic, 16 * 10)
        sc.activation(acol_sb[:], alb_sb[:], AF.Exp, bias=0.0, scale=1.0).then_inc(s_ac, 1)

    def init_vector(v):
        v.wait_ge(s_ac, 1)
        v.tensor_scalar_mul(acol_sb[:], acol_sb[:], -1.0)

    p.phase("init", sync=init_sync, scalar=init_scalar, vector=init_vector)
    p.barrier("init")

    xn_fm, cm_xnfm = p.sb("xn_fm", [128, 32 * max(TM, TP if NP else TM)], BF16)
    dt_raw, cm_dtraw = p.sb("dt_raw", [32, NT * 128], F32)
    R["dt_raw"] = dt_raw

    if NP:
        ln0 = ln_phase(p, "p0a", NP, x_prefix, xn_fm, TP, eps_sb, R)
        p.barrier("p0a")
        for cm in reversed(ln0["frees"]):
            p.free(cm)
        groups0 = [
            dict(kind="k", w=wk_r, n_fo=8, fo_cols=128, dst=kfm_s, dst2=ktm_s, tc0=0, ktc0=0),
            dict(kind="v", w=wv_r, n_fo=8, fo_cols=128, dst2=vtm_s, ktc0=0),
            dict(kind="win", w=win_r, n_fo=1, fo_cols=32, dt_off=0),
        ]
        groups0[0]["dst"] = None
        pr0 = proj_phase(p, "p0b", NP, xn_fm, TP, groups0, ln0, ropec_p, ropes_p, R)
        p.barrier("p0")
        for cm in reversed(pr0["frees"]):
            p.free(cm)

    ln1 = ln_phase(p, "p1a", NM, x_main, xn_fm, TM, eps_sb, R)
    p.barrier("p1a")
    for cm in reversed(ln1["frees"]):
        p.free(cm)
    groups1 = [
        dict(kind="q", w=wq_r, n_fo=32, fo_cols=128, dst=q_s, tc0=0),
        dict(kind="k", w=wk_r, n_fo=8, fo_cols=128, dst=kfm_s, dst2=ktm_s, tc0=0, ktc0=NP),
        dict(kind="v", w=wv_r, n_fo=8, fo_cols=128, dst2=vtm_s, ktc0=NP),
        dict(kind="g", w=wg_r, n_fo=32, fo_cols=128, dst=g_s, tc0=0),
        dict(kind="win", w=win_r, n_fo=1, fo_cols=32, dt_off=TP),
    ]
    pr1 = proj_phase(p, "p1b", NM, xn_fm, TM, groups1, ln1, ropec_m, ropes_m, R)

    s_dtsp = p.sem("s_dtsp")

    def dtspill_sync(q):
        q.wait_ge(pr1["s_ev"], pr1["n_ev"])
        if NP:
            q.wait_ge(pr0["s_ev"], pr0["n_ev"])
        q.dma_start(out=dtraw_s, in_=dt_raw[:]).then_inc(s_dtsp, 16)
        q.wait_ge(s_dtsp, 16)
        if p.debug:
            pass

    p.phase("dtspill", sync=dtspill_sync)
    p.barrier("p1")
    for cm in reversed(pr1["frees"]):
        p.free(cm)
    p.free(cm_dtraw)

    if debug:
        s_dbg = p.sem("s_xdbg")

        def dbg_sync(q):
            q.dma_start(out=xn_dbg, in_=xn_fm[:, 0:32 * TM]).then_inc(s_dbg, 16)
            q.wait_ge(s_dbg, 16)
        p.phase("xdbg", sync=dbg_sync)
        p.barrier("xdbg")

    fg = fgu_phase(p, xn_fm, TM, qguT, sgu_cols, zguT, h_s, R)
    p.barrier("fgu")
    for cm in reversed(fg["frees"]):
        p.free(cm)
    p.free(cm_xnfm)

    fd = fdn_phase(p, TM, qdnT, sdn_cols, zdnT, h_s, ffn_s, R)
    p.barrier("fdn")
    for cm in reversed(fd["frees"]):
        p.free(cm)

    at = att_phase(p, NM, NP, q_s, g_s, kfm_s, ktm_s, vtm_s, dtraw_s, y_s, R)
    p.barrier("att")
    for cm in reversed(at["frees"]):
        p.free(cm)

    wc = wcomb_phase(p, TM, wo_r, ffn_s, y_s, x_main, out_d, R)
    p.barrier("wcomb")

    p.emit()
    return p


def fgu_phase(p, xn_buf, Ttok, qguT, sgu_cols, zguT, h_s, R):
    STn = Ttok // 512
    pb = R["pb"]
    s_q = p.sem("fgu_q")
    s_zs = p.sem("fgu_zs")
    s_nib = p.sem("fgu_nib")
    s_deq = p.sem("fgu_deq")
    s_wtr = [p.sem(f"fgu_wtr{i}") for i in range(4)]
    s_peh = p.sem("fgu_peh")
    s_sil = p.sem("fgu_sil")
    s_ue = p.sem("fgu_ue")
    s_h = p.sem("fgu_h")
    s_sph = p.sem("fgu_sph")

    qg, cqg = p.sb("fgu_qg", [128, HID], I32)
    qu, cqu = p.sb("fgu_qu", [128, HID], I32)
    scg, csc1 = p.sb("fgu_scg", [128, 256], F32)
    scu, csc2 = p.sb("fgu_scu", [128, 256], F32)
    zg, cz1 = p.sb("fgu_zg", [128, 32], I32)
    zu, cz2 = p.sb("fgu_zu", [128, 32], I32)
    zsg, czs1 = p.sb("fgu_zsg", [128, 256], F32)
    zsu, czs2 = p.sb("fgu_zsu", [128, 256], F32)
    nibz, cnz = p.sb("fgu_nibz", [128, 256], I32)
    nib, cnib = p.sb("fgu_nib", [128, 2 * 2 * 512], I32)
    wtT, cwt = p.sb("fgu_wtT", [128, 2 * 2 * 512], BF16)
    wk, cwk = p.sb("fgu_wk", [128, 2 * 8 * 128], BF16)
    sg, csg = p.sb("fgu_sg", [128, Ttok], BF16)
    ub, cub = p.sb("fgu_ub", [128, Ttok], BF16)
    hst, chst = p.sb("fgu_hst", [128, Ttok], BF16)
    frees = [cqg, cqu, csc1, csc2, cz1, cz2, czs1, czs2, cnz, cnib, cwt, cwk,
             csg, cub, chst]

    NPAIR = GU_PAIR = 11

    def peh_idx(pair, sp, side, hc):
        return ((pair * 8 + sp) * 2 + side) * 32 + hc + 1

    def ks_idx(pair, sp, side, ks):
        return ((pair * 8 + sp) * 2 + side) * 8 + ks + 1

    def zs_idx(pair, side, sp):
        return pair * 16 + side * 8 + sp + 1

    def nibsl(side, slot):
        return nib[:, (side * 2 + slot) * 512:(side * 2 + slot + 1) * 512]

    def wtsl(side, slot):
        return wtT[:, (side * 2 + slot) * 512:(side * 2 + slot + 1) * 512]

    def wksl(side, slot):
        return wk[:, (side * 8 + slot) * 128:(side * 8 + slot + 1) * 128]

    def sync_fn(q):
        for pair in range(NPAIR):
            if pair >= 1:
                q.wait_ge(s_peh, peh_idx(pair - 1, 7, 1, 31))
            for dst, src in [(qg[:], qguT[pair]), (qu[:], qguT[pair + 11]),
                             (scg[:], sgu_cols[:, pair * 256:(pair + 1) * 256]),
                             (scu[:], sgu_cols[:, (pair + 11) * 256:(pair + 12) * 256]),
                             (zg[:], zguT[pair]), (zu[:], zguT[pair + 11])]:
                q.dma_start(out=dst, in_=src).then_inc(s_q, 16)
            for sp in range(8):
                for side in range(2):
                    for ks in range(8):
                        zks = ks_idx(pair, sp, side, ks)
                        zz = (pair * 8 + sp) * 8 + ks
                        q.wait_ge(s_deq, zks)
                        for j in range(4):
                            hc = ks * 4 + j
                            if peh_idx(pair, sp, side, hc) - 8 >= 1:
                                q.wait_ge(s_peh, peh_idx(pair, sp, side, hc) - 8)
                            q.dma_start(out=wksl(side, hc % 8),
                                        in_=wtsl(side, ks % 2)[:, j * 128:(j + 1) * 128],
                                        transpose=True).then_inc(s_wtr[side * 2 + zz % 2], 16)
                gsp = pair * 8 + sp
                q.wait_ge(s_h, gsp + 1)
                q.dma_start(out=h_s[gsp], in_=hst[:, 0:Ttok]).then_inc(s_sph, 16)
        q.wait_ge(s_sph, 16 * NPAIR * 8)
        tot = NPAIR * 8 * 8
        for side in range(2):
            for par in range(2):
                n = len([z for z in range(tot) if z % 2 == par])
                q.wait_ge(s_wtr[side * 2 + par], 16 * 4 * n)

    def v_fn(v):
        for pair in range(NPAIR):
            v.wait_ge(s_q, 16 * 6 * (pair + 1))
            if pair >= 1:
                v.wait_ge(s_deq, ks_idx(pair - 1, 7, 1, 7))
            for side in range(2):
                zt = zg if side == 0 else zu
                sct = scg if side == 0 else scu
                zst = zsg if side == 0 else zsu
                base = (pair * 2 + side) * 2
                if base >= 1:
                    v.wait_ge(s_zs, base)
                for sp in range(8):
                    inst = v.tensor_scalar(nibz[:, sp * 32:(sp + 1) * 32], zt[:], 4 * sp, 15,
                                           ALU.logical_shift_right, ALU.bitwise_and)
                inst.then_inc(s_zs, 1)
                v.wait_ge(s_zs, base + 1)
                for sp in range(8):
                    inst = v.scalar_tensor_tensor(zst[:, sp * 32:(sp + 1) * 32],
                                                  nibz[:, sp * 32:(sp + 1) * 32], -1.0,
                                                  sct[:, sp * 32:(sp + 1) * 32],
                                                  ALU.mult, ALU.mult)
                inst.then_inc(s_zs, 1)
            for sp in range(8):
                for side in range(2):
                    qt = qg if side == 0 else qu
                    for ks in range(8):
                        gks = ks_idx(pair, sp, side, ks)
                        if gks >= 3:
                            v.wait_ge(s_deq, gks - 2)
                        v.tensor_scalar(nibsl(side, ks % 2), qt[:, ks * 512:(ks + 1) * 512],
                                        4 * sp, 15, ALU.logical_shift_right,
                                        ALU.bitwise_and).then_inc(s_nib, 1)
                gsp = pair * 8 + sp
                v.wait_ge(s_peh, peh_idx(pair, sp, 1, 31))
                if gsp >= 1:
                    v.wait_ge(s_h, gsp)
                for st in range(STn):
                    v.tensor_copy(ub[:, st * 512:(st + 1) * 512], pb[4 + st][:, :]).then_inc(s_ue, 1)

    def sc_fn(sc):
        for pair in range(NPAIR):
            for sp in range(8):
                for side in range(2):
                    zst = zsg if side == 0 else zsu
                    sct = scg if side == 0 else scu
                    for ks in range(8):
                        gks = ks_idx(pair, sp, side, ks)
                        zz = (pair * 8 + sp) * 8 + ks
                        sc.wait_ge(s_nib, gks)
                        sc.wait_ge(s_zs, (pair * 2 + side) * 2 + 2)
                        if zz >= 2:
                            zp = zz - 2
                            cls = (zp % 2) if side == 0 else (2 + zp % 2)
                            sc.wait_ge(s_wtr[cls], 16 * 4 * (zp // 2 + 1))
                        for j in range(4):
                            g = ks * 4 + j
                            inst = sc.activation(wtsl(side, ks % 2)[:, j * 128:(j + 1) * 128],
                                                 nibsl(side, ks % 2)[:, j * 128:(j + 1) * 128],
                                                 AF.Identity,
                                                 bias=zst[:, sp * 32 + g:sp * 32 + g + 1],
                                                 scale=sct[:, sp * 32 + g:sp * 32 + g + 1])
                        inst.then_inc(s_deq, 1)
                gsp = pair * 8 + sp
                sc.wait_ge(s_peh, peh_idx(pair, sp, 0, 31))
                if gsp >= 1:
                    sc.wait_ge(s_h, gsp)
                for st in range(STn):
                    sc.activation(sg[:, st * 512:(st + 1) * 512], pb[st][:, :],
                                  AF.Silu, bias=0.0, scale=1.0).then_inc(s_sil, 1)


    def pe_fn(t):
        for pair in range(NPAIR):
            for sp in range(8):
                gsp = pair * 8 + sp
                for side in range(2):
                    if gsp >= 1:
                        if side == 0:
                            t.wait_ge(s_sil, STn * gsp)
                        else:
                            t.wait_ge(s_ue, STn * gsp)
                    for hc in range(32):
                        zz = (pair * 8 + sp) * 8 + hc // 4
                        cls = (zz % 2) if side == 0 else (2 + zz % 2)
                        t.wait_ge(s_wtr[cls], 16 * 4 * (zz // 2 + 1))
                        for st in range(STn):
                            inst = t.matmul(pb[side * 4 + st][:, :],
                                            wksl(side, hc % 8),
                                            xn_buf[:, hc * Ttok + st * 512:hc * Ttok + (st + 1) * 512],
                                            start=(hc == 0), stop=(hc == 31))
                        inst.then_inc(s_peh, 1)

    def gp_fn(gp):
        for pair in range(NPAIR):
            for sp in range(8):
                gsp = pair * 8 + sp
                gp.wait_ge(s_sil, STn * (gsp + 1))
                gp.wait_ge(s_ue, STn * (gsp + 1))
                if gsp >= 1:
                    gp.wait_ge(s_sph, 16 * gsp)
                gp.tensor_mul(hst[:, 0:Ttok], sg[:, 0:Ttok], ub[:, 0:Ttok]).then_inc(s_h, 1)

    p.phase("fgu", sync=sync_fn, vector=v_fn, scalar=sc_fn, tensor=pe_fn, gpsimd=gp_fn)
    return dict(frees=frees)



def kernel(**inputs):
    n_main, n_prefix = 16, 16
    in_maps = host_prep(inputs, n_main, n_prefix)
    p = build(n_main, n_prefix, debug=False)
    res = run_bass_kernel_spmd(p.nc, in_maps, core_ids=list(range(8)))
    out = np.empty((4, 4096, 4096), np.float32)
    for core in range(8):
        b, hf = core // 2, core % 2
        out[b, 2048 * hf:2048 * (hf + 1)] = res.results[core]["out"]
    return out


# version 09
# speedup vs baseline: 1.3995x; 1.1044x over previous
import numpy as np
import ml_dtypes
from contextlib import ExitStack

import concourse.bass as bass
import concourse.mybir as mybir
from concourse.bass_utils import run_bass_kernel_spmd

F32 = mybir.dt.float32
BF16 = mybir.dt.bfloat16
I32 = mybir.dt.int32
AF = mybir.ActivationFunctionType
ALU = mybir.AluOpType
AX = mybir.AxisListType

H, KVH, D, CHUNK = 32, 8, 128, 128
HID = 4096
L = 4096
ROPE_THETA = 10000.0
RMS_EPS = 1e-5
LN_EPS = 1e-5
NEG = -1e30

GU_CT = 22
NPLANE = 8
DN_CT = 4
HB = 88

BFnp = ml_dtypes.bfloat16



def _rope_tables_fm():
    inv_freq = (1.0 / ROPE_THETA ** (np.arange(0, D, 2, dtype=np.float32) / D)).astype(np.float32)
    ang = np.arange(L, dtype=np.float32)[:, None] * inv_freq[None, :]
    cos = np.concatenate([np.cos(ang), np.cos(ang)], axis=-1)
    sin = np.concatenate([np.sin(ang), np.sin(ang)], axis=-1)
    sinm = sin.copy()
    sinm[:, :64] *= -1.0
    return (np.ascontiguousarray(cos.T).astype(BFnp),
            np.ascontiguousarray(sinm.T).astype(BFnp))


def _wblocks(W):
    FO, K = W.shape
    return np.ascontiguousarray(W.reshape(FO // 128, 128, K // 128, 128).transpose(0, 3, 2, 1)).astype(np.float32)


def _gu_perm_c():
    perm = np.full(GU_CT * 128, -1, np.int64)
    perm[0:1376] = np.arange(0, 1376)
    perm[1408:1408 + 1376] = np.arange(1376, 2752)
    return perm


def _dn_perm_i():
    pos = np.arange(HB * 128)
    ct, sp, c = pos // 1024, (pos // 128) % 8, pos % 128
    corig = 128 * ct + c
    return np.where(corig < 1376, 8 * corig + sp, -1)


def host_prep(inputs, n_main=16, n_prefix=16):
    x = np.asarray(inputs["L0_attn_hidden_states"])
    Wq = np.asarray(inputs["L0_attn_Wq"]); Wk = np.asarray(inputs["L0_attn_Wk"])
    Wv = np.asarray(inputs["L0_attn_Wv"]); Wo = np.asarray(inputs["L0_attn_Wo"])
    Wg = np.asarray(inputs["L0_attn_Wg"]); Win = np.asarray(inputs["L0_attn_Win"])

    sh = {}
    sh["wq_r"] = _wblocks(Wq)
    sh["wk_r"] = _wblocks(Wk)
    sh["wv_r"] = _wblocks(Wv)
    sh["wg_r"] = _wblocks(Wg)
    sh["wo_r"] = _wblocks(Wo.T)
    sh["win_r"] = np.ascontiguousarray(Win.reshape(32, 32, 128).transpose(2, 1, 0)).astype(np.float32)[None]
    sh["dt_bias_col"] = np.asarray(inputs["L0_attn_dt_bias"]).reshape(32, 1).astype(np.float32)
    sh["albias_col"] = np.asarray(inputs["L0_attn_A_log_bias"]).reshape(32, 1).astype(np.float32)
    sh["gnw_col"] = np.asarray(inputs["L0_attn_gnorm_w"]).reshape(128, 1).astype(np.float32)
    sh["c_ident"] = np.eye(128, dtype=np.float32)
    sh["c_identb"] = np.eye(128, dtype=np.float32).astype(BFnp)
    sh["c_ones1"] = np.ones((128, 1), np.float32)
    sh["c_onesr"] = np.ones((1, 128), np.float32)
    ij = np.arange(128)
    sh["c_u"] = (ij[:, None] <= ij[None, :]).astype(np.float32)
    sh["c_mneg"] = np.where(ij[:, None] <= ij[None, :], 0.0, NEG).astype(np.float32)
    sh["c_eps"] = np.full((128, 2), 1.0, np.float32)
    sh["c_eps"][:, 0] = LN_EPS
    rsel = np.zeros((32, 32 * 128), np.float32)
    for k in range(32):
        rsel[k, 128 * k:128 * (k + 1)] = 1.0
    sh["c_rsel"] = rsel

    qgu = np.asarray(inputs["L0_ffn_qweight_gu"])
    sgu = np.asarray(inputs["L0_ffn_scales_gu"], np.float32)
    zgu = np.asarray(inputs["L0_ffn_qzeros_gu"])
    permc = _gu_perm_c()
    valid = permc >= 0
    qguT = np.zeros((GU_CT * 128, HID), np.int32)
    qguT[valid] = np.ascontiguousarray(qgu.T)[permc[valid]]
    sh["qguT"] = qguT.reshape(GU_CT, 128, HID)
    scol = np.zeros((128, GU_CT, NPLANE, 32), np.float32)
    pm = permc.reshape(GU_CT, 128)
    for ct in range(GU_CT):
        v = pm[ct] >= 0
        n = 8 * pm[ct][v][:, None] + np.arange(8)[None, :]
        scol[v, ct, :, :] = sgu.T[n]
    sh["sgu_cols"] = np.ascontiguousarray(scol.reshape(128, GU_CT * NPLANE * 32))
    zguT = np.zeros((GU_CT * 128, 32), np.int32)
    zguT[valid] = np.ascontiguousarray(zgu.T)[permc[valid]]
    sh["zguT"] = zguT.reshape(GU_CT, 128, 32)

    qdn = np.asarray(inputs["L0_ffn_qweight_dn"])
    sdn = np.asarray(inputs["L0_ffn_scales_dn"], np.float32)
    zdn = np.asarray(inputs["L0_ffn_qzeros_dn"])
    permi = _dn_perm_i()
    vi = permi >= 0
    qdnp = np.zeros((HB * 128, 512), np.int32)
    qdnp[vi] = qdn[permi[vi]]
    sh["qdnT"] = np.ascontiguousarray(qdnp.T).reshape(DN_CT, 128, HB * 128)
    sdnp = np.zeros((HB, HID), np.float32)
    sdnp[:86] = sdn
    dcol = np.zeros((128, DN_CT, NPLANE, HB), np.float32)
    for ct in range(DN_CT):
        n = 8 * (128 * ct + np.arange(128))[:, None] + np.arange(8)[None, :]
        dcol[:, ct, :, :] = sdnp.T[n]
    sh["sdn_cols"] = np.ascontiguousarray(dcol.reshape(128, DN_CT * NPLANE * HB))
    zdnp = np.zeros((HB, 512), np.int32)
    zdnp[:86] = zdn
    sh["zdnT"] = np.ascontiguousarray(zdnp.T)

    cos, sinm = _rope_tables_fm()
    TM, TP = n_main * CHUNK, n_prefix * CHUNK
    in_maps = []
    for core in range(8):
        b, hf = core // 2, core % 2
        off = 2048 * hf
        m = dict(sh)
        m["x_main"] = np.ascontiguousarray(x[b, off:off + TM]).astype(np.float32)
        m["ropec_m"] = np.ascontiguousarray(cos[:, off:off + TM])
        m["ropes_m"] = np.ascontiguousarray(sinm[:, off:off + TM])
        if TP > 0:
            poff = off - TP if hf == 1 else 0
            if hf == 1:
                m["x_prefix"] = np.ascontiguousarray(x[b, poff:off]).astype(np.float32)
            else:
                m["x_prefix"] = np.zeros((TP, HID), np.float32)
            m["ropec_p"] = np.ascontiguousarray(cos[:, poff:poff + TP])
            m["ropes_p"] = np.ascontiguousarray(sinm[:, poff:poff + TP])
        in_maps.append(m)
    return in_maps



class P:
    def __init__(self, n_main, n_prefix, debug):
        self.nc = bass.Bass("TRN2", target_bir_lowering=False, debug=False)
        self.ctx = ExitStack()
        self.n_main, self.n_prefix, self.debug = n_main, n_prefix, debug
        self.TM, self.TP = n_main * CHUNK, n_prefix * CHUNK
        self.phases = []
        self._frees = []

    def sb(self, name, shape, dtype):
        cm = self.nc.sbuf_tensor(name, shape, dtype)
        t = cm.__enter__()
        self._frees.append(cm)
        return t, cm

    @staticmethod
    def free(cm):
        cm.__exit__(None, None, None)

    def sem(self, name):
        return self.ctx.enter_context(self.nc.semaphore(name))

    def din(self, name, shape, dtype):
        return self.nc.dram_tensor(name, shape, dtype, kind="ExternalInput").ap()

    def dout(self, name, shape, dtype):
        return self.nc.dram_tensor(name, shape, dtype, kind="ExternalOutput").ap()

    def dtmp(self, name, shape, dtype):
        kind = "ExternalOutput" if self.debug else "Internal"
        return self.nc.dram_tensor(name, shape, dtype, kind=kind).ap()

    def phase(self, name, **emits):
        self.phases.append((name, emits))

    def barrier(self, name):
        if not hasattr(self, "_barsems"):
            self._barsems = [self.sem(f"bar{i}") for i in range(5)]
            self._barcnt = 0
        self._barcnt += 1
        k = self._barcnt
        sems = self._barsems

        def mk(i):
            def f(eng):
                eng.sem_inc(sems[i], 1)
                for j in range(5):
                    eng.wait_ge(sems[j], k)
            return f
        self.phase("bar_" + name, sync=mk(0), vector=mk(1), scalar=mk(2),
                   tensor=mk(3), gpsimd=mk(4))

    def emit(self):
        with self.nc.Block() as block:
            @block.sync
            def _(q):
                for _n, em in self.phases:
                    if "sync" in em:
                        em["sync"](q)

            @block.vector
            def _(v):
                for _n, em in self.phases:
                    if "vector" in em:
                        em["vector"](v)

            @block.scalar
            def _(sc):
                for _n, em in self.phases:
                    if "scalar" in em:
                        em["scalar"](sc)

            @block.tensor
            def _(t):
                for _n, em in self.phases:
                    if "tensor" in em:
                        em["tensor"](t)

            @block.gpsimd
            def _(gp):
                for _n, em in self.phases:
                    if "gpsimd" in em:
                        em["gpsimd"](gp)



def ln_phase(p, tag, nchunks, x_dram, xn_buf, Ttok, eps_sb=None, R=None):
    s_x1 = [p.sem(f"{tag}_x1a"), p.sem(f"{tag}_x1b")]
    s_x2 = s_x1
    npar = [(nchunks + 1) // 2, nchunks // 2]
    s_sum = p.sem(f"{tag}_sum")
    s_ssq = p.sem(f"{tag}_ssq")
    s_ch = p.sem(f"{tag}_ch")
    s_ln = p.sem(f"{tag}_ln")
    s_pt = p.sem(f"{tag}_pt")
    s_tr = p.sem(f"{tag}_tr")

    xbuf, cm_xb = p.sb(f"{tag}_xb", [128, 2 * HID], F32)
    sums, cm1 = p.sb(f"{tag}_sums", [128, nchunks], F32)
    ssqs, cm2 = p.sb(f"{tag}_ssqs", [128, nchunks], F32)
    mean, cm3 = p.sb(f"{tag}_mean", [128, nchunks], F32)
    var, cm4 = p.sb(f"{tag}_var", [128, nchunks], F32)
    rstd, cm5 = p.sb(f"{tag}_rstd", [128, nchunks], F32)
    nbias, cm6 = p.sb(f"{tag}_nbias", [128, nchunks], F32)
    xn_tm, cm_xn = p.sb(f"{tag}_xntm", [128, 2 * HID], BF16)
    sqscr = xn_tm
    frees = [cm_xb, cm1, cm2, cm3, cm4, cm5, cm6, cm_xn]

    def xsl(i):
        return xbuf[:, (i % 2) * HID:(i % 2 + 1) * HID]

    def xnsl(i):
        return xn_tm[:, (i % 2) * HID:(i % 2 + 1) * HID]

    def sync_fn(q):
        for tc in range(nchunks):
            if tc >= 2:
                q.wait_ge(s_ssq, tc - 1)
                q.wait_ge(s_sum, tc - 1)
            q.dma_start(out=xsl(tc), in_=x_dram[tc * 128:(tc + 1) * 128, :]).then_inc(s_x1[tc % 2], 16)
        for tc in range(nchunks):
            if tc == 0:
                q.wait_ge(s_sum, nchunks)
                q.wait_ge(s_ssq, nchunks)
            if tc >= 2:
                q.wait_ge(s_ln, tc - 1)
            q.dma_start(out=xsl(tc), in_=x_dram[tc * 128:(tc + 1) * 128, :]).then_inc(s_x2[tc % 2], 16)

    def v_fn(v):
        pbt = [R["pb"][4 + k].ap().bitcast(BF16) for k in range(4)]
        for tc in range(nchunks):
            v.wait_ge(s_x1[tc % 2], 16 * (tc // 2 + 1))
            v.reduce_sum(sums[:, tc:tc + 1], xsl(tc), axis=AX.X).then_inc(s_sum, 1)
        v.wait_ge(s_ssq, nchunks)
        v.wait_ge(s_sum, nchunks)
        v.tensor_scalar_mul(mean[:], sums[:], 1.0 / HID).then_inc(s_ch, 1)
        v.wait_ge(s_ch, 1)
        v.tensor_mul(var[:], mean[:], mean[:]).then_inc(s_ch, 1)
        v.wait_ge(s_ch, 2)
        v.scalar_tensor_tensor(var[:], ssqs[:], 1.0 / HID, var[:], ALU.mult, ALU.subtract).then_inc(s_ch, 1)
        v.wait_ge(s_ch, 5)
        v.scalar_tensor_tensor(nbias[:], mean[:], -1.0, rstd[:], ALU.mult, ALU.mult).then_inc(s_ch, 1)
        for tc in range(nchunks):
            for hc in range(32):
                tt = tc * 32 + hc
                v.wait_ge(s_pt, tt + 1)
                v.tensor_copy(xn_buf[:, hc * Ttok + tc * 128:hc * Ttok + (tc + 1) * 128],
                              pbt[tt % 4][:, 0:128]).then_inc(s_tr, 1)

    def sc_fn(sc):
        for tc in range(nchunks):
            sc.wait_ge(s_x1[tc % 2], 16 * (tc // 2 + 1))
            if tc >= 1:
                sc.wait_ge(s_ssq, tc)
            sc.activation(sqscr[:, 0:HID], xsl(tc), AF.Square, bias=0.0, scale=1.0,
                          accum_out=ssqs[:, tc:tc + 1]).then_inc(s_ssq, 1)
        sc.wait_ge(s_ch, 3)
        sc.activation(nbias[:], var[:], AF.Ln, bias=eps_sb[:, 0:1], scale=1.0).then_inc(s_ch, 1)
        sc.wait_ge(s_ch, 4)
        sc.activation(rstd[:], nbias[:], AF.Exp, bias=0.0, scale=-0.5).then_inc(s_ch, 1)
        sc.wait_ge(s_ch, 6)
        sc.wait_ge(s_ssq, nchunks)
        for tc in range(nchunks):
            sc.wait_ge(s_x2[tc % 2], 16 * (npar[tc % 2] + tc // 2 + 1))
            if tc >= 2:
                sc.wait_ge(s_pt, 32 * (tc - 1))
            sc.activation(xnsl(tc), xsl(tc), AF.Identity,
                          bias=nbias[:, tc:tc + 1], scale=rstd[:, tc:tc + 1]).then_inc(s_ln, 1)

    def pe_fn(t):
        identb = R["identb_sb"]
        for tc in range(nchunks):
            t.wait_ge(s_ln, tc + 1)
            for hc in range(32):
                tt = tc * 32 + hc
                if tt >= 4:
                    t.wait_ge(s_tr, tt - 3)
                t.transpose(pbt_pe[tt % 4][:, 0:128],
                            xnsl(tc)[:, hc * 128:(hc + 1) * 128],
                            identb[:]).then_inc(s_pt, 1)

    pbt_pe = [R["pb"][4 + k].ap().bitcast(BF16) for k in range(4)]
    p.phase(tag, sync=sync_fn, vector=v_fn, scalar=sc_fn, tensor=pe_fn)
    return dict(sem=s_tr, n=32 * nchunks, frees=frees)



def proj_phase(p, tag, nchunks, xn_buf, Ttok, groups, ln_dep, ropec_d, ropes_d, R):
    STn = Ttok // 512
    assert Ttok % 512 == 0 and STn <= 4
    s_w = [p.sem(f"{tag}_wa"), p.sem(f"{tag}_wb2")]
    s_pe = p.sem(f"{tag}_pe")
    s_ev = p.sem(f"{tag}_ev")
    s_rp1 = p.sem(f"{tag}_rp1")
    s_rp = p.sem(f"{tag}_rp")
    s_sp = p.sem(f"{tag}_sp")
    s_ktr = p.sem(f"{tag}_ktr")
    s_rop = p.sem(f"{tag}_rop")

    wbuf, cw = p.sb(f"{tag}_wb", [128, 2 * 32 * 128], BF16)
    ev_scr, ce = p.sb(f"{tag}_ev", [128, 512], F32)
    t1, c1 = p.sb(f"{tag}_t1", [128, 512], F32)
    t2, c2 = p.sb(f"{tag}_t2", [128, 512], F32)
    sw, c2b = p.sb(f"{tag}_sw", [128, 512], F32)
    stage, c3 = p.sb(f"{tag}_stg", [128, Ttok], BF16)
    ktm_stage, c4 = p.sb(f"{tag}_ktms", [128, Ttok], BF16)
    ropec, c5 = p.sb(f"{tag}_ropec", [128, Ttok], BF16)
    ropes, c6 = p.sb(f"{tag}_ropes", [128, Ttok], BF16)
    frees = [cw, ce, c1, c2, c2b, c3, c4, c5, c6]

    pb = R["pb"]

    seq = []
    for g in groups:
        for fo in range(g["n_fo"]):
            seq.append((g, fo))

    ev_cnt, rp_cnt, sp_cnt, ktr_cnt = {}, {}, {}, {}
    c_ev = c_rp = c_sp = c_ktr = 0
    for si, (g, fo) in enumerate(seq):
        for st in range(STn):
            c_ev += 1
            ev_cnt[(si, st)] = c_ev
            if g["kind"] in ("q", "k"):
                c_rp += 1
                rp_cnt[(si, st)] = c_rp
        k = g["kind"]
        if k in ("q", "g"):
            c_sp += 16
        elif k == "k":
            c_sp += 32 if g.get("dst") is not None else 16
            c_ktr += 16 * nchunks
        elif k == "v":
            c_sp += 16
            c_ktr += 16 * nchunks
        sp_cnt[si] = c_sp
        ktr_cnt[si] = c_ktr

    def prev_sp(si):
        return sp_cnt[si - 1] if si >= 1 else 0

    def gp_fn(gp):
        for si, (g, fo) in enumerate(seq):
            if si >= 2:
                gp.wait_ge(s_pe, (si - 1) * STn)
            fw = g["fo_cols"]
            gp.dma_start(out=wbuf[:, (si % 2) * 4096:(si % 2) * 4096 + 32 * fw],
                         in_=g["w"][fo].rearrange("p hc c -> p (hc c)")).then_inc(s_w[si % 2], 16)

    def pe_fn(t):
        t.wait_ge(ln_dep["sem"], ln_dep["n"])
        for si, (g, fo) in enumerate(seq):
            t.wait_ge(s_w[si % 2], 16 * (si // 2 + 1))
            if si >= 2:
                t.wait_ge(s_ev, ev_cnt[(si - 2, STn - 1)])
            fw = g["fo_cols"]
            for st in range(STn):
                bank = pb[(si % 2) * 4 + st]
                for hc in range(32):
                    inst = t.matmul(bank[0:fw, :],
                                    wbuf[:, (si % 2) * 4096 + hc * fw:(si % 2) * 4096 + (hc + 1) * fw],
                                    xn_buf[:, hc * Ttok + st * 512:hc * Ttok + (st + 1) * 512],
                                    start=(hc == 0), stop=(hc == 31))
                inst.then_inc(s_pe, 1)

    def sc_fn(sc):
        for si, (g, fo) in enumerate(seq):
            kind = g["kind"]
            for st in range(STn):
                sc.wait_ge(s_pe, si * STn + st + 1)
                bank = pb[(si % 2) * 4 + st]
                if kind in ("q", "k"):
                    i = rp_cnt[(si, st)]
                    if i >= 2:
                        sc.wait_ge(s_rp1, 2 * i - 3)
                    sc.activation(ev_scr[:], bank[:, :], AF.Identity,
                                  bias=0.0, scale=1.0).then_inc(s_ev, 1)
                elif kind in ("v", "g"):
                    if st == 0:
                        sc.wait_ge(s_sp, prev_sp(si))
                    sc.activation(stage[:, st * 512:(st + 1) * 512], bank[:, :],
                                  AF.Identity, bias=0.0, scale=1.0).then_inc(s_ev, 1)
                else:
                    sc.activation(R["dt_raw"][0:32, g["dt_off"] + st * 512:g["dt_off"] + (st + 1) * 512],
                                  bank[0:32, :], AF.Identity,
                                  bias=R["dtb_sb"][:, 0:1], scale=1.0).then_inc(s_ev, 1)

    def v_fn(v):
        v.wait_ge(s_rop, 32)
        for si, (g, fo) in enumerate(seq):
            if g["kind"] not in ("q", "k"):
                continue
            for st in range(STn):
                i = rp_cnt[(si, st)]
                v.wait_ge(s_ev, ev_cnt[(si, st)])
                if i >= 2:
                    v.wait_ge(s_rp, i - 1)
                    v.wait_ge(s_rp1, 2 * (i - 1))
                if st == 0:
                    v.wait_ge(s_sp, prev_sp(si))
                cs = ropec[:, st * 512:(st + 1) * 512]
                ss = ropes[:, st * 512:(st + 1) * 512]
                v.tensor_copy(sw[0:64, :], ev_scr[64:128, :])
                v.tensor_copy(sw[64:128, :], ev_scr[0:64, :])
                v.tensor_mul(t1[:], ev_scr[:], cs).then_inc(s_rp1, 1)
                v.wait_ge(s_rp1, 2 * i - 1)
                v.tensor_mul(t2[:], sw[:], ss).then_inc(s_rp1, 1)
                v.wait_ge(s_rp1, 2 * i)
                v.tensor_add(stage[:, st * 512:(st + 1) * 512], t1[:], t2[:]).then_inc(s_rp, 1)

    def sync_fn(q):
        q.dma_start(out=ropec[:], in_=ropec_d[:, 0:Ttok]).then_inc(s_rop, 16)
        q.dma_start(out=ropes[:], in_=ropes_d[:, 0:Ttok]).then_inc(s_rop, 16)
        for si, (g, fo) in enumerate(seq):
            kind = g["kind"]
            if kind == "win":
                continue
            if kind in ("q", "k"):
                q.wait_ge(s_rp, rp_cnt[(si, STn - 1)])
            else:
                q.wait_ge(s_ev, ev_cnt[(si, STn - 1)])
            src = stage.rearrange("p (t c) -> p t c", c=128)
            if kind in ("q", "g", "k") and g.get("dst") is not None:
                dst = (g["dst"].rearrange("t p (f c) -> t p f c", c=128)
                       [g["tc0"]:g["tc0"] + nchunks, :, fo, :].transpose([1, 0, 2]))
                q.dma_start(out=dst, in_=src).then_inc(s_sp, 16)
            if kind in ("k", "v"):
                if si >= 1:
                    q.wait_ge(s_sp, prev_sp(si))
                for tc in range(nchunks):
                    q.dma_start(out=ktm_stage[:, tc * 128:(tc + 1) * 128],
                                in_=stage[:, tc * 128:(tc + 1) * 128],
                                transpose=True).then_inc(s_ktr, 16)
                q.wait_ge(s_ktr, ktr_cnt[si])
                dst2 = (g["dst2"].rearrange("t p (f c) -> t p f c", c=128)
                        [g["ktc0"]:g["ktc0"] + nchunks, :, fo, :].transpose([1, 0, 2]))
                q.dma_start(out=dst2, in_=ktm_stage.rearrange("p (t c) -> p t c", c=128)).then_inc(s_sp, 16)
        q.wait_ge(s_sp, c_sp)
        if c_ktr:
            q.wait_ge(s_ktr, c_ktr)

    p.phase(tag, sync=sync_fn, gpsimd=gp_fn, scalar=sc_fn, tensor=pe_fn, vector=v_fn)
    return dict(frees=frees, s_sp=s_sp, n_sp=c_sp, s_ev=s_ev, n_ev=c_ev)




def fdn_phase(p, TM, qdnT, sdn_cols, zdnT, h_s, ffn_s, R):
    TH = TM // 2 if TM >= 1024 else TM
    NHALF = TM // TH
    ST2 = TH // 512
    assert TH % 512 == 0 and ST2 <= 2
    pb = R["pb"]
    identb = R["identb_sb"]
    pbt = [pb[4 + k].ap().bitcast(BF16) for k in range(4)]
    s_hld = p.sem("fdn_hld")
    s_qd = [p.sem("fdn_qda"), p.sem("fdn_qdb")]
    s_zc = p.sem("fdn_zc")
    s_zs = p.sem("fdn_zs")
    s_nx = p.sem("fdn_nx")
    s_dq = p.sem("fdn_dq")
    s_pet = p.sem("fdn_pet")
    s_te = p.sem("fdn_te")
    s_pe = p.sem("fdn_pe")
    s_ev = p.sem("fdn_ev")
    s_sp = [p.sem("fdn_spa"), p.sem("fdn_spb")]

    hbuf, ch = p.sb("fdn_h", [128, HB * TH], BF16)
    qsub, cq = p.sb("fdn_qsub", [128, 2 * 1024], I32)
    nib, cn = p.sb("fdn_nib", [128, 2 * 1024], I32)
    wtT, cw = p.sb("fdn_wtT", [128, 2 * 1024], BF16)
    wkd, ck = p.sb("fdn_wkd", [128, 2 * 8 * 128], BF16)
    scc, cs1 = p.sb("fdn_scc", [128, 2 * HB], F32)
    zc, cs2 = p.sb("fdn_zc", [128, HB], I32)
    nibz = nib
    zsc, cs4 = p.sb("fdn_zsc", [128, 2 * HB], F32)
    stage, cst = p.sb("fdn_stage", [128, 2 * 512], BF16)
    frees = [ch, cq, cn, cw, ck, cs1, cs2, cs4, cst]

    NIS = 11
    npass = NHALF * DN_CT * 4
    nb = 2 * ST2

    def it_idx(half, ct, pg, isub):
        return ((half * DN_CT + ct) * 4 + pg) * NIS + isub

    def pass_idx(half, ct, pg):
        return (half * DN_CT + ct) * 4 + pg

    def sync_fn(q):
        for half in range(NHALF):
            if half >= 1:
                q.wait_ge(s_pe, 4 * (it_idx(half - 1, DN_CT - 1, 3, NIS - 1) + 1))
            for hb in range(HB):
                q.dma_start(out=hbuf[:, hb * TH:(hb + 1) * TH],
                            in_=h_s[hb][:, half * TH:half * TH + TH]).then_inc(s_hld, 16)
            for ct in range(DN_CT):
                for pg in range(4):
                    pidx = pass_idx(half, ct, pg)
                    if pidx >= 1:
                        q.wait_ge(s_dq, 2 * NIS * pidx)
                    if pg == 0:
                        q.dma_start(out=scc[:], in_=sdn_cols[:, ct * NPLANE * HB + pg * 2 * HB:
                                                             ct * NPLANE * HB + (pg * 2 + 2) * HB]).then_inc(s_zc, 16)
                        q.dma_start(out=zc[:], in_=zdnT[ct * 128:(ct + 1) * 128, :]).then_inc(s_zc, 16)
                    else:
                        q.dma_start(out=scc[:], in_=sdn_cols[:, ct * NPLANE * HB + pg * 2 * HB:
                                                             ct * NPLANE * HB + (pg * 2 + 2) * HB]).then_inc(s_zc, 32)
                    for isub in range(NIS):
                        gi = it_idx(half, ct, pg, isub)
                        if gi >= 2:
                            q.wait_ge(s_nx, 2 * (gi - 1))
                        q.dma_start(out=qsub[:, (gi % 2) * 1024:(gi % 2 + 1) * 1024],
                                    in_=qdnT[ct][:, isub * 1024:(isub + 1) * 1024]).then_inc(s_qd[gi % 2], 16)
                    for spl in range(2):
                        sp_ = pg * 2 + spl
                        for st in range(ST2):
                            t0 = half * TH + st * 512
                            dst2 = ffn_s[8 * ct:8 * ct + 8, :, t0:t0 + 512] \
                                .rearrange("a (pl e) t -> a pl e t", e=8)[:, :, sp_, :]
                            ev_i = pidx * nb + spl * ST2 + st
                            src = stage[:, (ev_i % 2) * 512:(ev_i % 2 + 1) * 512]
                            q.wait_ge(s_ev, ev_i + 1)
                            q.dma_start(out=dst2, in_=src).then_inc(s_sp[ev_i % 2], 16)
        ntot = npass * nb
        q.wait_ge(s_sp[0], 16 * ((ntot + 1) // 2))
        q.wait_ge(s_sp[1], 16 * (ntot // 2))

    def _evacs(v, ei):
        for jj in range(8):
            tt = ei * 8 + jj
            v.wait_ge(s_pet, tt + 1)
            if ei >= 2:
                v.wait_ge(s_pe, 2 * (ei - 1))
            v.tensor_copy(wkd[:, ((ei % 2) * 8 + jj) * 128:((ei % 2) * 8 + jj + 1) * 128],
                          pbt[tt % 4][:, 0:128]).then_inc(s_te, 1)

    def v_fn(v):
        pend = [None]

        def flush(v):
            if pend[0] is not None:
                _evacs(v, pend[0])
                pend[0] = None

        for half in range(NHALF):
            for ct in range(DN_CT):
                for pg in range(4):
                    pidx = pass_idx(half, ct, pg)
                    flush(v)
                    v.wait_ge(s_zc, 32 * (pidx + 1))
                    if pidx >= 1:
                        v.wait_ge(s_dq, 2 * NIS * pidx)
                    for spl in range(2):
                        sp_ = pg * 2 + spl
                        inst = v.tensor_scalar(nibz[:, spl * HB:(spl + 1) * HB], zc[:], 4 * sp_, 15,
                                               ALU.logical_shift_right, ALU.bitwise_and)
                    inst.then_inc(s_zs, 1)
                    v.wait_ge(s_zs, 2 * pidx + 1)
                    for spl in range(2):
                        inst = v.scalar_tensor_tensor(zsc[:, spl * HB:(spl + 1) * HB],
                                                      nibz[:, spl * HB:(spl + 1) * HB], -1.0,
                                                      scc[:, spl * HB:(spl + 1) * HB],
                                                      ALU.mult, ALU.mult)
                    inst.then_inc(s_zs, 1)

                    for isub in range(NIS):
                        gi = it_idx(half, ct, pg, isub)
                        v.wait_ge(s_qd[gi % 2], 16 * (gi // 2 + 1))
                        if isub == 0:
                            v.wait_ge(s_zs, 2 * pidx + 2)
                        for spl in range(2):
                            sp_ = pg * 2 + spl
                            ei = gi * 2 + spl
                            if ei >= 2:
                                v.wait_ge(s_dq, ei - 1)
                            v.tensor_scalar(nib[:, (ei % 2) * 1024:(ei % 2 + 1) * 1024],
                                            qsub[:, (gi % 2) * 1024:(gi % 2 + 1) * 1024],
                                            4 * sp_, 15, ALU.logical_shift_right,
                                            ALU.bitwise_and).then_inc(s_nx, 1)
                            flush(v)
                            pend[0] = ei
        flush(v)

    def sc_fn(sc):
        for half in range(NHALF):
            for ct in range(DN_CT):
                for pg in range(4):
                    pidx = pass_idx(half, ct, pg)
                    for isub in range(NIS):
                        gi = it_idx(half, ct, pg, isub)
                        for spl in range(2):
                            ei = gi * 2 + spl
                            sc.wait_ge(s_nx, ei + 1)
                            sc.wait_ge(s_zs, 2 * pidx + 2)
                            if ei >= 2:
                                sc.wait_ge(s_pet, 8 * (ei - 1))
                            for q8 in range(8):
                                g = 8 * isub + q8
                                src = nib[:, (ei % 2) * 1024:(ei % 2 + 1) * 1024] \
                                    .rearrange("p (s c) -> p s c", c=128)[:, :, q8 * 16:(q8 + 1) * 16]
                                dst = wtT[:, (ei % 2) * 1024:(ei % 2 + 1) * 1024] \
                                    .rearrange("p (s c) -> p s c", c=128)[:, :, q8 * 16:(q8 + 1) * 16]
                                inst = sc.activation(dst, src, AF.Identity,
                                                     bias=zsc[:, spl * HB + g:spl * HB + g + 1],
                                                     scale=scc[:, spl * HB + g:spl * HB + g + 1])
                            inst.then_inc(s_dq, 1)
                    sc.wait_ge(s_pe, 4 * NIS * (pidx + 1))
                    for spl in range(2):
                        for st in range(ST2):
                            ev_i = pidx * nb + spl * ST2 + st
                            if ev_i >= 2:
                                zz = ev_i - 2
                                sc.wait_ge(s_sp[zz % 2], 16 * (zz // 2 + 1))
                            sc.activation(stage[:, (ev_i % 2) * 512:(ev_i % 2 + 1) * 512],
                                          pb[spl * ST2 + st][:, :], AF.Identity,
                                          bias=0.0, scale=1.0).then_inc(s_ev, 1)

    def pe_fn(t):
        def transposes(ei):
            t.wait_ge(s_dq, ei + 1)
            for jj in range(8):
                tt = ei * 8 + jj
                if tt >= 4:
                    t.wait_ge(s_te, tt - 3)
                t.transpose(pbt[tt % 4][:, 0:128],
                            wtT[:, (ei % 2) * 1024 + jj * 128:(ei % 2) * 1024 + (jj + 1) * 128],
                            identb[:]).then_inc(s_pet, 1)

        def mms(half, pidx, gi, isub, spl):
            ei = gi * 2 + spl
            t.wait_ge(s_te, 8 * (ei + 1))
            for jj in range(8):
                hb = isub * 8 + jj
                for st in range(ST2):
                    inst = t.matmul(
                        pb[spl * ST2 + st][:, :],
                        wkd[:, ((ei % 2) * 8 + jj) * 128:((ei % 2) * 8 + jj + 1) * 128],
                        hbuf[:, hb * TH + st * 512:hb * TH + st * 512 + 512],
                        start=(isub == 0 and jj == 0), stop=(isub == NIS - 1 and jj == 7))
            inst.then_inc(s_pe, 2)

        prev = None
        for half in range(NHALF):
            if prev is not None:
                mms(*prev)
                prev = None
            t.wait_ge(s_hld, 16 * HB * (half + 1))
            for ct in range(DN_CT):
                for pg in range(4):
                    pidx = pass_idx(half, ct, pg)
                    if prev is not None:
                        mms(*prev)
                        prev = None
                    if pidx >= 1:
                        t.wait_ge(s_ev, pidx * nb)
                    for isub in range(NIS):
                        gi = it_idx(half, ct, pg, isub)
                        for spl in range(2):
                            transposes(gi * 2 + spl)
                            if prev is not None:
                                mms(*prev)
                            prev = (half, pidx, gi, isub, spl)
        mms(*prev)

    p.phase("fdn", sync=sync_fn, vector=v_fn, scalar=sc_fn, tensor=pe_fn)
    return dict(frees=frees)


def att_phase(p, NM, NP, q_s, g_s, kfm_s, ktm_s, vtm_s, dtraw_s, y_s, R):
    NT = NM + NP
    pb = R["pb"]
    ident_sb, ones1_sb, onesr_sb = R["ident_sb"], R["ones1_sb"], R["onesr_sb"]
    u_sb, mneg_sb, acol_sb, gnw_sb = R["u_sb"], R["mneg_sb"], R["acol_sb"], R["gnw_sb"]
    eps_sb, c_rsel = R["eps_sb"], R["c_rsel"]

    s_dt = p.sem("att_dt")
    s_dta = p.sem("att_dta")
    s_dtv = p.sem("att_dtv")
    s_ld = [p.sem("att_lda"), p.sem("att_ldb")]
    s_clp = p.sem("att_clp")
    s_clv = p.sem("att_clv")
    s_cla = p.sem("att_cla")
    s_xk = p.sem("att_xk")
    s_hp = p.sem("att_hp")
    s_hv = p.sem("att_hv")
    s_ha = p.sem("att_ha")
    s_hg = p.sem("att_hg")
    s_ytr = p.sem("att_ytr")
    s_ysp = p.sem("att_ysp")

    TPc = NP * 128
    NTT = NT * 128
    rsel_sb, c0 = p.sb("att_rsel", [32, 32 * 128], F32)
    dtb, c1 = p.sb("att_dtraw", [32, NTT], F32)
    spb, c2 = p.sb("att_spb", [32, NTT], F32)
    gall, c3 = p.sb("att_gall", [32, NTT], F32)
    qc, c4 = p.sb("att_qc", [128, 2 * 4096], BF16)
    gc, c5 = p.sb("att_gc", [128, 2 * 4096], BF16)
    kf, c6 = p.sb("att_kf", [128, 2 * 1024], BF16)
    kt, c7 = p.sb("att_kt", [128, 2 * 1024], BF16)
    vt, c8 = p.sb("att_vt", [128, 2 * 1024], BF16)
    Sf, c9 = p.sb("att_Sf", [128, 32 * 128], F32)
    Sb, c10 = p.sb("att_Sb", [128, 32 * 128], BF16)
    xc_all, c11 = p.sb("att_xc", [128, 32 * 128], BF16)
    kd_all, c12 = p.sb("att_kd", [128, 32 * 128], BF16)
    dtT, c13 = p.sb("att_dtT", [128, 32], F32)
    gt, c14 = p.sb("att_gt", [128, 32], F32)
    GT, c15 = p.sb("att_GT", [128, 32], F32)
    glrow, c16 = p.sb("att_glrow", [1, 32], F32)
    egrow, c17 = p.sb("att_egrow", [1, 32], F32)
    EGB, c18 = p.sb("att_EGB", [128, 32], F32)
    d1, c19 = p.sb("att_d1", [128, 32], F32)
    EJL, c20 = p.sb("att_EJL", [128, 32], F32)
    EI, c21 = p.sb("att_EI", [128, 32], F32)
    gth, c22 = p.sb("att_gth", [32, 128], F32)
    d2, c23 = p.sb("att_d2", [128, 2 * 128], F32)
    dec, c24 = p.sb("att_dec", [128, 2 * 128], BF16)
    pbf, c25 = p.sb("att_pbf", [128, 2 * 128], BF16)
    yscr, c26 = p.sb("att_yscr", [128, 2 * 128], F32)
    ytm, c27 = p.sb("att_ytm", [128, 2 * 128], F32)
    ynb, c28 = p.sb("att_ynb", [128, 2 * 128], BF16)
    ynf, c29 = p.sb("att_ynf", [128, 2 * 128], BF16)
    sgt, c30 = p.sb("att_sgt", [128, 32 * 128], F32)
    tmul, c31 = p.sb("att_tmul", [128, 2 * 128], BF16)
    ystg, c32 = p.sb("att_ystg", [128, 32 * 128], BF16)
    sqt, c33 = p.sb("att_sqt", [128, 2 * 128], F32)
    ssq, c34 = p.sb("att_ssq", [128, 2], F32)
    rms, c35 = p.sb("att_rms", [128, 2], F32)
    frees = [c0, c1, c2, c3, c4, c5, c6, c7, c8, c9, c10, c11, c12, c13, c14, c15,
             c16, c17, c18, c19, c20, c21, c22, c23, c24, c25, c26, c27, c28,
             c29, c30, c31, c32, c33, c34, c35]

    NCLP, NCLV, NCLA = 7, 6, 5
    def is_main(n):
        return n >= NP
    HP_M, HV_M, HA_M, HG_M = 5, 9, 4, 1
    HP_P, HV_P = 1, 2

    hp_base = [0]; hv_base = [0]; ha_base = [0]; hg_base = [0]; xk_base = [0]
    ytr_base = [0]; ysp_base = [0]; ld_base = [[0], [0]]
    for n in range(NT):
        m = is_main(n)
        hp_base.append(hp_base[-1] + 32 * (HP_M if m else HP_P))
        hv_base.append(hv_base[-1] + 32 * (HV_M if m else HV_P))
        ha_base.append(ha_base[-1] + 32 * (HA_M if m else 0))
        hg_base.append(hg_base[-1] + 32 * (HG_M if m else 0))
        xk_base.append(xk_base[-1] + 32)
        ytr_base.append(ytr_base[-1] + (16 * 32 if m else 0))
        ysp_base.append(ysp_base[-1] + (16 * 32 if m else 0))
        nl = 5 if m else 2
        for par in range(2):
            ld_base[par].append(ld_base[par][-1] + (16 * nl if n % 2 == par else 0))

    def qsl(n):
        return qc[:, (n % 2) * 4096:(n % 2) * 4096 + 4096]

    def gsl(n):
        return gc[:, (n % 2) * 4096:(n % 2) * 4096 + 4096]

    def kfsl(n):
        return kf[:, (n % 2) * 1024:(n % 2) * 1024 + 1024]

    def ktsl(n):
        return kt[:, (n % 2) * 1024:(n % 2) * 1024 + 1024]

    def vtsl(n):
        return vt[:, (n % 2) * 1024:(n % 2) * 1024 + 1024]

    def chunk_consumed_waits(eng, n):
        eng.wait_ge(s_hp, hp_base[n + 1])
        eng.wait_ge(s_hv, hv_base[n + 1])
        if is_main(n):
            eng.wait_ge(s_ha, ha_base[n + 1])
            eng.wait_ge(s_hg, hg_base[n + 1])

    def sync_fn(q):
        q.dma_start(out=rsel_sb[:], in_=c_rsel).then_inc(s_dt, 16)
        q.dma_start(out=dtb[:], in_=dtraw_s).then_inc(s_dt, 16)

        def loads(n):
            par = n % 2
            if n >= 2:
                chunk_consumed_waits(q, n - 2)
            if is_main(n):
                nm = n - NP
                q.dma_start(out=qsl(n), in_=q_s[nm]).then_inc(s_ld[par], 16)
                q.dma_start(out=gsl(n), in_=g_s[nm]).then_inc(s_ld[par], 16)
                q.dma_start(out=kfsl(n), in_=kfm_s[nm]).then_inc(s_ld[par], 16)
            q.dma_start(out=ktsl(n), in_=ktm_s[n]).then_inc(s_ld[par], 16)
            q.dma_start(out=vtsl(n), in_=vtm_s[n]).then_inc(s_ld[par], 16)

        loads(0)
        for n in range(NT):
            if n + 1 < NT:
                loads(n + 1)
            if is_main(n):
                nm = n - NP
                for h in range(32):
                    q.wait_ge(s_hv, hv_base[n] + h * HV_M + 9)
                    if h >= 2:
                        q.wait_ge(s_hg, hg_base[n] + h - 1)
                    q.dma_start(out=ynf[:, (h % 2) * 128:(h % 2) * 128 + 128],
                                in_=ynb[:, (h % 2) * 128:(h % 2) * 128 + 128],
                                transpose=True).then_inc(s_ytr, 16)
                    q.wait_ge(s_ha, ha_base[n] + h * HA_M + 4)
                    q.dma_start(out=y_s[h][:, nm * 128:(nm + 1) * 128],
                                in_=ystg[:, h * 128:(h + 1) * 128]).then_inc(s_ysp, 16)
        q.wait_ge(s_ysp, ysp_base[NT])
        q.wait_ge(s_ytr, ytr_base[NT])

    def v_fn(v):
        v.memzero(Sf[:])
        v.memzero(Sb[:]).then_inc(s_dtv, 1)
        v.wait_ge(s_dt, 32)
        v.tensor_scalar_max(gall[:], dtb[:], 0.0).then_inc(s_dtv, 1)
        v.wait_ge(s_dta, 3)
        v.wait_ge(s_dtv, 2)
        v.tensor_add(spb[:], spb[:], gall[:]).then_inc(s_dtv, 1)
        v.wait_ge(s_dtv, 3)
        v.tensor_scalar(gall[:], spb[:], acol_sb[0:32, 0:1], None, ALU.mult).then_inc(s_dtv, 1)
        for n in range(NT):
            m = is_main(n)
            if n >= 1:
                v.wait_ge(s_hp, hp_base[n])
                v.wait_ge(s_xk, xk_base[n])
                v.wait_ge(s_cla, NCLA * n)
            v.wait_ge(s_clp, NCLP * n + 1)
            v.tensor_copy(dtT[:], pb[0][:, 0:32]).then_inc(s_clv, 1)
            v.wait_ge(s_clp, NCLP * n + 2)
            v.tensor_copy(gt[:], pb[1][:, 0:32]).then_inc(s_clv, 1)
            v.wait_ge(s_clp, NCLP * n + 3)
            v.tensor_copy(GT[:], pb[2][:, 0:32]).then_inc(s_clv, 1)
            v.wait_ge(s_clp, NCLP * n + 5)
            v.tensor_copy(EGB[:], pb[1][:, 0:32]).then_inc(s_clv, 1)
            v.wait_ge(s_clp, NCLP * n + 6)
            v.tensor_sub(d1[:], pb[2][:, 0:32], GT[:]).then_inc(s_clv, 1)
            v.wait_ge(s_clp, NCLP * n + 7)
            v.tensor_copy(gth[:], pb[0][0:32, 0:128]).then_inc(s_clv, 1)
            v.wait_ge(s_ld[n % 2], ld_base[n % 2][n + 1])
            for h in range(32):
                kv = h // 4
                v.wait_ge(s_cla, NCLA * n + 3)
                v.tensor_scalar(xc_all[:, h * 128:(h + 1) * 128], vtsl(n)[:, kv * 128:(kv + 1) * 128],
                                dtT[:, h:h + 1], None, ALU.mult)
                v.tensor_scalar(kd_all[:, h * 128:(h + 1) * 128], ktsl(n)[:, kv * 128:(kv + 1) * 128],
                                EJL[:, h:h + 1], None, ALU.mult).then_inc(s_xk, 1)
            if m:
                for h in range(32):
                    hb = hv_base[n] + h * HV_M
                    v.wait_ge(s_hp, hp_base[n] + h * HP_M + 1)
                    v.scalar_tensor_tensor(d2[:, (h % 2) * 128:(h % 2) * 128 + 128],
                                           pb[3][:, 0:128], GT[:, h:h + 1], mneg_sb[:],
                                           ALU.subtract, ALU.add).then_inc(s_hv, 1)
                    v.wait_ge(s_hp, hp_base[n] + h * HP_M + 2)
                    v.wait_ge(s_ha, ha_base[n] + h * HA_M + 1)
                    v.tensor_mul(pbf[:, (h % 2) * 128:(h % 2) * 128 + 128],
                                 pb[h % 2][:, 0:128],
                                 dec[:, (h % 2) * 128:(h % 2) * 128 + 128]).then_inc(s_hv, 1)
                    v.wait_ge(s_hp, hp_base[n] + h * HP_M + 3)
                    v.tensor_scalar(yscr[:, (h % 2) * 128:(h % 2) * 128 + 128],
                                    pb[4 + (h % 2) * 2][:, 0:128], EI[:, h:h + 1], None,
                                    ALU.mult).then_inc(s_hv, 1)
                    v.wait_ge(s_hp, hp_base[n] + h * HP_M + 4)
                    v.wait_ge(s_hv, hb + 3)
                    v.tensor_add(ytm[:, (h % 2) * 128:(h % 2) * 128 + 128],
                                 pb[5 + (h % 2) * 2][:, 0:128],
                                 yscr[:, (h % 2) * 128:(h % 2) * 128 + 128]).then_inc(s_hv, 1)
                    v.wait_ge(s_hv, hb + 4)
                    if h >= 2:
                        v.wait_ge(s_hv, hv_base[n] + (h - 2) * HV_M + 6)
                    v.tensor_mul(sqt[:, (h % 2) * 128:(h % 2) * 128 + 128],
                                 ytm[:, (h % 2) * 128:(h % 2) * 128 + 128],
                                 ytm[:, (h % 2) * 128:(h % 2) * 128 + 128]).then_inc(s_hv, 1)
                    v.wait_ge(s_hv, hb + 5)
                    v.reduce_sum(ssq[:, h % 2:h % 2 + 1],
                                 sqt[:, (h % 2) * 128:(h % 2) * 128 + 128],
                                 axis=AX.X).then_inc(s_hv, 1)
                    v.wait_ge(s_hp, hp_base[n] + h * HP_M + 5)
                    v.scalar_tensor_tensor(Sf[:, h * 128:(h + 1) * 128], Sf[:, h * 128:(h + 1) * 128],
                                           EGB[:, h:h + 1], pb[2][:, 0:128],
                                           ALU.mult, ALU.add).then_inc(s_hv, 1)
                    v.wait_ge(s_hv, hb + 7)
                    v.tensor_copy(Sb[:, h * 128:(h + 1) * 128], Sf[:, h * 128:(h + 1) * 128]).then_inc(s_hv, 1)
                    v.wait_ge(s_ha, ha_base[n] + h * HA_M + 3)
                    v.tensor_scalar(ynb[:, (h % 2) * 128:(h % 2) * 128 + 128],
                                    ytm[:, (h % 2) * 128:(h % 2) * 128 + 128],
                                    rms[:, h % 2:h % 2 + 1], None, ALU.mult).then_inc(s_hv, 1)
            else:
                for h in range(32):
                    hb = hv_base[n] + h * HV_P
                    v.wait_ge(s_hp, hp_base[n] + h * HP_P + 1)
                    v.scalar_tensor_tensor(Sf[:, h * 128:(h + 1) * 128], Sf[:, h * 128:(h + 1) * 128],
                                           EGB[:, h:h + 1], pb[2 + h % 2][:, 0:128],
                                           ALU.mult, ALU.add).then_inc(s_hv, 1 if n == NP - 1 else 2)
                    if n == NP - 1:
                        v.wait_ge(s_hv, hb + 1)
                        v.tensor_copy(Sb[:, h * 128:(h + 1) * 128], Sf[:, h * 128:(h + 1) * 128]).then_inc(s_hv, 1)

    def sc_fn(sc):
        sc.wait_ge(s_dt, 32)
        sc.activation(spb[:], dtb[:], AF.Abs, bias=0.0, scale=1.0).then_inc(s_dta, 1)
        sc.wait_ge(s_dta, 1)
        sc.activation(spb[:], spb[:], AF.Exp, bias=0.0, scale=-1.0).then_inc(s_dta, 1)
        sc.wait_ge(s_dta, 2)
        sc.activation(spb[:], spb[:], AF.Ln, bias=eps_sb[0:32, 1:2], scale=1.0).then_inc(s_dta, 1)
        for n in range(NT):
            m = is_main(n)
            if n >= 1:
                sc.wait_ge(s_hv, hv_base[n])
                sc.wait_ge(s_xk, xk_base[n])
            sc.wait_ge(s_clp, NCLP * n + 4)
            sc.activation(glrow[0:1, :], pb[0][0:1, 0:32], AF.Identity, bias=0.0, scale=1.0).then_inc(s_cla, 1)
            sc.wait_ge(s_cla, NCLA * n + 1)
            sc.activation(egrow[0:1, :], glrow[0:1, :], AF.Exp, bias=0.0, scale=1.0).then_inc(s_cla, 1)
            sc.wait_ge(s_clv, NCLV * n + 5)
            sc.activation(EJL[:], d1[:], AF.Exp, bias=0.0, scale=1.0).then_inc(s_cla, 1)
            sc.wait_ge(s_clv, NCLV * n + 3)
            if not m:
                sc.activation(EI[:], GT[:], AF.Exp, bias=0.0, scale=1.0).then_inc(s_cla, 2)
                continue
            sc.activation(EI[:], GT[:], AF.Exp, bias=0.0, scale=1.0).then_inc(s_cla, 1)
            sc.wait_ge(s_ld[n % 2], ld_base[n % 2][n + 1])
            sc.wait_ge(s_ysp, ysp_base[n])
            if n >= 1:
                sc.wait_ge(s_hg, hg_base[n])
            for h in range(32):
                inst = sc.activation(sgt[:, h * 128:(h + 1) * 128],
                                     gsl(n)[:, h * 128:(h + 1) * 128], AF.Silu,
                                     bias=0.0, scale=1.0)
            inst.then_inc(s_cla, 1)
            for h in range(32):
                ab = ha_base[n] + h * HA_M
                sc.wait_ge(s_hv, hv_base[n] + h * HV_M + 1)
                if h >= 2:
                    sc.wait_ge(s_hv, hv_base[n] + (h - 2) * HV_M + 2)
                sc.activation(dec[:, (h % 2) * 128:(h % 2) * 128 + 128],
                              d2[:, (h % 2) * 128:(h % 2) * 128 + 128], AF.Exp,
                              bias=0.0, scale=1.0).then_inc(s_ha, 1)
                sc.wait_ge(s_hv, hv_base[n] + h * HV_M + 6)
                sc.activation(rms[:, h % 2:h % 2 + 1], ssq[:, h % 2:h % 2 + 1], AF.Ln,
                              bias=eps_sb[:, 0:1], scale=1.0 / 128).then_inc(s_ha, 1)
                sc.wait_ge(s_ha, ab + 2)
                sc.activation(rms[:, h % 2:h % 2 + 1], rms[:, h % 2:h % 2 + 1], AF.Exp,
                              bias=0.0, scale=-0.5).then_inc(s_ha, 1)
                sc.wait_ge(s_hg, hg_base[n] + h + 1)
                sc.activation(ystg[:, h * 128:(h + 1) * 128],
                              tmul[:, (h % 2) * 128:(h % 2) * 128 + 128], AF.Identity,
                              bias=0.0, scale=gnw_sb[:, 0:1]).then_inc(s_ha, 1)

    def gp_fn(gp):
        for n in range(NT):
            if not is_main(n):
                continue
            for h in range(32):
                gp.wait_ge(s_ytr, ytr_base[n] + 16 * (h + 1))
                gp.wait_ge(s_cla, NCLA * n + 5)
                if h >= 2:
                    gp.wait_ge(s_ha, ha_base[n] + (h - 2) * HA_M + 4)
                gp.tensor_mul(tmul[:, (h % 2) * 128:(h % 2) * 128 + 128],
                              ynf[:, (h % 2) * 128:(h % 2) * 128 + 128],
                              sgt[:, h * 128:(h + 1) * 128]).then_inc(s_hg, 1)

    def pe_fn(t):
        t.wait_ge(s_dta, 3)
        t.wait_ge(s_dtv, 4)
        for n in range(NT):
            m = is_main(n)
            par = n % 2
            nl = 5 if m else 2
            t.wait_ge(s_ld[par], ld_base[par][n + 1])
            if n >= 1:
                t.wait_ge(s_hv, hv_base[n])
                if is_main(n - 1):
                    t.wait_ge(s_ha, ha_base[n])
            t.transpose(pb[0][:, 0:32], spb[0:32, n * 128:(n + 1) * 128], ident_sb[0:32, 0:32]).then_inc(s_clp, 1)
            t.transpose(pb[1][:, 0:32], gall[0:32, n * 128:(n + 1) * 128], ident_sb[0:32, 0:32]).then_inc(s_clp, 1)
            t.wait_ge(s_clv, NCLV * n + 2)
            t.matmul(pb[2][:, 0:32], u_sb[:], gt[:], start=True, stop=True).then_inc(s_clp, 1)
            t.wait_ge(s_clv, NCLV * n + 1)
            t.matmul(pb[0][0:1, 0:32], ones1_sb[:], gt[:], start=True, stop=True).then_inc(s_clp, 1)
            t.wait_ge(s_cla, NCLA * n + 2)
            t.matmul(pb[1][:, 0:32], onesr_sb[:], egrow[0:1, :], start=True, stop=True).then_inc(s_clp, 1)
            t.wait_ge(s_clv, NCLV * n + 3)
            t.matmul(pb[2][:, 0:32], onesr_sb[:], glrow[0:1, :], start=True, stop=True).then_inc(s_clp, 1)
            t.wait_ge(s_cla, NCLA * n + 1)
            t.transpose(pb[0][0:32, 0:128], GT[:], ident_sb[:]).then_inc(s_clp, 1)
            if m:
                for h in range(32):
                    hb = hp_base[n] + h * HP_M
                    if h >= 1:
                        t.wait_ge(s_hv, hv_base[n] + (h - 1) * HV_M + 1)
                    t.wait_ge(s_clv, NCLV * n + 6)
                    t.matmul(pb[3][:, 0:128], rsel_sb[0:32, h * 128:(h + 1) * 128], gth[:],
                             start=True, stop=True).then_inc(s_hp, 1)
                    if h >= 2:
                        t.wait_ge(s_hv, hv_base[n] + (h - 2) * HV_M + 2)
                    t.matmul(pb[h % 2][:, 0:128], kfsl(n)[:, (h // 4) * 128:(h // 4 + 1) * 128],
                             qsl(n)[:, h * 128:(h + 1) * 128], start=True, stop=True).then_inc(s_hp, 1)
                    if h >= 2:
                        t.wait_ge(s_hv, hv_base[n] + (h - 2) * HV_M + 3)
                    t.matmul(pb[4 + (h % 2) * 2][:, 0:128], qsl(n)[:, h * 128:(h + 1) * 128],
                             Sb[:, h * 128:(h + 1) * 128], start=True, stop=True).then_inc(s_hp, 1)
                    t.wait_ge(s_hv, hv_base[n] + h * HV_M + 2)
                    t.wait_ge(s_xk, xk_base[n] + h + 1)
                    if h >= 2:
                        t.wait_ge(s_hv, hv_base[n] + (h - 2) * HV_M + 4)
                    t.matmul(pb[5 + (h % 2) * 2][:, 0:128], pbf[:, (h % 2) * 128:(h % 2) * 128 + 128],
                             xc_all[:, h * 128:(h + 1) * 128], start=True, stop=True).then_inc(s_hp, 1)
                    if h >= 1:
                        t.wait_ge(s_hv, hv_base[n] + (h - 1) * HV_M + 7)
                    t.matmul(pb[2][:, 0:128], kd_all[:, h * 128:(h + 1) * 128],
                             xc_all[:, h * 128:(h + 1) * 128], start=True, stop=True).then_inc(s_hp, 1)
            else:
                for h in range(32):
                    t.wait_ge(s_xk, xk_base[n] + h + 1)
                    if h >= 2:
                        t.wait_ge(s_hv, hv_base[n] + (h - 2) * HV_P + 2)
                    t.matmul(pb[2 + h % 2][:, 0:128], kd_all[:, h * 128:(h + 1) * 128],
                             xc_all[:, h * 128:(h + 1) * 128], start=True, stop=True).then_inc(s_hp, 1)

    p.phase("att", sync=sync_fn, vector=v_fn, scalar=sc_fn, tensor=pe_fn, gpsimd=gp_fn)
    return dict(frees=frees)



def wcomb_phase(p, TM, wo_r, ffn_s, y_s, x_main, out_d, R):
    ST = TM // 512
    pb = R["pb"]
    s_y = p.sem("wc_y")
    s_w = [p.sem("wc_wa"), p.sem("wc_wb")]
    s_pe = p.sem("wc_pe")
    s_ev = p.sem("wc_ev")
    s_ftr = [p.sem("wc_ftra"), p.sem("wc_ftrb")]
    s_fl = p.sem("wc_fl")
    s_xl = s_fl
    s_ot = p.sem("wc_ot")
    s_os = [p.sem("wc_osa"), p.sem("wc_osb")]

    ybuf, cy = p.sb("wc_y", [128, 32 * TM], BF16)
    wob, cw = p.sb("wc_wob", [128, 2 * 4096], BF16)
    fbuf, cf = p.sb("wc_f", [128, 2 * 512], BF16)
    xbuf, cx = p.sb("wc_x", [128, 2 * 512], F32)
    sumb, cs = p.sb("wc_sum", [128, 2 * 512], BF16)
    sumT, ct_ = p.sb("wc_sumT", [128, 2 * 512], BF16)
    outb, co = p.sb("wc_out", [128, 2 * 512], F32)
    frees = [cy, cw, cf, cx, cs, ct_, co]

    items = [(fo, st) for fo in range(32) for st in range(ST)]

    def it(fo, st):
        return fo * ST + st

    def sync_fn(q):
        for dq in range(32):
            q.dma_start(out=ybuf[:, dq * TM:(dq + 1) * TM], in_=y_s[dq]).then_inc(s_y, 16)
        for fo, st in items:
            i = it(fo, st)
            sl = i % 2
            if i >= 2:
                q.wait_ge(s_ev, i - 1)
            q.dma_start(out=fbuf[:, sl * 512:(sl + 1) * 512],
                        in_=ffn_s[fo][:, st * 512:(st + 1) * 512]).then_inc(s_fl, 16)
            if i >= 2:
                q.wait_ge(s_ot, i - 1)
            xs = x_main.rearrange("(s u t) f -> s u t f", u=4, t=128)[st, :, :, :] \
                [:, :, fo * 128:(fo + 1) * 128]
            q.dma_start(out=xbuf[:, sl * 512:(sl + 1) * 512],
                        in_=xs.transpose([1, 0, 2])).then_inc(s_xl, 16)
            q.wait_ge(s_ev, i + 1)
            for u in range(4):
                q.dma_start(out=sumT[:, sl * 512 + u * 128:sl * 512 + (u + 1) * 128],
                            in_=sumb[:, sl * 512 + u * 128:sl * 512 + (u + 1) * 128],
                            transpose=True).then_inc(s_ftr[i % 2], 16)
            q.wait_ge(s_ot, i + 1)
            for u in range(4):
                q.dma_start(out=out_d[st * 512 + u * 128:st * 512 + (u + 1) * 128,
                                      fo * 128:(fo + 1) * 128],
                            in_=outb[:, sl * 512 + u * 128:sl * 512 + (u + 1) * 128]).then_inc(s_os[i % 2], 16)
        ni = len(items)
        q.wait_ge(s_os[0], 16 * 4 * ((ni + 1) // 2))
        if ni >= 2:
            q.wait_ge(s_os[1], 16 * 4 * (ni // 2))

    def gp_fn(gp):
        for fo in range(32):
            if fo >= 2:
                gp.wait_ge(s_pe, ST * (fo - 1))
            gp.dma_start(out=wob[:, (fo % 2) * 4096:(fo % 2) * 4096 + 4096],
                         in_=wo_r[fo].rearrange("p hc c -> p (hc c)")).then_inc(s_w[fo % 2], 16)

    def pe_fn(t):
        t.wait_ge(s_y, 16 * 32)
        for fo in range(32):
            t.wait_ge(s_w[fo % 2], 16 * (fo // 2 + 1))
            if fo >= 2:
                t.wait_ge(s_ev, ST * (fo - 1))
            for st in range(ST):
                bank = pb[(fo % 2) * 4 + st]
                for dq in range(32):
                    inst = t.matmul(bank[:, :],
                                    wob[:, (fo % 2) * 4096 + dq * 128:(fo % 2) * 4096 + (dq + 1) * 128],
                                    ybuf[:, dq * TM + st * 512:dq * TM + (st + 1) * 512],
                                    start=(dq == 0), stop=(dq == 31))
                inst.then_inc(s_pe, 1)

    def v_fn(v):
        for fo, st in items:
            i = it(fo, st)
            sl = i % 2
            v.wait_ge(s_pe, i + 1)
            v.wait_ge(s_fl, 32 * (i + 1))
            if i >= 2:
                v.wait_ge(s_ftr[i % 2], 16 * 4 * (i // 2))
            v.tensor_add(sumb[:, sl * 512:(sl + 1) * 512],
                         pb[(fo % 2) * 4 + st][:, :],
                         fbuf[:, sl * 512:(sl + 1) * 512]).then_inc(s_ev, 1)
            v.wait_ge(s_ftr[i % 2], 16 * 4 * (i // 2 + 1))
            v.wait_ge(s_xl, 32 * (i + 1))
            if i >= 2:
                zz = i - 2
                v.wait_ge(s_os[zz % 2], 16 * 4 * (zz // 2 + 1))
            v.tensor_add(outb[:, sl * 512:(sl + 1) * 512],
                         sumT[:, sl * 512:(sl + 1) * 512],
                         xbuf[:, sl * 512:(sl + 1) * 512]).then_inc(s_ot, 1)

    p.phase("wcomb", sync=sync_fn, gpsimd=gp_fn, tensor=pe_fn, vector=v_fn)
    return dict(frees=frees)



def build(n_main=16, n_prefix=16, debug=False):
    p = P(n_main, n_prefix, debug)
    TM, TP = p.TM, p.TP
    NM, NP = n_main, n_prefix
    NT = NM + NP

    x_main = p.din("x_main", [TM, HID], F32)
    x_prefix = p.din("x_prefix", [TP, HID], F32) if NP else None
    wq_r = p.din("wq_r", [32, 128, 32, 128], F32)
    wk_r = p.din("wk_r", [8, 128, 32, 128], F32)
    wv_r = p.din("wv_r", [8, 128, 32, 128], F32)
    wg_r = p.din("wg_r", [32, 128, 32, 128], F32)
    wo_r = p.din("wo_r", [32, 128, 32, 128], F32)
    win_r = p.din("win_r", [1, 128, 32, 32], F32)
    dt_bias_col = p.din("dt_bias_col", [32, 1], F32)
    albias_col = p.din("albias_col", [32, 1], F32)
    gnw_col = p.din("gnw_col", [128, 1], F32)
    ropec_m = p.din("ropec_m", [128, TM], BF16)
    ropes_m = p.din("ropes_m", [128, TM], BF16)
    ropec_p = p.din("ropec_p", [128, TP], BF16) if NP else None
    ropes_p = p.din("ropes_p", [128, TP], BF16) if NP else None
    c_ident = p.din("c_ident", [128, 128], F32)
    c_identb = p.din("c_identb", [128, 128], BF16)
    c_ones1 = p.din("c_ones1", [128, 1], F32)
    c_onesr = p.din("c_onesr", [1, 128], F32)
    c_u = p.din("c_u", [128, 128], F32)
    c_mneg = p.din("c_mneg", [128, 128], F32)
    c_eps = p.din("c_eps", [128, 2], F32)
    c_rsel = p.din("c_rsel", [32, 32 * 128], F32)
    qguT = p.din("qguT", [GU_CT, 128, HID], I32)
    sgu_cols = p.din("sgu_cols", [128, GU_CT * NPLANE * 32], F32)
    zguT = p.din("zguT", [GU_CT, 128, 32], I32)
    qdnT = p.din("qdnT", [DN_CT, 128, HB * 128], I32)
    sdn_cols = p.din("sdn_cols", [128, DN_CT * NPLANE * HB], F32)
    zdnT = p.din("zdnT", [512, HB], I32)

    out_d = p.dout("out", [TM, HID], F32)

    q_s = p.dtmp("q_s", [NM, 128, 4096], BF16)
    g_s = p.dtmp("g_s", [NM, 128, 4096], BF16)
    kfm_s = p.dtmp("kfm_s", [NM, 128, 1024], BF16)
    ktm_s = p.dtmp("ktm_s", [NT, 128, 1024], BF16)
    vtm_s = p.dtmp("vtm_s", [NT, 128, 1024], BF16)
    dtraw_s = p.dtmp("dtraw_s", [32, NT * 128], F32)
    h_s = p.dtmp("h_s", [HB, 128, TM], BF16)
    ffn_s = p.dtmp("ffn_s", [32, 128, TM], BF16)
    y_s = p.dtmp("y_s", [32, 128, TM], BF16)
    xn_dbg = p.dtmp("xn_dbg", [128, 32 * TM], BF16) if debug else None

    ident_sb, _ = p.sb("ident_sb", [128, 128], F32)
    identb_sb, _ = p.sb("identb_sb", [128, 128], BF16)
    ones1_sb, _ = p.sb("ones1_sb", [128, 1], F32)
    onesr_sb, _ = p.sb("onesr_sb", [1, 128], F32)
    u_sb, _ = p.sb("u_sb", [128, 128], F32)
    mneg_sb, _ = p.sb("mneg_sb", [128, 128], F32)
    dtb_sb, _ = p.sb("dtb_sb", [32, 1], F32)
    alb_sb, _ = p.sb("alb_sb", [32, 1], F32)
    acol_sb, _ = p.sb("acol_sb", [32, 1], F32)
    gnw_sb, _ = p.sb("gnw_sb", [128, 1], F32)
    eps_sb, _ = p.sb("eps_sb", [128, 2], F32)

    pb = [p.ctx.enter_context(p.nc.psum_tensor(f"pb{i}", [128, 512], F32)) for i in range(8)]
    R = dict(pb=pb, dtb_sb=dtb_sb, c_rsel=c_rsel, ident_sb=ident_sb,
             identb_sb=identb_sb, ones1_sb=ones1_sb,
             onesr_sb=onesr_sb, u_sb=u_sb, mneg_sb=mneg_sb, acol_sb=acol_sb,
             gnw_sb=gnw_sb, eps_sb=eps_sb)

    s_ic = p.sem("s_ic")
    s_ac = p.sem("s_ac")

    def init_sync(q):
        for t, d in [(ident_sb, c_ident), (identb_sb, c_identb),
                     (ones1_sb, c_ones1), (onesr_sb, c_onesr),
                     (u_sb, c_u), (mneg_sb, c_mneg), (dtb_sb, dt_bias_col),
                     (alb_sb, albias_col), (gnw_sb, gnw_col), (eps_sb, c_eps)]:
            q.dma_start(out=t[:], in_=d).then_inc(s_ic, 16)

    def init_scalar(sc):
        sc.wait_ge(s_ic, 16 * 10)
        sc.activation(acol_sb[:], alb_sb[:], AF.Exp, bias=0.0, scale=1.0).then_inc(s_ac, 1)

    def init_vector(v):
        v.wait_ge(s_ac, 1)
        v.tensor_scalar_mul(acol_sb[:], acol_sb[:], -1.0)

    p.phase("init", sync=init_sync, scalar=init_scalar, vector=init_vector)
    p.barrier("init")

    xn_fm, cm_xnfm = p.sb("xn_fm", [128, 32 * max(TM, TP if NP else TM)], BF16)
    dt_raw, cm_dtraw = p.sb("dt_raw", [32, NT * 128], F32)
    R["dt_raw"] = dt_raw

    if NP:
        ln0 = ln_phase(p, "p0a", NP, x_prefix, xn_fm, TP, eps_sb, R)
        p.barrier("p0a")
        for cm in reversed(ln0["frees"]):
            p.free(cm)
        groups0 = [
            dict(kind="k", w=wk_r, n_fo=8, fo_cols=128, dst=kfm_s, dst2=ktm_s, tc0=0, ktc0=0),
            dict(kind="v", w=wv_r, n_fo=8, fo_cols=128, dst2=vtm_s, ktc0=0),
            dict(kind="win", w=win_r, n_fo=1, fo_cols=32, dt_off=0),
        ]
        groups0[0]["dst"] = None
        pr0 = proj_phase(p, "p0b", NP, xn_fm, TP, groups0, ln0, ropec_p, ropes_p, R)
        p.barrier("p0")
        for cm in reversed(pr0["frees"]):
            p.free(cm)

    ln1 = ln_phase(p, "p1a", NM, x_main, xn_fm, TM, eps_sb, R)
    p.barrier("p1a")
    for cm in reversed(ln1["frees"]):
        p.free(cm)
    groups1 = [
        dict(kind="q", w=wq_r, n_fo=32, fo_cols=128, dst=q_s, tc0=0),
        dict(kind="k", w=wk_r, n_fo=8, fo_cols=128, dst=kfm_s, dst2=ktm_s, tc0=0, ktc0=NP),
        dict(kind="v", w=wv_r, n_fo=8, fo_cols=128, dst2=vtm_s, ktc0=NP),
        dict(kind="g", w=wg_r, n_fo=32, fo_cols=128, dst=g_s, tc0=0),
        dict(kind="win", w=win_r, n_fo=1, fo_cols=32, dt_off=TP),
    ]
    pr1 = proj_phase(p, "p1b", NM, xn_fm, TM, groups1, ln1, ropec_m, ropes_m, R)

    s_dtsp = p.sem("s_dtsp")

    def dtspill_sync(q):
        q.wait_ge(pr1["s_ev"], pr1["n_ev"])
        if NP:
            q.wait_ge(pr0["s_ev"], pr0["n_ev"])
        q.dma_start(out=dtraw_s, in_=dt_raw[:]).then_inc(s_dtsp, 16)
        q.wait_ge(s_dtsp, 16)
        if p.debug:
            pass

    p.phase("dtspill", sync=dtspill_sync)
    p.barrier("p1")
    for cm in reversed(pr1["frees"]):
        p.free(cm)
    p.free(cm_dtraw)

    if debug:
        s_dbg = p.sem("s_xdbg")

        def dbg_sync(q):
            q.dma_start(out=xn_dbg, in_=xn_fm[:, 0:32 * TM]).then_inc(s_dbg, 16)
            q.wait_ge(s_dbg, 16)
        p.phase("xdbg", sync=dbg_sync)
        p.barrier("xdbg")

    fg = fgu_phase(p, xn_fm, TM, qguT, sgu_cols, zguT, h_s, R)
    p.barrier("fgu")
    for cm in reversed(fg["frees"]):
        p.free(cm)
    p.free(cm_xnfm)

    fd = fdn_phase(p, TM, qdnT, sdn_cols, zdnT, h_s, ffn_s, R)
    p.barrier("fdn")
    for cm in reversed(fd["frees"]):
        p.free(cm)

    at = att_phase(p, NM, NP, q_s, g_s, kfm_s, ktm_s, vtm_s, dtraw_s, y_s, R)
    p.barrier("att")
    for cm in reversed(at["frees"]):
        p.free(cm)

    wc = wcomb_phase(p, TM, wo_r, ffn_s, y_s, x_main, out_d, R)
    p.barrier("wcomb")

    p.emit()
    return p


def fgu_phase(p, xn_buf, Ttok, qguT, sgu_cols, zguT, h_s, R):
    STn = Ttok // 512
    pb = R["pb"]
    identb = R["identb_sb"]
    pbt = [pb[4 + k].ap().bitcast(BF16) for k in range(4)]
    s_q = p.sem("fgu_q")
    s_zs = p.sem("fgu_zs")
    s_nib = p.sem("fgu_nib")
    s_deq = p.sem("fgu_deq")
    s_pet = p.sem("fgu_pet")
    s_te = p.sem("fgu_te")
    s_peh = p.sem("fgu_peh")
    s_sil = p.sem("fgu_sil")
    s_ue = p.sem("fgu_ue")
    s_h = p.sem("fgu_h")
    s_sph = p.sem("fgu_sph")

    qg, cqg = p.sb("fgu_qg", [128, HID], I32)
    qu, cqu = p.sb("fgu_qu", [128, HID], I32)
    scg, csc1 = p.sb("fgu_scg", [128, 256], F32)
    scu, csc2 = p.sb("fgu_scu", [128, 256], F32)
    zg, cz1 = p.sb("fgu_zg", [128, 32], I32)
    zu, cz2 = p.sb("fgu_zu", [128, 32], I32)
    zsg, czs1 = p.sb("fgu_zsg", [128, 256], F32)
    zsu, czs2 = p.sb("fgu_zsu", [128, 256], F32)
    nibz, cnz = p.sb("fgu_nibz", [128, 256], I32)
    nib, cnib = p.sb("fgu_nib", [128, 2 * 2 * 512], I32)
    wtT, cwt = p.sb("fgu_wtT", [128, 2 * 2 * 512], BF16)
    wk, cwk = p.sb("fgu_wk", [128, 2 * 8 * 128], BF16)
    sg, csg = p.sb("fgu_sg", [128, Ttok], BF16)
    ub, cub = p.sb("fgu_ub", [128, Ttok], BF16)
    hst, chst = p.sb("fgu_hst", [128, Ttok], BF16)
    frees = [cqg, cqu, csc1, csc2, cz1, cz2, czs1, czs2, cnz, cnib, cwt, cwk,
             csg, cub, chst]

    NPAIR = 11

    def peh_idx(pair, sp, side, hc):
        return ((pair * 8 + sp) * 2 + side) * 32 + hc + 1

    def ks_idx(pair, sp, side, ks):
        return ((pair * 8 + sp) * 2 + side) * 8 + ks + 1

    def nibsl(side, slot):
        return nib[:, (side * 2 + slot) * 512:(side * 2 + slot + 1) * 512]

    def wtsl(side, slot):
        return wtT[:, (side * 2 + slot) * 512:(side * 2 + slot + 1) * 512]

    def wksl(side, slot):
        return wk[:, (side * 8 + slot) * 128:(side * 8 + slot + 1) * 128]

    def sync_fn(q):
        for pair in range(NPAIR):
            if pair >= 1:
                q.wait_ge(s_peh, peh_idx(pair - 1, 7, 1, 31))
            for dst, src in [(qg[:], qguT[pair]), (qu[:], qguT[pair + 11]),
                             (scg[:], sgu_cols[:, pair * 256:(pair + 1) * 256]),
                             (scu[:], sgu_cols[:, (pair + 11) * 256:(pair + 12) * 256]),
                             (zg[:], zguT[pair]), (zu[:], zguT[pair + 11])]:
                q.dma_start(out=dst, in_=src).then_inc(s_q, 16)
            for sp in range(8):
                gsp = pair * 8 + sp
                q.wait_ge(s_h, gsp + 1)
                q.dma_start(out=h_s[gsp], in_=hst[:, 0:Ttok]).then_inc(s_sph, 16)
        q.wait_ge(s_sph, 16 * NPAIR * 8)

    def v_fn(v):
        def evacs(pair, sp, side, ks):
            gt0 = (ks_idx(pair, sp, side, ks) - 1) * 4
            for jj in range(4):
                hc = ks * 4 + jj
                tt = gt0 + jj
                v.wait_ge(s_pet, tt + 1)
                if peh_idx(pair, sp, side, hc) - 8 >= 1:
                    v.wait_ge(s_peh, peh_idx(pair, sp, side, hc) - 8)
                v.tensor_copy(wksl(side, hc % 8), pbt[tt % 4][:, 0:128]).then_inc(s_te, 1)

        for pair in range(NPAIR):
            v.wait_ge(s_q, 16 * 6 * (pair + 1))
            if pair >= 1:
                v.wait_ge(s_deq, ks_idx(pair - 1, 7, 1, 7))
            for side in range(2):
                zt = zg if side == 0 else zu
                sct = scg if side == 0 else scu
                zst = zsg if side == 0 else zsu
                base = (pair * 2 + side) * 2
                if base >= 1:
                    v.wait_ge(s_zs, base)
                for sp in range(8):
                    inst = v.tensor_scalar(nibz[:, sp * 32:(sp + 1) * 32], zt[:], 4 * sp, 15,
                                           ALU.logical_shift_right, ALU.bitwise_and)
                inst.then_inc(s_zs, 1)
                v.wait_ge(s_zs, base + 1)
                for sp in range(8):
                    inst = v.scalar_tensor_tensor(zst[:, sp * 32:(sp + 1) * 32],
                                                  nibz[:, sp * 32:(sp + 1) * 32], -1.0,
                                                  sct[:, sp * 32:(sp + 1) * 32],
                                                  ALU.mult, ALU.mult)
                inst.then_inc(s_zs, 1)
            for sp in range(8):
                for side in range(2):
                    qt = qg if side == 0 else qu
                    for ks in range(8):
                        gks = ks_idx(pair, sp, side, ks)
                        if gks >= 3:
                            v.wait_ge(s_deq, gks - 2)
                        v.tensor_scalar(nibsl(side, ks % 2), qt[:, ks * 512:(ks + 1) * 512],
                                        4 * sp, 15, ALU.logical_shift_right,
                                        ALU.bitwise_and).then_inc(s_nib, 1)
                        if ks >= 1:
                            evacs(pair, sp, side, ks - 1)
                        elif side == 1:
                            evacs(pair, sp, 0, 7)
                evacs(pair, sp, 1, 7)
                gsp = pair * 8 + sp
                v.wait_ge(s_peh, peh_idx(pair, sp, 1, 31))
                if gsp >= 1:
                    v.wait_ge(s_h, gsp)
                for st in range(STn):
                    v.tensor_copy(ub[:, st * 512:(st + 1) * 512], pb[st][:, :]).then_inc(s_ue, 1)

    def sc_fn(sc):
        for pair in range(NPAIR):
            for sp in range(8):
                for side in range(2):
                    zst = zsg if side == 0 else zsu
                    sct = scg if side == 0 else scu
                    for ks in range(8):
                        gks = ks_idx(pair, sp, side, ks)
                        sc.wait_ge(s_nib, gks)
                        sc.wait_ge(s_zs, (pair * 2 + side) * 2 + 2)
                        if gks >= 3:
                            sc.wait_ge(s_pet, 4 * (gks - 2))
                        for j in range(4):
                            g = ks * 4 + j
                            inst = sc.activation(wtsl(side, ks % 2)[:, j * 128:(j + 1) * 128],
                                                 nibsl(side, ks % 2)[:, j * 128:(j + 1) * 128],
                                                 AF.Identity,
                                                 bias=zst[:, sp * 32 + g:sp * 32 + g + 1],
                                                 scale=sct[:, sp * 32 + g:sp * 32 + g + 1])
                        inst.then_inc(s_deq, 1)
                    if side == 0:
                        gsp = pair * 8 + sp
                        sc.wait_ge(s_peh, peh_idx(pair, sp, 0, 31))
                        if gsp >= 1:
                            sc.wait_ge(s_h, gsp)
                        for st in range(STn):
                            sc.activation(sg[:, st * 512:(st + 1) * 512], pb[st][:, :],
                                          AF.Silu, bias=0.0, scale=1.0).then_inc(s_sil, 1)

    def pe_fn(t):
        def transposes(pair, sp, side, ks):
            gks = ks_idx(pair, sp, side, ks)
            t.wait_ge(s_deq, gks)
            gt0 = (gks - 1) * 4
            for j in range(4):
                tt = gt0 + j
                if tt >= 4:
                    t.wait_ge(s_te, tt - 3)
                t.transpose(pbt[tt % 4][:, 0:128],
                            wtsl(side, ks % 2)[:, j * 128:(j + 1) * 128],
                            identb[:]).then_inc(s_pet, 1)

        transposes(0, 0, 0, 0)
        for pair in range(NPAIR):
            for sp in range(8):
                gsp = pair * 8 + sp
                for side in range(2):
                    if side == 0:
                        if gsp >= 1:
                            t.wait_ge(s_ue, STn * gsp)
                    else:
                        t.wait_ge(s_sil, STn * (gsp + 1))
                    for hc in range(32):
                        if hc % 4 == 0 and hc // 4 + 1 <= 7:
                            transposes(pair, sp, side, hc // 4 + 1)
                        gks = ks_idx(pair, sp, side, hc // 4)
                        t.wait_ge(s_te, 4 * gks)
                        for st in range(STn):
                            inst = t.matmul(pb[st][:, :],
                                            wksl(side, hc % 8),
                                            xn_buf[:, hc * Ttok + st * 512:hc * Ttok + (st + 1) * 512],
                                            start=(hc == 0), stop=(hc == 31))
                        inst.then_inc(s_peh, 1)
                    if side == 0:
                        transposes(pair, sp, 1, 0)
                    elif sp <= 6:
                        transposes(pair, sp + 1, 0, 0)
                    elif pair <= NPAIR - 2:
                        transposes(pair + 1, 0, 0, 0)

    def gp_fn(gp):
        for pair in range(NPAIR):
            for sp in range(8):
                gsp = pair * 8 + sp
                gp.wait_ge(s_sil, STn * (gsp + 1))
                gp.wait_ge(s_ue, STn * (gsp + 1))
                if gsp >= 1:
                    gp.wait_ge(s_sph, 16 * gsp)
                gp.tensor_mul(hst[:, 0:Ttok], sg[:, 0:Ttok], ub[:, 0:Ttok]).then_inc(s_h, 1)

    p.phase("fgu", sync=sync_fn, vector=v_fn, scalar=sc_fn, tensor=pe_fn, gpsimd=gp_fn)
    return dict(frees=frees)



def kernel(**inputs):
    n_main, n_prefix = 16, 16
    in_maps = host_prep(inputs, n_main, n_prefix)
    p = build(n_main, n_prefix, debug=False)
    res = run_bass_kernel_spmd(p.nc, in_maps, core_ids=list(range(8)))
    out = np.empty((4, 4096, 4096), np.float32)
    for core in range(8):
        b, hf = core // 2, core % 2
        out[b, 2048 * hf:2048 * (hf + 1)] = res.results[core]["out"]
    return out
